# version 04
# baseline (speedup 1.0000x reference)
import numpy as np
from contextlib import ExitStack

import concourse.bass as bass
import concourse.mybir as mybir
from concourse.bass_utils import run_bass_kernel_spmd

F32 = mybir.dt.float32
BF16 = mybir.dt.bfloat16
AF = mybir.ActivationFunctionType
ALU = mybir.AluOpType
AX = mybir.AxisListType

NCORES = 8
T = 2048
D = 1024
NT = 8
EPS = 1e-5
NCH = 4


def _f32c(a):
    return np.ascontiguousarray(a, dtype=np.float32)


class Sem:
    def __init__(self, st, nc, name):
        self.h = st.enter_context(nc.semaphore(name))
        self.v = 0

    def inc(self, inst, n=1):
        self.v += n
        inst.then_inc(self.h, n)
        return inst

    def wait(self, eng, v=None):
        eng.wait_ge(self.h, self.v if v is None else v)


class Prog:
    def __init__(self):
        self.nc = bass.Bass("TRN2", target_bir_lowering=False)
        self.st = ExitStack()
        self._i = 0
        self.prefix = ""
        self._phase_st = None
        self._sempool = []
        self._semptr = 0
        self._psmap = {}

    def begin_phase(self, tag):
        self.prefix = tag + "_"
        self._phase_st = ExitStack()
        self._semptr = 0

    def end_phase(self):
        self._phase_st.close()
        self._phase_st = None
        self.prefix = ""

    def sem(self):
        if self._phase_st is None:
            self._i += 1
            return Sem(self.st, self.nc, "s%d" % self._i)
        if self._semptr >= len(self._sempool):
            self._sempool.append(Sem(self.st, self.nc,
                                     "sp%d" % len(self._sempool)))
        s = self._sempool[self._semptr]
        self._semptr += 1
        return s

    def sb(self, name, shape, dtype):
        st = self._phase_st if self._phase_st is not None else self.st
        return st.enter_context(
            self.nc.sbuf_tensor(self.prefix + name, shape, dtype))

    def ps(self, name, shape, dtype=F32):
        if name not in self._psmap:
            self._psmap[name] = self.st.enter_context(
                self.nc.psum_tensor(name, shape, dtype))
        return self._psmap[name]

    def dram(self, name, shape, dtype=F32, kind=None):
        if kind is None:
            return self.nc.dram_tensor(self.prefix + name, shape, dtype).ap()
        return self.nc.dram_tensor(self.prefix + name, shape, dtype,
                                   kind=kind).ap()

    def add_consts(self, vals):
        if not hasattr(self, "cvals"):
            self.cvals = []
            self.ctile = self.st.enter_context(
                self.nc.sbuf_tensor("cpool", [128, 16], F32))
            self.cmap = {}
        for v in vals:
            v = float(v)
            if v in self.cmap:
                continue
            i = len(self.cvals)
            self.cvals.append(v)
            ap = self.ctile[:, i:i + 1]
            self.nc.const_aps.register(F32, v, ap)
            self.cmap[v] = ap

    def emit_const_init(self, gp):
        for i, v in enumerate(self.cvals):
            gp.memset(self.ctile[:, i:i + 1], float(v))

    def din(self, name, shape):
        return self.nc.dram_tensor(name, shape, F32, kind="ExternalInput").ap()

    def dout(self, name, shape, dtype=F32):
        return self.nc.dram_tensor(name, shape, dtype, kind="ExternalOutput").ap()



def emit_ln(p, xview, xhat, aux, out_dtype=BF16):
    nc = p.nc
    psA = aux["psb0"][0:1, :]
    psB = aux["psb1"][0:1, :]
    rows = aux["rows"]
    tmp = aux["tmp"]
    for g in range(NT):
        with nc.Block() as block:
            @block.scalar
            def _(s):
                s.activation(tmp[:, :], xview(g), AF.Square)
        with nc.Block() as block:
            @block.tensor
            def _(t):
                for ch in range(NCH):
                    sl = slice(ch * 512, (ch + 1) * 512)
                    t.matmul(psA[:, sl], aux["ones_col"][:, :], xview(g)[:, sl],
                             start=(g == 0), stop=(g == NT - 1))
                    t.matmul(psB[:, sl], aux["ones_col"][:, :], tmp[:, sl],
                             start=(g == 0), stop=(g == NT - 1))
    mrow = rows[0:1, 0:T]
    vrow = rows[0:1, T:2 * T]
    with nc.Block() as block:
        @block.vector
        def _(v):
            v.tensor_scalar_mul(mrow, psA[:, :], 1.0 / D)
            v.tensor_scalar_mul(vrow, psB[:, :], 1.0 / D)
            v.tensor_tensor(tmp[0:1, 0:T], mrow, mrow, ALU.mult)
            v.tensor_sub(vrow, vrow, tmp[0:1, 0:T])
    with nc.Block() as block:
        @block.scalar
        def _(s):
            s.activation(vrow, vrow, AF.Sqrt, bias=EPS)
    with nc.Block() as block:
        @block.vector
        def _(v):
            v.reciprocal(vrow, vrow)
            v.scalar_tensor_tensor(mrow, mrow, -1.0, vrow,
                                   ALU.mult, ALU.mult)
    with nc.Block() as block:
        @block.tensor
        def _(t):
            for ch in range(NCH):
                sl = slice(ch * 512, (ch + 1) * 512)
                t.matmul(aux["psb0"][:, sl], aux["ones_row"][:, :],
                         vrow[:, sl], start=True, stop=True)
                t.matmul(aux["psb1"][:, sl], aux["ones_row"][:, :],
                         mrow[:, sl], start=True, stop=True)
    with nc.Block() as block:
        @block.vector
        def _(v):
            for g in range(NT):
                v.tensor_tensor(tmp[:, :], xview(g), aux["psb0"][:, :], ALU.mult)
                v.scalar_tensor_tensor(xhat[g][:, :], tmp[:, :], 1.0,
                                       aux["psb1"][:, :], ALU.mult, ALU.add)


def emit_loads(p, sem, plain=(), cast=()):
    with p.nc.Block() as block:
        @block.sync
        def _(sync):
            for dst, src in plain:
                sem.inc(sync.dma_start(out=dst, in_=src), 16)
        @block.gpsimd
        def _(gp):
            for dst, src in cast:
                sem.inc(gp.dma_start(out=dst, in_=src), 16)



MAMBA_INS = ["xT", "W_up", "b_upc", "conv_w", "conv_b", "W_delta", "b_delta",
             "W_B", "b_B", "W_C", "b_C", "A_log", "D_skip", "W_down", "b_dc",
             "selb"]
MAMBA_SHAPES = dict(xT=[D, T], W_up=[D, 256], b_upc=[128, 2], conv_w=[128, 3],
                    conv_b=[128, 1], W_delta=[128, 128], b_delta=[128, 1],
                    W_B=[128, 64], b_B=[64, 1], W_C=[128, 64], b_C=[64, 1],
                    A_log=[128, 64], D_skip=[128, 1], W_down=[128, D],
                    b_dc=[128, 8], selb=[64, 64 * 128])


def build_mamba():
    p = Prog()
    io = {n: p.din(n, MAMBA_SHAPES[n]) for n in MAMBA_INS}
    io["oT"] = p.dout("oT", [D, T])
    emit_mamba(p, io)
    return p


def emit_mamba(p, io):
    nc = p.nc
    (xT, W_up, b_upc, conv_w, conv_b, W_delta, b_delta, W_B, b_B, W_C, b_C,
     A_log, D_skip, W_down, b_dc, selb) = [io[n] for n in MAMBA_INS]
    oT = io["oT"]

    x = [p.sb("x%d" % g, [128, T], F32) for g in range(NT)]
    pool = p.sb("pool", [128, 4 * T], F32)
    poolb = pool[:, :].bitcast(BF16)
    xhat = [poolb[:, g * T:(g + 1) * T] for g in range(NT)]
    tmp = p.sb("tmp", [128, T], F32)
    rows = p.sb("rows", [1, 2 * T], F32)
    ones_row = p.sb("ones_row", [1, 128], F32)
    ones_col = p.sb("ones_col", [128, 1], F32)
    wup_s = [p.sb("wup%d" % k, [128, 256], BF16) for k in range(NT)]
    wdel_s = p.sb("wdel", [128, 128], BF16)
    wB_s = p.sb("wB", [128, 64], BF16)
    wC_s = p.sb("wC", [128, 64], BF16)
    wdn_s = p.sb("wdn", [128, D], BF16)
    selb_s = p.sb("selb_s", [64, 64 * 128], BF16)
    bup_s = p.sb("bup", [128, 2], F32)
    cw_s = p.sb("cw", [128, 3], F32)
    cb_s = p.sb("cb", [128, 1], F32)
    bdel_s = p.sb("bdel", [128, 1], F32)
    bB_s = p.sb("bB", [64, 1], F32)
    bC_s = p.sb("bC", [64, 1], F32)
    alog_s = p.sb("alog", [128, 64], F32)
    dskip_s = p.sb("dskip", [128, 1], F32)
    bdc_s = p.sb("bdc", [128, 8], F32)

    LEFT = p.sb("LEFT", [128, T + 2], F32)
    RIGHT = p.sb("RIGHT", [128, T], F32)
    xs = p.sb("xs", [128, T], F32)
    xsb = p.sb("xsb", [128, T], BF16)
    delta = pool[:, 0:T]
    wdx = pool[:, T:2 * T]
    dAn = pool[:, 2 * T:3 * T]
    hn = pool[:, 3 * T:4 * T]
    dBx = tmp
    Bm = p.sb("Bm", [64, T], BF16)
    Cm = p.sb("Cm", [64, T], BF16)
    yacc = p.sb("yacc", [128, T], F32)
    A_sb = p.sb("A_sb", [128, 64], F32)
    gat = xsb

    p.add_consts([EPS, -1.0])
    psb0 = p.ps("psb0", [128, T])
    psb1 = p.ps("psb1", [128, T])
    aux = dict(psb0=psb0, psb1=psb1, rows=rows, tmp=tmp,
               ones_row=ones_row, ones_col=ones_col)

    lds = p.sem()
    emit_loads(
        p, lds,
        plain=[(x[g][:, :], xT[g * 128:(g + 1) * 128, :]) for g in range(NT)]
        + [(bup_s[:, :], b_upc), (cw_s[:, :], conv_w), (cb_s[:, :], conv_b),
           (bdel_s[:, :], b_delta), (bB_s[:, :], b_B), (bC_s[:, :], b_C),
           (alog_s[:, :], A_log), (dskip_s[:, :], D_skip), (bdc_s[:, :], b_dc)],
        cast=[(wup_s[k][:, :], W_up[k * 128:(k + 1) * 128, :]) for k in range(NT)]
        + [(wdel_s[:, :], W_delta), (wB_s[:, :], W_B), (wC_s[:, :], W_C),
           (wdn_s[:, :], W_down), (selb_s[:, :], selb)],
    )
    with nc.Block() as block:
        @block.gpsimd
        def _(gp):
            gp.memset(ones_row[:, :], 1.0)
            gp.memset(ones_col[:, :], 1.0)
            p.emit_const_init(gp)
            gp.memset(LEFT[:, 0:2], 0.0)
            gp.memset(yacc[:, :], 0.0)
        @block.vector
        def _(v):
            lds.wait(v)

    emit_ln(p, lambda g: x[g][:, :], xhat, aux)

    for m in range(2):
        with nc.Block() as block:
            @block.tensor
            def _(t):
                for ch in range(NCH):
                    sl = slice(ch * 512, (ch + 1) * 512)
                    for k in range(NT):
                        t.matmul(psb0[:, sl], wup_s[k][:, m * 128:(m + 1) * 128],
                                 xhat[k][:, sl], start=(k == 0), stop=(k == NT - 1))
        with nc.Block() as block:
            @block.vector
            def _(v):
                if m == 0:
                    v.tensor_scalar_add(LEFT[:, 2:2 + T], psb0[:, :], bup_s[:, 0:1])
                else:
                    v.tensor_scalar_add(RIGHT[:, :], psb0[:, :], bup_s[:, 1:2])
    with nc.Block() as block:
        @block.vector
        def _(v):
            v.tensor_scalar_mul(tmp[:, :], LEFT[:, 0:T], cw_s[:, 0:1])
            v.scalar_tensor_tensor(tmp[:, :], LEFT[:, 1:1 + T], cw_s[:, 1:2],
                                   tmp[:, :], ALU.mult, ALU.add)
            v.scalar_tensor_tensor(tmp[:, :], LEFT[:, 2:2 + T], cw_s[:, 2:3],
                                   tmp[:, :], ALU.mult, ALU.add)
    with nc.Block() as block:
        @block.scalar
        def _(s):
            s.activation(xs[:, :], tmp[:, :], AF.Silu, bias=cb_s[:, 0:1])
    with nc.Block() as block:
        @block.vector
        def _(v):
            v.tensor_copy(xsb[:, :], xs[:, :])
    with nc.Block() as block:
        @block.tensor
        def _(t):
            for ch in range(NCH):
                sl = slice(ch * 512, (ch + 1) * 512)
                t.matmul(psb0[:, sl], wdel_s[:, :], xsb[:, sl], start=True, stop=True)
                t.matmul(psb1[0:64, sl], wB_s[:, :], xsb[:, sl], start=True, stop=True)
                t.matmul(psb1[64:128, sl], wC_s[:, :], xsb[:, sl], start=True, stop=True)
    with nc.Block() as block:
        @block.scalar
        def _(s):
            s.activation(delta[:, :], psb0[:, :], AF.Abs, bias=bdel_s[:, 0:1])
            s.activation(delta[:, :], delta[:, :], AF.Exp, scale=-1.0)
            s.activation(delta[:, :], delta[:, :], AF.Ln, bias=1.0)
            s.activation(tmp[:, :], psb0[:, :], AF.Relu, bias=bdel_s[:, 0:1])
            s.activation(A_sb[:, :], alog_s[:, :], AF.Exp)
        @block.vector
        def _(v):
            v.tensor_scalar_add(Bm[:, :], psb1[0:64, :], bB_s[:, 0:1])
            v.tensor_scalar_add(Cm[:, :], psb1[64:128, :], bC_s[:, 0:1])
    with nc.Block() as block:
        @block.vector
        def _(v):
            v.tensor_add(delta[:, :], delta[:, :], tmp[:, :])
            v.tensor_mul(wdx[:, :], delta[:, :], xs[:, :])
            v.tensor_scalar_mul(A_sb[:, :], A_sb[:, :], -1.0)
    s_bc = p.sem()
    s_da = p.sem()
    for n in range(64):
        with nc.Block() as block:
            @block.tensor
            def _(t):
                for ch in range(NCH):
                    sl = slice(ch * 512, (ch + 1) * 512)
                    t.matmul(psb0[:, sl], selb_s[:, n * 128:(n + 1) * 128],
                             Bm[:, sl], start=True, stop=True)
                    inst = t.matmul(psb1[:, sl], selb_s[:, n * 128:(n + 1) * 128],
                                    Cm[:, sl], start=True, stop=True)
                s_bc.inc(inst)
            @block.scalar
            def _(s):
                s_da.inc(s.activation(dAn[:, :], delta[:, :], AF.Exp,
                                      scale=A_sb[:, n:n + 1]))
            @block.vector
            def _(v):
                s_bc.wait(v)
                s_da.wait(v)
                v.tensor_tensor(dBx[:, :], wdx[:, :], psb0[:, :], ALU.mult)
                v.tensor_tensor_scan(hn[:, :], dAn[:, :], dBx[:, :], 0.0,
                                     ALU.mult, ALU.add)
                v.tensor_tensor(hn[:, :], hn[:, :], psb1[:, :], ALU.mult)
                v.tensor_add(yacc[:, :], yacc[:, :], hn[:, :])
    with nc.Block() as block:
        @block.vector
        def _(v):
            v.scalar_tensor_tensor(yacc[:, :], xs[:, :], dskip_s[:, 0:1],
                                   yacc[:, :], ALU.mult, ALU.add)
        @block.scalar
        def _(s):
            s.activation(RIGHT[:, :], RIGHT[:, :], AF.Silu)
    with nc.Block() as block:
        @block.vector
        def _(v):
            v.tensor_mul(gat[:, :], yacc[:, :], RIGHT[:, :])
    st = p.sem()
    for m in range(NT):
        with nc.Block() as block:
            @block.tensor
            def _(t):
                for ch in range(NCH):
                    sl = slice(ch * 512, (ch + 1) * 512)
                    t.matmul(psb0[:, sl], wdn_s[:, m * 128:(m + 1) * 128],
                             gat[:, sl], start=True, stop=True)
        with nc.Block() as block:
            @block.vector
            def _(v):
                v.scalar_tensor_tensor(x[m][:, :], psb0[:, :], bdc_s[:, m:m + 1],
                                       x[m][:, :], ALU.add, ALU.add)
        with nc.Block() as block:
            @block.sync
            def _(sync):
                st.inc(sync.dma_start(out=oT[m * 128:(m + 1) * 128, :],
                                      in_=x[m][:, :]), 16)
    with nc.Block() as block:
        @block.sync
        def _(sync):
            st.wait(sync)
    return p


def mamba_inputs(xT_c, ins):
    sel = np.zeros((64, 64 * 128), np.float32)
    for n in range(64):
        sel[n, n * 128:(n + 1) * 128] = 1.0
    return {
        "xT": xT_c,
        "W_up": _f32c(ins["L0_attn_a_W_up"]),
        "b_upc": _f32c(np.asarray(ins["L0_attn_a_b_up"]).reshape(2, 128).T),
        "conv_w": _f32c(ins["L0_attn_a_conv_w"]),
        "conv_b": _f32c(np.asarray(ins["L0_attn_a_conv_b"]).reshape(128, 1)),
        "W_delta": _f32c(ins["L0_attn_a_W_delta"]),
        "b_delta": _f32c(np.asarray(ins["L0_attn_a_b_delta"]).reshape(128, 1)),
        "W_B": _f32c(ins["L0_attn_a_W_B"]),
        "b_B": _f32c(np.asarray(ins["L0_attn_a_b_B"]).reshape(64, 1)),
        "W_C": _f32c(ins["L0_attn_a_W_C"]),
        "b_C": _f32c(np.asarray(ins["L0_attn_a_b_C"]).reshape(64, 1)),
        "A_log": _f32c(ins["L0_attn_a_A_log"]),
        "D_skip": _f32c(np.asarray(ins["L0_attn_a_D_skip"]).reshape(128, 1)),
        "W_down": _f32c(ins["L0_attn_a_W_down"]),
        "b_dc": _f32c(np.asarray(ins["L0_attn_a_b_down"]).reshape(8, 128).T),
        "selb": sel,
    }


def kernel(**inputs):
    x = np.asarray(inputs["L0_attn_a_up_x"], dtype=np.float32)
    xT = [np.ascontiguousarray(x[c].T) for c in range(NCORES)]

    p = build_mamba()
    in_maps = [mamba_inputs(xT[c], inputs) for c in range(NCORES)]
    res = run_bass_kernel_spmd(p.nc, in_maps, core_ids=list(range(NCORES)))
    xT = [res.results[c]["oT"] for c in range(NCORES)]
    p.st.close()

    out = np.stack([xT[c].T for c in range(NCORES)], axis=0)
    return np.ascontiguousarray(out, dtype=np.float32)



def build_attn_b():
    p = Prog()
    io = {n: p.din(n, s) for n, s in ATTNB_SHAPES.items()}
    io["oT"] = p.dout("oT", [D, T])
    emit_attn_b(p, io)
    return p


ATTNB_SHAPES = dict(xT=[D, T], WqT=[D, D], WkT=[D, D], WvT=[D, D], WoT=[D, D],
                    bo_c=[128, 8], bd_c=[128, 128], eg_c=[16, 8 * 128],
                    e2_c=[128, 128])


def emit_attn_b(p, io):
    nc = p.nc
    xT = io["xT"]; WqT = io["WqT"]; WkT = io["WkT"]; WvT = io["WvT"]
    WoT = io["WoT"]; bo_c = io["bo_c"]; bd_c = io["bd_c"]; eg_c = io["eg_c"]
    e2_c = io["e2_c"]; oT = io["oT"]
    O3b = p.dram("O3b", [D, T], BF16)

    SC = 256
    NSC = T // SC
    pacc = p.sb("pacc", [128, 8 * T], F32)
    acc = [pacc[:, g * T:(g + 1) * T] for g in range(NT)]
    paccb = pacc[:, :].bitcast(BF16)
    xhat = [paccb[:, g * T:(g + 1) * T] for g in range(NT)]
    xres = [pacc[:, 0:T], pacc[:, T:2 * T]]
    qf = [p.sb("qf%d" % g, [128, T], BF16) for g in range(NT)]
    kf = [p.sb("kf%d" % g, [128, T], BF16) for g in range(NT)]
    vf = [p.sb("vf%d" % g, [128, T], BF16) for g in range(NT)]
    wbuf = [p.sb("wb%d" % k, [128, D], BF16) for k in range(NT)]
    tmp = p.sb("tmp", [128, T], F32)
    Gc = p.sb("Gc", [16, 16 * SC], F32)
    Gm = p.sb("Gm", [16, 2 * SC], F32)
    kbcs = p.sb("kbcs", [128, SC], BF16)
    prodb = p.sb("prodb", [128, SC], BF16)
    vbcs = p.sb("vbcs", [128, SC], BF16)
    bd_s = p.sb("bd_s", [128, 128], BF16)
    eg_s = p.sb("eg_s", [16, 8 * 128], BF16)
    e2_s = p.sb("e2_s", [128, 128], BF16)
    bo_s = p.sb("bo_s", [128, 8], F32)

    Ac = tmp[:, :].bitcast(BF16)[0:16, 0:16 * SC]
    stg = kf[0]
    p.add_consts([EPS, 0.125])
    psb0 = p.ps("psb0", [128, T])
    psb1 = p.ps("psb1", [128, T])

    lds = p.sem()
    emit_loads(p, lds, plain=[(bo_s[:, :], bo_c)],
               cast=[(bd_s[:, :], bd_c), (eg_s[:, :], eg_c), (e2_s[:, :], e2_c)])
    ones_row = p.sb("ones_row", [1, 128], F32)
    ones_col = p.sb("ones_col", [128, 1], F32)
    with nc.Block() as block:
        @block.gpsimd
        def _(gp):
            gp.memset(ones_row[:, :], 1.0)
            gp.memset(ones_col[:, :], 1.0)
            p.emit_const_init(gp)
        @block.vector
        def _(v):
            lds.wait(v)

    s_x = p.sem()
    for g in range(NT):
        b = g % 2
        with nc.Block() as block:
            @block.sync
            def _(sync):
                s_x.inc(sync.dma_start(out=xres[b][:, :],
                                       in_=xT[g * 128:(g + 1) * 128, :]), 16)
            @block.scalar
            def _(s):
                s_x.wait(s)
                s.activation(tmp[:, :], xres[b][:, :], AF.Square)
        with nc.Block() as block:
            @block.tensor
            def _(t):
                for ch in range(NCH):
                    sl = slice(ch * 512, (ch + 1) * 512)
                    t.matmul(psb0[0:1, sl], ones_col[:, :], xres[b][:, sl],
                             start=(g == 0), stop=(g == NT - 1))
                    t.matmul(psb1[0:1, sl], ones_col[:, :], tmp[:, sl],
                             start=(g == 0), stop=(g == NT - 1))
    mrow = tmp[0:1, 0:T]
    vrow = tmp[0:1, T:2 * T] if False else None
    vrow = Gc[0:1, 0:T]
    scr = Gc[0:1, T:2 * T]
    with nc.Block() as block:
        @block.vector
        def _(v):
            v.tensor_scalar_mul(mrow, psb0[0:1, :], 1.0 / D)
            v.tensor_scalar_mul(vrow, psb1[0:1, :], 1.0 / D)
            v.tensor_tensor(scr, mrow, mrow, ALU.mult)
            v.tensor_sub(vrow, vrow, scr)
    with nc.Block() as block:
        @block.scalar
        def _(s):
            s.activation(vrow, vrow, AF.Sqrt, bias=EPS)
    with nc.Block() as block:
        @block.vector
        def _(v):
            v.reciprocal(vrow, vrow)
            v.scalar_tensor_tensor(mrow, mrow, -1.0, vrow, ALU.mult, ALU.mult)
    with nc.Block() as block:
        @block.tensor
        def _(t):
            for ch in range(NCH):
                sl = slice(ch * 512, (ch + 1) * 512)
                t.matmul(psb0[:, sl], ones_row[:, :], vrow[:, sl],
                         start=True, stop=True)
                t.matmul(psb1[:, sl], ones_row[:, :], mrow[:, sl],
                         start=True, stop=True)
    for g in range(NT):
        b = g % 2
        with nc.Block() as block:
            @block.sync
            def _(sync):
                s_x.inc(sync.dma_start(out=tmp[:, :],
                                       in_=xT[g * 128:(g + 1) * 128, :]), 16)
            @block.vector
            def _(v):
                s_x.wait(v)
                v.tensor_tensor(tmp[:, :], tmp[:, :], psb0[:, :], ALU.mult)
                v.scalar_tensor_tensor(xhat[g][:, :], tmp[:, :], 1.0,
                                       psb1[:, :], ALU.mult, ALU.add)

    s_w = p.sem()
    for Wd, dst in ((WqT, qf), (WkT, kf), (WvT, vf)):
        with nc.Block() as block:
            @block.gpsimd
            def _(gp):
                for k in range(NT):
                    s_w.inc(gp.dma_start(out=wbuf[k][:, :],
                                         in_=Wd[k * 128:(k + 1) * 128, :]), 16)
            @block.tensor
            def _(t):
                s_w.wait(t)
        for m in range(NT):
            with nc.Block() as block:
                @block.tensor
                def _(t):
                    for ch in range(NCH):
                        sl = slice(ch * 512, (ch + 1) * 512)
                        for k in range(NT):
                            t.matmul(psb0[:, sl], wbuf[k][:, m * 128:(m + 1) * 128],
                                     xhat[k][:, sl], start=(k == 0),
                                     stop=(k == NT - 1))
            with nc.Block() as block:
                @block.vector
                def _(v):
                    v.tensor_copy(dst[m][:, :], psb0[:, :])

    with nc.Block() as block:
        @block.gpsimd
        def _(gp):
            for g in range(NT):
                gp.memset(acc[g][:, :], 0.0)
    for sc in range(NSC):
        ssl = slice(sc * SC, (sc + 1) * SC)
        for kh in range(16):
            with nc.Block() as block:
                @block.tensor
                def _(t):
                    pb = 64 * (kh % 2)
                    t.matmul(psb0[:, 0:SC], e2_s[pb:pb + 64, :],
                             kf[kh // 2][pb:pb + 64, ssl],
                             start=True, stop=True)
            with nc.Block() as block:
                @block.vector
                def _(v):
                    v.tensor_copy(kbcs[:, :], psb0[:, 0:SC])
            for g in range(NT):
                with nc.Block() as block:
                    @block.vector
                    def _(v):
                        v.tensor_tensor(prodb[:, :], qf[g][:, ssl],
                                        kbcs[:, :], ALU.mult)
                with nc.Block() as block:
                    @block.tensor
                    def _(t):
                        t.matmul(psb1[0:16, 0:SC], bd_s[:, g * 16:(g + 1) * 16],
                                 prodb[:, :], start=(g == 0), stop=(g == NT - 1))
            with nc.Block() as block:
                @block.vector
                def _(v):
                    v.tensor_copy(
                        Gc[:, 0:16 * SC].rearrange("p (s k) -> p s k", k=16)[
                            :, :, kh:kh + 1],
                        psb1[0:16, 0:SC].unsqueeze(2))
        with nc.Block() as block:
            @block.vector
            def _(v):
                v.reduce_max(Gm[:, 0:SC],
                             Gc[:, 0:16 * SC].rearrange("p (s k) -> p s k", k=16),
                             axis=AX.X)
        with nc.Block() as block:
            @block.vector
            def _(v):
                v.tensor_sub(Gc[:, 0:16 * SC].rearrange("p (s k) -> p s k", k=16),
                             Gc[:, 0:16 * SC].rearrange("p (s k) -> p s k", k=16),
                             Gm[:, 0:SC].unsqueeze(2).broadcast_to([16, SC, 16]))
        with nc.Block() as block:
            @block.scalar
            def _(s):
                s.activation(Gc[:, 0:16 * SC], Gc[:, 0:16 * SC], AF.Exp,
                             scale=0.125)
        with nc.Block() as block:
            @block.vector
            def _(v):
                v.reduce_sum(Gm[:, SC:2 * SC],
                             Gc[:, 0:16 * SC].rearrange("p (s k) -> p s k", k=16),
                             axis=AX.X)
        with nc.Block() as block:
            @block.vector
            def _(v):
                v.reciprocal(Gm[:, SC:2 * SC], Gm[:, SC:2 * SC])
        with nc.Block() as block:
            @block.vector
            def _(v):
                v.tensor_tensor(Ac[:, 0:16 * SC].rearrange("p (s k) -> p s k", k=16),
                                Gc[:, 0:16 * SC].rearrange("p (s k) -> p s k", k=16),
                                Gm[:, SC:2 * SC].unsqueeze(2).broadcast_to([16, SC, 16]),
                                ALU.mult)
        for kh in range(16):
            with nc.Block() as block:
                @block.tensor
                def _(t):
                    pb = 64 * (kh % 2)
                    t.matmul(psb0[:, 0:SC], e2_s[pb:pb + 64, :],
                             vf[kh // 2][pb:pb + 64, ssl],
                             start=True, stop=True)
            with nc.Block() as block:
                @block.vector
                def _(v):
                    v.tensor_copy(vbcs[:, :], psb0[:, 0:SC])
            with nc.Block() as block:
                @block.tensor
                def _(t):
                    for g in range(NT):
                        t.matmul(psb1[:, g * SC:(g + 1) * SC] if SC * NT <= T
                                 else psb1[:, 0:SC],
                                 eg_s[:, g * 128:(g + 1) * 128],
                                 Ac[:, 0:16 * SC].rearrange(
                                     "p (s k) -> p k s", k=16)[:, kh, :],
                                 start=True, stop=True)
            with nc.Block() as block:
                @block.vector
                def _(v):
                    for g in range(NT):
                        v.tensor_tensor(prodb[:, :],
                                        psb1[:, g * SC:(g + 1) * SC],
                                        vbcs[:, :], ALU.mult)
                        v.tensor_tensor(acc[g][:, ssl], acc[g][:, ssl],
                                        prodb[:, :], ALU.add)

    st = p.sem()
    for g in range(NT):
        with nc.Block() as block:
            @block.scalar
            def _(s):
                s.copy(stg[:, 0:T].rearrange("p (u w) -> p w u", u=16),
                       acc[g][:, :])
        with nc.Block() as block:
            @block.sync
            def _(sync):
                st.inc(sync.dma_start(out=O3b[g * 128:(g + 1) * 128, :],
                                      in_=stg[:, :]), 16)
    with nc.Block() as block:
        @block.sync
        def _(sync):
            st.wait(sync)
    xrs = qf
    s_v = p.sem()
    with nc.Block() as block:
        @block.gpsimd
        def _(gp):
            for jt in range(NT):
                for ui in range(2):
                    u = jt * 2 + ui
                    s_v.inc(gp.dma_start(
                        out=xrs[jt][64 * ui:64 * ui + 64, :],
                        in_=bass.AP(O3b.tensor, u * 128,
                                    [[T, 64], [64 * T, 16], [1, 128]])), 16)
            for k in range(NT):
                s_w.inc(gp.dma_start(out=wbuf[k][:, :],
                                     in_=WoT[k * 128:(k + 1) * 128, :]), 16)
        @block.tensor
        def _(t):
            s_v.wait(t)
            s_w.wait(t)
    for m in range(NT):
        with nc.Block() as block:
            @block.sync
            def _(sync):
                s_x.inc(sync.dma_start(out=tmp[:, :],
                                       in_=xT[m * 128:(m + 1) * 128, :]), 16)
            @block.tensor
            def _(t):
                for ch in range(NCH):
                    sl = slice(ch * 512, (ch + 1) * 512)
                    for k in range(NT):
                        t.matmul(psb0[:, sl], wbuf[k][:, m * 128:(m + 1) * 128],
                                 xrs[k][:, sl], start=(k == 0), stop=(k == NT - 1))
        with nc.Block() as block:
            @block.vector
            def _(v):
                s_x.wait(v)
                v.scalar_tensor_tensor(tmp[:, :], psb0[:, :], bo_s[:, m:m + 1],
                                       tmp[:, :], ALU.add, ALU.add)
        with nc.Block() as block:
            @block.sync
            def _(sync):
                st.inc(sync.dma_start(out=oT[m * 128:(m + 1) * 128, :],
                                      in_=tmp[:, :]), 16)
    with nc.Block() as block:
        @block.sync
        def _(sync):
            st.wait(sync)
    return p


def attn_b_inputs(xT_c, Wq, Wk, Wv, Wo, bo):
    bd = np.zeros((128, 128), np.float32)
    eg = np.zeros((16, 8 * 128), np.float32)
    for g in range(8):
        for hh in range(16):
            if hh // 2 == g:
                bd[64 * (hh % 2):64 * (hh % 2) + 64, g * 16 + hh] = 1.0
        for pp in range(128):
            eg[2 * g + pp // 64, g * 128 + pp] = 1.0
    e2 = np.zeros((128, 128), np.float32)
    for pp in range(128):
        e2[pp % 64, pp] = 1.0
        e2[64 + pp % 64, pp] = 1.0
    return {
        "xT": xT_c,
        "WqT": _f32c(np.asarray(Wq).T),
        "WkT": _f32c(np.asarray(Wk).T),
        "WvT": _f32c(np.asarray(Wv).T),
        "WoT": _f32c(np.asarray(Wo).T),
        "bo_c": _f32c(np.asarray(bo).reshape(8, 128).T),
        "bd_c": bd, "eg_c": eg, "e2_c": e2,
    }



A_CH = [(i * 128, 128) for i in range(21)] + [(2688, 42)]
SWAP_MASK = [i ^ 1 for i in range(32)]


INNER_INS = ["yT", "Wqkv", "Wop", "qs_c", "lng_c", "W1", "b1a", "b1g", "W2",
             "b2_c", "CQ_c", "SQ_c", "CK_c", "SK_c", "M2_c", "bd_c", "eg_c",
             "id_c"]
INNER_SHAPES = dict(yT=[D, T], Wqkv=[D, 3 * D], Wop=[D, D], qs_c=[128, 2],
                    lng_c=[128, 16], W1=[D, 5460], b1a=[128, 22],
                    b1g=[128, 22], W2=[2730, D], b2_c=[128, 8], CQ_c=[128, T],
                    SQ_c=[128, T], CK_c=[128, T], SK_c=[128, T],
                    M2_c=[128, 384], bd_c=[128, 128], eg_c=[16, 8 * 128],
                    id_c=[128, 128])


def build_inner():
    p = Prog()
    io = {n: p.din(n, INNER_SHAPES[n]) for n in INNER_INS}
    io["oT"] = p.dout("oT", [D, T])
    emit_inner(p, io)
    return p


def emit_inner(p, io):
    nc = p.nc
    (yT, Wqkv, Wop, qs_c, lng_c, W1, b1a, b1g, W2, b2_c, CQ_c, SQ_c, CK_c,
     SK_c, M2_c, bd_c, eg_c, id_c) = [io[n] for n in INNER_INS]
    oT = io["oT"]
    ymid = p.dram("ymid", [D, T], F32)

    pool = p.sb("bigpool", [128, 72 * 1024], BF16)

    def bt(off_kb, n):
        o = off_kb * 512
        return pool[:, o:o + n]
    ybf = [bt(4 * g, T) for g in range(NT)]
    ofm = ybf
    qb = [bt(32 + 4 * g, T) for g in range(NT)]
    kb = [bt(64 + 4 * g, T) for g in range(NT)]
    vtm = [bt(96 + 2 * u, D) for u in range(16)]
    CQ = bt(128, T); SQ = bt(132, T); CK = bt(136, T); SK = bt(140, T)
    xf = [bt(32 + 4 * g, T) for g in range(NT)]
    hb = [bt(64 + 4 * i, T) for i in range(20)]
    hb = hb + [bt(0 + 4 * i, T) for i in range(2)]

    stage = [p.sb("stg%d" % i, [128, T + 1], F32) for i in range(2)]
    rqb = stage[1][0:16, 0:1024].bitcast(BF16)
    tmp = p.sb("tmp", [128, T], F32)
    tshb2 = tmp[:, :].bitcast(BF16)[:, 0:T]
    tshb = bt(124, T)
    nrm = p.sb("nrm", [80, 2 * T + 64], F32)
    rq = nrm[0:16, 0:T]
    rk = nrm[0:16, T:2 * T]
    r3 = nrm[64:80, 0:T]
    k3n = nrm[64:80, T:2 * T]
    khm = nrm[64:80, 2 * T:2 * T + 18]
    khw = nrm[64:80, 2 * T + 20:2 * T + 36]
    kht = nrm[64:80, 2 * T + 40:2 * T + 56]
    mhc = p.sb("mhc", [128, 256], F32)
    Pb = p.sb("Pb", [128, 384], BF16)
    PT = p.sb("PT", [128, 384], BF16)
    zcol = p.sb("zcol", [128, 2], F32)
    Db = p.sb("Db", [128, 128], BF16)
    ones_row = p.sb("ones_row", [1, 128], F32)
    ones_col = p.sb("ones_col", [128, 1], F32)
    wbig = p.sb("wbig", [128, 6144], BF16)
    wcol = [wbig[:, b * 1024:(b + 1) * 1024] for b in range(2)]
    wcg = [wbig[:, 2048 + b * 1024:2048 + (b + 1) * 1024] for b in range(2)]
    wrow = wbig[:, 0:4096]
    w2c = [wbig[:, b * 2816:(b + 1) * 2816] for b in range(2)]
    bd_s = p.sb("bd_s", [128, 128], F32)
    eg_s = p.sb("eg_s", [16, 8 * 128], BF16)
    idb = p.sb("idb", [128, 128], BF16)
    M2b = p.sb("M2b", [128, 384], BF16)
    qs_s = p.sb("qs_s", [128, 4], F32)
    lng_s = p.sb("lng_s", [128, 16], F32)
    b1a_s = p.sb("b1a_s", [128, 22], F32)
    b1g_s = p.sb("b1g_s", [128, 22], F32)
    b2_s = p.sb("b2_s", [128, 8], F32)

    p.add_consts([EPS, 8.0])
    psb0 = p.ps("psb0", [128, T])
    psb1 = p.ps("psb1", [128, T])
    psst = psb1[0:16, :]
    psot = psb1[0:64, 0:128]

    lds = p.sem()
    emit_loads(
        p, lds,
        plain=[(bd_s[:, :], bd_c), (qs_s[:, 0:2], qs_c), (lng_s[:, :], lng_c),
               (b1a_s[:, :], b1a), (b1g_s[:, :], b1g), (b2_s[:, :], b2_c)],
        cast=[(CQ[:, :], CQ_c), (SQ[:, :], SQ_c), (CK[:, :], CK_c),
              (SK[:, :], SK_c), (M2b[:, :], M2_c), (idb[:, :], id_c),
              (eg_s[:, :], eg_c)],
    )
    with nc.Block() as block:
        @block.gpsimd
        def _(gp):
            gp.memset(ones_row[:, :], 1.0)
            gp.memset(ones_col[:, :], 1.0)
            gp.memset(stage[0][:, 0:1], 0.0)
            gp.memset(stage[1][:, 0:1], 0.0)
            p.emit_const_init(gp)
        @block.vector
        def _(v):
            lds.wait(v)
            v.stream_shuffle(qs_s[:, 2:3], qs_s[:, 0:1], SWAP_MASK)
            v.stream_shuffle(qs_s[:, 3:4], qs_s[:, 1:2], SWAP_MASK)
    with nc.Block() as block:
        @block.vector
        def _(v):
            v.tensor_scalar_mul(CQ[:, :], CQ[:, :], qs_s[:, 0:1])
            v.tensor_scalar_mul(SQ[:, :], SQ[:, :], qs_s[:, 2:3])
            v.tensor_scalar_mul(CK[:, :], CK[:, :], qs_s[:, 1:2])
            v.tensor_scalar_mul(SK[:, :], SK[:, :], qs_s[:, 3:4])

    idf = p.sb("idf", [128, 128], F32)
    with nc.Block() as block:
        @block.sync
        def _(sync):
            lds.inc(sync.dma_start(out=idf[:, :], in_=id_c), 16)
        @block.vector
        def _(v):
            lds.wait(v)

    s_y = p.sem()
    for g in range(NT):
        with nc.Block() as block:
            @block.sync
            def _(sync):
                s_y.inc(sync.dma_start(out=stage[g % 2][:, 1:T + 1],
                                       in_=yT[g * 128:(g + 1) * 128, :]), 16)
            @block.vector
            def _(v):
                s_y.wait(v)
                sh = 0 if g >= 4 else 1
                v.tensor_copy(ybf[g][:, :], stage[g % 2][:, sh:sh + T])

    s_w = p.sem()
    def load_wcols(eng, buf, src_d, col0, width=128, nk=NT, rk2=128):
        for k in range(nk):
            s_w.inc(eng.dma_start(
                out=buf[:, k * width:(k + 1) * width][0:rk2, :],
                in_=src_d[k * 128:k * 128 + rk2, col0:col0 + width]), 16)

    with nc.Block() as block:
        @block.gpsimd
        def _(gp):
            load_wcols(gp, wcol[0], Wqkv, 0)
    for m in range(16):
        buf = wcol[m % 2]
        nbuf = wcol[(m + 1) % 2]
        wv = s_w.v
        with nc.Block() as block:
            @block.gpsimd
            def _(gp):
                if m < 15:
                    load_wcols(gp, nbuf, Wqkv, (m + 1) * 128)
            @block.tensor
            def _(t):
                t.wait_ge(s_w.h, wv)
                for ch in range(NCH):
                    sl = slice(ch * 512, (ch + 1) * 512)
                    for k in range(NT):
                        t.matmul(psb0[:, sl], buf[:, k * 128:(k + 1) * 128],
                                 ybf[k][:, sl], start=(k == 0), stop=(k == NT - 1))
        out_t = qb[m] if m < 8 else kb[m - 8]
        with nc.Block() as block:
            @block.scalar
            def _(s):
                s.activation(out_t[:, :], psb0[:, :], AF.Copy)
                s.activation(tmp[:, :], psb0[:, :], AF.Square)
        with nc.Block() as block:
            @block.tensor
            def _(t):
                mm = m % 8
                for ch in range(NCH):
                    sl = slice(ch * 512, (ch + 1) * 512)
                    t.matmul(psst[:, sl], bd_s[:, mm * 16:(mm + 1) * 16],
                             tmp[:, sl], start=(mm == 0), stop=(mm == 7))
        if m == 7 or m == 15:
            dst = rq if m == 7 else rk
            with nc.Block() as block:
                @block.scalar
                def _(s):
                    s.activation(dst, psst[:, :], AF.Sqrt)
            with nc.Block() as block:
                @block.vector
                def _(v):
                    v.tensor_scalar_max(dst, dst, 1e-12)
            with nc.Block() as block:
                @block.vector
                def _(v):
                    v.reciprocal(dst, dst)

    for side in range(2):
        src_t = qb if side == 0 else kb
        Cp = CQ if side == 0 else CK
        Sp = SQ if side == 0 else SK
        rr = rq if side == 0 else rk
        with nc.Block() as block:
            @block.vector
            def _(v):
                v.tensor_copy(rqb[:, :], rr)
        for g in range(NT):
            with nc.Block() as block:
                @block.tensor
                def _(t):
                    for ch in range(NCH):
                        sl = slice(ch * 512, (ch + 1) * 512)
                        t.matmul(psb0[:, sl], eg_s[:, g * 128:(g + 1) * 128],
                                 rqb[:, sl], start=True, stop=True)
                @block.vector
                def _(v):
                    v.stream_shuffle(tshb[:, :], src_t[g][:, :], SWAP_MASK)
                    v.tensor_tensor(tshb2[:, :], src_t[g][:, :], Cp[:, :], ALU.mult)
                    v.tensor_tensor(tshb[:, :], tshb[:, :], Sp[:, :], ALU.mult)
            with nc.Block() as block:
                @block.vector
                def _(v):
                    v.tensor_add(tshb2[:, :], tshb2[:, :], tshb[:, :])
                    v.tensor_tensor(src_t[g][:, :], tshb2[:, :], psb0[:, :],
                                    ALU.mult)
        for g in range(NT):
            with nc.Block() as block:
                @block.scalar
                def _(s):
                    s.activation(tmp[:, :], src_t[g][:, :], AF.Square)
            with nc.Block() as block:
                @block.tensor
                def _(t):
                    for ch in range(NCH):
                        sl = slice(ch * 512, (ch + 1) * 512)
                        t.matmul(psb1[64:80, sl], bd_s[:, g * 16:(g + 1) * 16],
                                 tmp[:, sl], start=(g == 0), stop=(g == NT - 1))
        with nc.Block() as block:
            @block.scalar
            def _(s):
                s.activation(r3 if side == 0 else k3n, psb1[64:80, :], AF.Sqrt)
    with nc.Block() as block:
        @block.gpsimd
        def _(gp):
            gp.memset(khm[:, 0:2], 0.0)
        @block.vector
        def _(v):
            v.reduce_max(khm[:, 2:18], k3n.rearrange("p (c u) -> p c u", u=128),
                         axis=AX.X)
    with nc.Block() as block:
        @block.vector
        def _(v):
            v.tensor_max(kht, khm[:, 0:16], khm[:, 1:17])
    with nc.Block() as block:
        @block.vector
        def _(v):
            v.tensor_max(khw, kht, khm[:, 2:18])
    with nc.Block() as block:
        @block.vector
        def _(v):
            for qc in range(16):
                v.scalar_tensor_tensor(
                    r3[:, qc * 128:(qc + 1) * 128],
                    r3[:, qc * 128:(qc + 1) * 128], -8.0,
                    khw[:, qc:qc + 1].broadcast_to([16, 128]),
                    ALU.mult, ALU.mult)
    with nc.Block() as block:
        @block.tensor
        def _(t):
            for qc in range(16):
                t.transpose(psb0[:, qc * 16:(qc + 1) * 16],
                            r3[:, qc * 128:(qc + 1) * 128], idf[64:80, 0:16])
    with nc.Block() as block:
        @block.vector
        def _(v):
            v.tensor_copy(mhc[:, :], psb0[:, 0:256])

    for oc in range(2):
        with nc.Block() as block:
            @block.gpsimd
            def _(gp):
                for k in range(NT):
                    s_w.inc(gp.dma_start(
                        out=wrow[:, k * 512:(k + 1) * 512],
                        in_=Wqkv[k * 128:(k + 1) * 128,
                                 2048 + oc * 512:2560 + oc * 512]), 16)
        for u in range(16):
            with nc.Block() as block:
                @block.tensor
                def _(t):
                    t.wait_ge(s_w.h, s_w.v)
                    for k in range(NT):
                        t.matmul(psb0[:, (u % 4) * 512:(u % 4) * 512 + 512],
                                 ybf[k][:, u * 128:(u + 1) * 128],
                                 wrow[:, k * 512:(k + 1) * 512],
                                 start=(k == 0), stop=(k == NT - 1))
            with nc.Block() as block:
                @block.vector
                def _(v):
                    v.tensor_copy(vtm[u][:, oc * 512:(oc + 1) * 512],
                                  psb0[:, (u % 4) * 512:(u % 4) * 512 + 512])

    s_sc = p.sem(); s_zi = p.sem(); s_exp = p.sem(); s_db = p.sem()
    s_tp = p.sem(); s_ptv = p.sem(); s_ot = p.sem()
    b_sc, b_zi, b_exp, b_db, b_tp, b_ptv, b_ot = (
        s_sc.v, s_zi.v, s_exp.v, s_db.v, s_tp.v, s_ptv.v, s_ot.v)
    iters = [(h, qc) for h in range(16) for qc in range(16)]
    with nc.Block() as block:
        @block.tensor
        def _(t):
            for i, (h, qc) in enumerate(iters):
                gt, pb = h // 2, 64 * (h % 2)
                qsl = slice(qc * 128, (qc + 1) * 128)
                kr0 = max(0, qc * 128 - 256)
                w = qc * 128 + 128 - kr0
                nu = w // 128
                t.wait_ge(s_ptv.h, b_ptv + i)
                t.matmul(psb0[:, 0:w], qb[gt][pb:pb + 64, qsl],
                         kb[gt][pb:pb + 64, kr0:kr0 + w], start=True, stop=False)
                t.matmul(psb0[:, 0:w], idb[:, :], M2b[:, 384 - w:384],
                         start=False, stop=True).then_inc(s_sc.h, 1)
                t.wait_ge(s_exp.h, b_exp + i + 1)
                t.wait_ge(s_db.h, b_db + i + 1)
                for u in range(nu):
                    ins = t.matmul(psb0[:, 512 + u * 128:512 + (u + 1) * 128],
                                   Pb[:, u * 128:(u + 1) * 128], Db[:, :],
                                   start=True, stop=True)
                ins.then_inc(s_tp.h, 1)
                t.wait_ge(s_ptv.h, b_ptv + i + 1)
                for u in range(nu):
                    ins = t.matmul(psot[:, :],
                                   vtm[kr0 // 128 + u][:, 64 * h:64 * h + 64],
                                   PT[:, u * 128:(u + 1) * 128],
                                   start=(u == 0), stop=(u == nu - 1))
                ins.then_inc(s_ot.h, 1)
        @block.scalar
        def _(s):
            for i, (h, qc) in enumerate(iters):
                kr0 = max(0, qc * 128 - 256)
                w = qc * 128 + 128 - kr0
                s.wait_ge(s_sc.h, b_sc + i + 1)
                if i > 0:
                    s.wait_ge(s_tp.h, b_tp + i)
                s.activation(Pb[:, 0:w], psb0[:, 0:w], AF.Exp,
                             bias=mhc[:, qc * 16 + h:qc * 16 + h + 1],
                             scale=8.0, accum_out=zcol[:, 0:1]
                             ).then_inc(s_exp.h, 1)
        @block.vector
        def _(v):
            for i, (h, qc) in enumerate(iters):
                gt, pb = h // 2, 64 * (h % 2)
                qsl = slice(qc * 128, (qc + 1) * 128)
                kr0 = max(0, qc * 128 - 256)
                w = qc * 128 + 128 - kr0
                v.wait_ge(s_exp.h, b_exp + i + 1)
                v.reciprocal(zcol[:, 1:2], zcol[:, 0:1]).then_inc(s_zi.h, 1)
                v.wait_ge(s_zi.h, b_zi + i + 1)
                v.tensor_scalar_mul(Db[:, :], idf[:, :], zcol[:, 1:2]
                                    ).then_inc(s_db.h, 1)
                v.wait_ge(s_tp.h, b_tp + i + 1)
                v.tensor_copy(PT[:, 0:w], psb0[:, 512:512 + w]
                              ).then_inc(s_ptv.h, 1)
                v.wait_ge(s_ot.h, b_ot + i + 1)
                v.tensor_copy(ofm[gt][pb:pb + 64, qsl], psot[:, :])
    for s in (s_sc, s_zi, s_exp, s_db, s_tp, s_ptv, s_ot):
        s.v += len(iters)

    with nc.Block() as block:
        @block.gpsimd
        def _(gp):
            load_wcols(gp, wcol[0], Wop, 0)
    s_st = p.sem()
    for m in range(NT):
        buf = wcol[m % 2]
        nbuf = wcol[(m + 1) % 2]
        wv = s_w.v
        yv = s_y.v
        with nc.Block() as block:
            @block.gpsimd
            def _(gp):
                if m < NT - 1:
                    load_wcols(gp, nbuf, Wop, (m + 1) * 128)
            @block.sync
            def _(sync):
                s_y.inc(sync.dma_start(out=stage[m % 2][:, 1:T + 1],
                                       in_=yT[m * 128:(m + 1) * 128, :]), 16)
            @block.tensor
            def _(t):
                t.wait_ge(s_w.h, wv)
                for ch in range(NCH):
                    sl = slice(ch * 512, (ch + 1) * 512)
                    for k in range(NT):
                        t.matmul(psb0[:, sl], buf[:, k * 128:(k + 1) * 128],
                                 ofm[k][:, sl], start=(k == 0), stop=(k == NT - 1))
        with nc.Block() as block:
            @block.vector
            def _(v):
                s_y.wait(v)
                v.tensor_tensor(stage[m % 2][:, 1:T + 1],
                                stage[m % 2][:, 1:T + 1], psb0[:, :], ALU.add)
        with nc.Block() as block:
            @block.sync
            def _(sync):
                s_st.inc(sync.dma_start(out=ymid[m * 128:(m + 1) * 128, :],
                                        in_=stage[m % 2][:, 1:T + 1]), 16)

    rows = None
    with nc.Block() as block:
        @block.sync
        def _(sync):
            s_st.wait(sync)
    for g in range(NT):
        with nc.Block() as block:
            @block.sync
            def _(sync):
                s_y.inc(sync.dma_start(out=stage[g % 2][:, 1:T + 1],
                                       in_=ymid[g * 128:(g + 1) * 128, :]), 16)
            @block.scalar
            def _(s):
                s_y.wait(s)
                sh = 0 if g >= 4 else 1
                s.activation(tmp[:, :], stage[g % 2][:, sh:sh + T], AF.Square)
        with nc.Block() as block:
            @block.tensor
            def _(t):
                sh = 0 if g >= 4 else 1
                for ch in range(NCH):
                    sl = slice(ch * 512, (ch + 1) * 512)
                    t.matmul(psb0[0:1, sl], ones_col[:, :],
                             stage[g % 2][:, sh:sh + T][:, sl],
                             start=(g == 0), stop=(g == NT - 1))
                    t.matmul(psb1[0:1, sl], ones_col[:, :], tmp[:, sl],
                             start=(g == 0), stop=(g == NT - 1))
    mrow = tmp[0:1, 0:T]
    vrow = stage[0][0:1, 1:T + 1]
    scr = stage[1][0:1, 1:T + 1]
    with nc.Block() as block:
        @block.vector
        def _(v):
            v.tensor_scalar_mul(mrow, psb0[0:1, :], 1.0 / D)
            v.tensor_scalar_mul(vrow, psb1[0:1, :], 1.0 / D)
            v.tensor_tensor(scr, mrow, mrow, ALU.mult)
            v.tensor_sub(vrow, vrow, scr)
    with nc.Block() as block:
        @block.scalar
        def _(s):
            s.activation(vrow, vrow, AF.Sqrt, bias=EPS)
    with nc.Block() as block:
        @block.vector
        def _(v):
            v.reciprocal(vrow, vrow)
            v.scalar_tensor_tensor(mrow, mrow, -1.0, vrow, ALU.mult, ALU.mult)
    with nc.Block() as block:
        @block.tensor
        def _(t):
            for ch in range(NCH):
                sl = slice(ch * 512, (ch + 1) * 512)
                t.matmul(psb0[:, sl], ones_row[:, :], vrow[:, sl],
                         start=True, stop=True)
                t.matmul(psb1[:, sl], ones_row[:, :], mrow[:, sl],
                         start=True, stop=True)
    for g in range(NT):
        with nc.Block() as block:
            @block.sync
            def _(sync):
                s_y.inc(sync.dma_start(out=stage[g % 2][:, 1:T + 1],
                                       in_=ymid[g * 128:(g + 1) * 128, :]), 16)
            @block.vector
            def _(v):
                s_y.wait(v)
                sh = 0 if g >= 4 else 1
                v.tensor_tensor(tmp[:, :], stage[g % 2][:, sh:sh + T],
                                psb0[:, :], ALU.mult)
                v.scalar_tensor_tensor(tmp[:, :], tmp[:, :], 1.0,
                                       psb1[:, :], ALU.mult, ALU.add)
                v.tensor_scalar(xf[g][:, :], tmp[:, :], lng_s[:, g:g + 1],
                                lng_s[:, 8 + g:9 + g], ALU.mult, ALU.add)

    def load_w1(eng, bufa, bufg, mi):
        off, wd = A_CH[mi]
        for k in range(NT):
            s_w.inc(eng.dma_start(
                out=bufa[:, k * 128:k * 128 + wd],
                in_=W1[k * 128:(k + 1) * 128, off:off + wd]), 16)
            s_w.inc(eng.dma_start(
                out=bufg[:, k * 128:k * 128 + wd],
                in_=W1[k * 128:(k + 1) * 128, 2730 + off:2730 + off + wd]), 16)
    with nc.Block() as block:
        @block.gpsimd
        def _(gp):
            load_w1(gp, wcol[0], wcg[0], 0)
    for mi in range(22):
        off, wd = A_CH[mi]
        bufa, bufg = wcol[mi % 2], wcg[mi % 2]
        wv = s_w.v
        with nc.Block() as block:
            @block.gpsimd
            def _(gp):
                if mi < 21:
                    load_w1(gp, wcol[(mi + 1) % 2], wcg[(mi + 1) % 2], mi + 1)
            @block.tensor
            def _(t):
                t.wait_ge(s_w.h, wv)
                for ch in range(NCH):
                    sl = slice(ch * 512, (ch + 1) * 512)
                    for k in range(NT):
                        t.matmul(psb0[0:wd, sl], bufa[:, k * 128:k * 128 + wd],
                                 xf[k][:, sl], start=(k == 0), stop=(k == NT - 1))
                    for k in range(NT):
                        t.matmul(psb1[0:wd, sl], bufg[:, k * 128:k * 128 + wd],
                                 xf[k][:, sl], start=(k == 0), stop=(k == NT - 1))
        with nc.Block() as block:
            @block.scalar
            def _(s):
                s.activation(tmp[0:wd, :], psb1[0:wd, :], AF.Gelu,
                             bias=b1g_s[0:wd, mi:mi + 1])
        with nc.Block() as block:
            @block.vector
            def _(v):
                v.scalar_tensor_tensor(hb[mi][0:wd, :], psb0[0:wd, :],
                                       b1a_s[0:wd, mi:mi + 1], tmp[0:wd, :],
                                       ALU.add, ALU.mult)

    def load_w2(eng, buf, m):
        for ki in range(22):
            off, wd = A_CH[ki]
            s_w.inc(eng.dma_start(
                out=buf[:, ki * 128:(ki + 1) * 128][0:wd, :],
                in_=W2[off:off + wd, m * 128:(m + 1) * 128]), 16)
    with nc.Block() as block:
        @block.gpsimd
        def _(gp):
            load_w2(gp, w2c[0], 0)
    so = p.sem()
    for m in range(NT):
        buf = w2c[m % 2]
        wv = s_w.v
        with nc.Block() as block:
            @block.gpsimd
            def _(gp):
                if m < NT - 1:
                    load_w2(gp, w2c[(m + 1) % 2], m + 1)
            @block.sync
            def _(sync):
                s_y.inc(sync.dma_start(out=stage[m % 2][:, 1:T + 1],
                                       in_=ymid[m * 128:(m + 1) * 128, :]), 16)
            @block.tensor
            def _(t):
                t.wait_ge(s_w.h, wv)
                for ch in range(NCH):
                    sl = slice(ch * 512, (ch + 1) * 512)
                    for ki in range(22):
                        off, wd = A_CH[ki]
                        t.matmul(psb0[:, sl],
                                 buf[:, ki * 128:(ki + 1) * 128][0:wd, :],
                                 hb[ki][0:wd, sl],
                                 start=(ki == 0), stop=(ki == 21))
        with nc.Block() as block:
            @block.vector
            def _(v):
                s_y.wait(v)
                v.scalar_tensor_tensor(stage[m % 2][:, 1:T + 1], psb0[:, :],
                                       b2_s[:, m:m + 1],
                                       stage[m % 2][:, 1:T + 1],
                                       ALU.add, ALU.add)
        with nc.Block() as block:
            @block.sync
            def _(sync):
                so.inc(sync.dma_start(out=oT[m * 128:(m + 1) * 128, :],
                                      in_=stage[m % 2][:, 1:T + 1]), 16)
    with nc.Block() as block:
        @block.sync
        def _(sync):
            so.wait(sync)
    return p


PERM64 = np.empty(64, np.int64)
PERM64[0::2] = np.arange(32)
PERM64[1::2] = np.arange(32, 64)


def rotary_tables():
    i = np.arange(32, dtype=np.float64)
    inv_freq = 1.0 / 10000.0 ** (2 * i / 64.0)
    t = np.arange(T, dtype=np.float64)
    f = t[:, None] * inv_freq[None, :]
    base = (2 * i + 0.4 * 64) / (1.4 * 64)
    power = (t - T // 2) / 128.0
    scale = base[None, :] ** power[:, None]
    inv = 1.0 / scale
    CQ = np.empty((T, 64)); SQ = np.empty((T, 64))
    CK = np.empty((T, 64)); SK = np.empty((T, 64))
    CQ[:, 0::2] = np.cos(f) * scale; CQ[:, 1::2] = np.cos(f) * scale
    SQ[:, 0::2] = -np.sin(f) * scale; SQ[:, 1::2] = np.sin(f) * scale
    CK[:, 0::2] = np.cos(f) * inv; CK[:, 1::2] = np.cos(f) * inv
    SK[:, 0::2] = -np.sin(f) * inv; SK[:, 1::2] = np.sin(f) * inv
    out = []
    for Z in (CQ, SQ, CK, SK):
        Z2 = np.tile(Z, (1, 2))
        out.append(_f32c(Z2.T))
    return out


def inner_mask():
    M = np.zeros((128, 384), np.float32)
    ii = np.arange(128)[:, None]
    jj = np.arange(128)[None, :]
    M[:, 0:128] = np.where(jj < ii, -1e9, 0.0)
    M[:, 256:384] = np.where(jj > ii, -1e9, 0.0)
    return M


def inner_consts():
    bd = np.zeros((128, 128), np.float32)
    for g in range(8):
        for hh in range(16):
            if hh // 2 == g:
                pb = 64 * (hh % 2)
                bd[pb:pb + 64, g * 16 + hh] = 1.0
    eg = np.zeros((16, 8 * 128), np.float32)
    for g in range(8):
        for pp in range(128):
            eg[2 * g + pp // 64, g * 128 + pp] = 1.0
    CQ, SQ, CK, SK = rotary_tables()
    return dict(bd_c=bd, eg_c=eg, id_c=np.eye(128, dtype=np.float32),
                CQ_c=CQ, SQ_c=SQ, CK_c=CK, SK_c=SK, M2_c=inner_mask())


def inner_inputs(yT_c, li, ins, consts):
    Wqkv = np.asarray(ins["L0_ffn_Wqkv"][li], np.float64).copy()
    colperm = np.concatenate([h * 64 + PERM64 for h in range(16)])
    Wp = Wqkv.copy()
    Wp[:, 0:1024] = Wqkv[:, 0:1024][:, colperm]
    Wp[:, 1024:2048] = Wqkv[:, 1024:2048][:, colperm]
    Wo = np.asarray(ins["L0_ffn_Wo"][li], np.float64)
    qs = np.asarray(ins["L0_ffn_q_scale"][li])[PERM64]
    ks = np.asarray(ins["L0_ffn_k_scale"][li])[PERM64]
    qsc = np.stack([np.tile(qs, 2), np.tile(ks, 2)], axis=1)
    lngc = np.concatenate(
        [np.asarray(ins["L0_ffn_ln_g"][li]).reshape(8, 128).T,
         np.asarray(ins["L0_ffn_ln_b"][li]).reshape(8, 128).T], axis=1)
    b1 = np.asarray(ins["L0_ffn_b1"][li])
    b1a = np.zeros((128, 22), np.float32)
    b1g = np.zeros((128, 22), np.float32)
    for mi, (off, wd) in enumerate(A_CH):
        b1a[0:wd, mi] = b1[off:off + wd]
        b1g[0:wd, mi] = b1[2730 + off:2730 + off + wd]
    return {
        "yT": yT_c,
        "Wqkv": _f32c(Wp),
        "Wop": _f32c(Wo),
        "qs_c": _f32c(qsc),
        "lng_c": _f32c(lngc),
        "W1": _f32c(ins["L0_ffn_W1"][li]),
        "b1a": b1a, "b1g": b1g,
        "W2": _f32c(ins["L0_ffn_W2"][li]),
        "b2_c": _f32c(np.asarray(ins["L0_ffn_b2"][li]).reshape(8, 128).T),
        **consts,
    }



LINQKV_INS = ["xT", "WqT", "WkT", "WvT", "bq_c", "bk_c", "bv_c", "beta_e"]
LINQKV_SHAPES = dict(xT=[D, T], WqT=[D, D], WkT=[D, D], WvT=[D, D],
                     bq_c=[128, 8], bk_c=[128, 8], bv_c=[128, 8],
                     beta_e=[128, 8])


def build_lin_qkv():
    p = Prog()
    io = {n: p.din(n, LINQKV_SHAPES[n]) for n in LINQKV_INS}
    io["qo"] = p.dout("qo", [D, T], BF16)
    io["ko"] = p.dout("ko", [D, T], BF16)
    io["vo"] = p.dout("vo", [D, T], BF16)
    emit_lin_qkv(p, io)
    return p


def emit_lin_qkv(p, io):
    nc = p.nc
    (xT, WqT, WkT, WvT, bq_c, bk_c, bv_c, beta_e) = [io[n] for n in LINQKV_INS]
    qo = io["qo"]; ko = io["ko"]; vo = io["vo"]

    pool = p.sb("pool", [128, 4 * T], F32)
    poolb = pool[:, :].bitcast(BF16)
    xhat = [poolb[:, g * T:(g + 1) * T] for g in range(NT)]
    tmp = p.sb("tmp", [128, T], F32)
    t2 = p.sb("t2", [128, T], F32)
    rows = p.sb("rows", [1, 2 * T], F32)
    ones_row = p.sb("ones_row", [1, 128], F32)
    ones_col = p.sb("ones_col", [128, 1], F32)
    wbuf = [p.sb("wb%d" % k, [128, D], BF16) for k in range(NT)]
    scol = p.sb("scol", [128, 8], F32)
    rcol = p.sb("rcol", [128, 8], F32)
    bq_s = p.sb("bq_s", [128, 8], F32)
    bk_s = p.sb("bk_s", [128, 8], F32)
    bv_s = p.sb("bv_s", [128, 8], F32)
    brq = p.sb("brq", [128, 8], F32)
    brk = p.sb("brk", [128, 8], F32)
    outb = p.sb("outb", [128, T], BF16)
    beta_sb = p.sb("beta_sb", [128, 8], F32)
    xres = [pool[:, 0:T], pool[:, T:2 * T]]

    LN8 = float(np.log(8.0))
    p.add_consts([EPS, LN8])
    psb0 = p.ps("psb0", [128, T])
    psb1 = p.ps("psb1", [128, T])

    lds = p.sem()
    emit_loads(p, lds,
               plain=[(beta_sb[:, :], beta_e), (bq_s[:, :], bq_c),
                      (bk_s[:, :], bk_c), (bv_s[:, :], bv_c)])
    with nc.Block() as block:
        @block.gpsimd
        def _(gp):
            gp.memset(ones_row[:, :], 1.0)
            gp.memset(ones_col[:, :], 1.0)
            p.emit_const_init(gp)
        @block.scalar
        def _(s):
            lds.wait(s)
            s.activation(scol[:, :], beta_sb[:, :], AF.Exp, bias=LN8)
    with nc.Block() as block:
        @block.vector
        def _(v):
            v.reciprocal(rcol[:, :], scol[:, :])
    with nc.Block() as block:
        @block.vector
        def _(v):
            v.tensor_mul(brq[:, :], bq_s[:, :], rcol[:, :])
            v.tensor_mul(brk[:, :], bk_s[:, :], rcol[:, :])

    s_x = p.sem()
    for g in range(NT):
        b = g % 2
        with nc.Block() as block:
            @block.sync
            def _(sync):
                s_x.inc(sync.dma_start(out=xres[b][:, :],
                                       in_=xT[g * 128:(g + 1) * 128, :]), 16)
            @block.scalar
            def _(s):
                s_x.wait(s)
                s.activation(tmp[:, :], xres[b][:, :], AF.Square)
        with nc.Block() as block:
            @block.tensor
            def _(t):
                for ch in range(NCH):
                    sl = slice(ch * 512, (ch + 1) * 512)
                    t.matmul(psb0[0:1, sl], ones_col[:, :], xres[b][:, sl],
                             start=(g == 0), stop=(g == NT - 1))
                    t.matmul(psb1[0:1, sl], ones_col[:, :], tmp[:, sl],
                             start=(g == 0), stop=(g == NT - 1))
    mrow = rows[0:1, 0:T]
    vrow = rows[0:1, T:2 * T]
    with nc.Block() as block:
        @block.vector
        def _(v):
            v.tensor_scalar_mul(mrow, psb0[0:1, :], 1.0 / D)
            v.tensor_scalar_mul(vrow, psb1[0:1, :], 1.0 / D)
            v.tensor_tensor(tmp[0:1, 0:T], mrow, mrow, ALU.mult)
            v.tensor_sub(vrow, vrow, tmp[0:1, 0:T])
    with nc.Block() as block:
        @block.scalar
        def _(s):
            s.activation(vrow, vrow, AF.Sqrt, bias=EPS)
    with nc.Block() as block:
        @block.vector
        def _(v):
            v.reciprocal(vrow, vrow)
            v.scalar_tensor_tensor(mrow, mrow, -1.0, vrow, ALU.mult, ALU.mult)
    with nc.Block() as block:
        @block.tensor
        def _(t):
            for ch in range(NCH):
                sl = slice(ch * 512, (ch + 1) * 512)
                t.matmul(psb0[:, sl], ones_row[:, :], vrow[:, sl],
                         start=True, stop=True)
                t.matmul(psb1[:, sl], ones_row[:, :], mrow[:, sl],
                         start=True, stop=True)
    for g in range(NT):
        b = g % 2
        with nc.Block() as block:
            @block.sync
            def _(sync):
                s_x.inc(sync.dma_start(out=tmp[:, :],
                                       in_=xT[g * 128:(g + 1) * 128, :]), 16)
            @block.vector
            def _(v):
                s_x.wait(v)
                v.tensor_tensor(t2[:, :], tmp[:, :], psb0[:, :], ALU.mult)
                v.scalar_tensor_tensor(xhat[g][:, :], t2[:, :], 1.0,
                                       psb1[:, :], ALU.mult, ALU.add)

    s_w = p.sem()
    st = p.sem()
    for which, Wd, bc, br, outd in ((0, WqT, bq_s, brq, qo), (1, WkT, bk_s, brk, ko),
                                    (2, WvT, bv_s, None, vo)):
        with nc.Block() as block:
            @block.gpsimd
            def _(gp):
                for k in range(NT):
                    s_w.inc(gp.dma_start(out=wbuf[k][:, :],
                                         in_=Wd[k * 128:(k + 1) * 128, :]), 16)
            @block.tensor
            def _(t):
                s_w.wait(t)
        for m in range(NT):
            with nc.Block() as block:
                @block.tensor
                def _(t):
                    for ch in range(NCH):
                        sl = slice(ch * 512, (ch + 1) * 512)
                        for k in range(NT):
                            t.matmul(psb0[:, sl], wbuf[k][:, m * 128:(m + 1) * 128],
                                     xhat[k][:, sl], start=(k == 0),
                                     stop=(k == NT - 1))
            if which < 2:
                with nc.Block() as block:
                    @block.scalar
                    def _(s):
                        s.activation(tmp[:, :], psb0[:, :], AF.Abs,
                                     bias=br[:, m:m + 1], scale=rcol[:, m:m + 1])
                        s.activation(tmp[:, :], tmp[:, :], AF.Exp, scale=-1.0)
                        s.activation(tmp[:, :], tmp[:, :], AF.Ln, bias=1.0)
                        s.activation(t2[:, :], psb0[:, :], AF.Relu,
                                     bias=br[:, m:m + 1], scale=rcol[:, m:m + 1])
                with nc.Block() as block:
                    @block.vector
                    def _(v):
                        v.tensor_add(outb[:, :], tmp[:, :], t2[:, :])
            else:
                with nc.Block() as block:
                    @block.vector
                    def _(v):
                        v.tensor_scalar_add(outb[:, :], psb0[:, :], bc[:, m:m + 1])
            with nc.Block() as block:
                @block.sync
                def _(sync):
                    st.inc(sync.dma_start(out=outd[m * 128:(m + 1) * 128, :],
                                          in_=outb[:, :]), 16)
    with nc.Block() as block:
        @block.sync
        def _(sync):
            st.wait(sync)
    return p


def build_lin_main():
    p = Prog()
    nc = p.nc
    io = dict(
        xT=p.din("xT", [D, T]),
        qtb=nc.dram_tensor("qtb", [D, T], BF16, kind="ExternalInput").ap(),
        kall=nc.dram_tensor("kall", [8 * D, T], BF16, kind="ExternalInput").ap(),
        vall=nc.dram_tensor("vall", [8 * D, T], BF16, kind="ExternalInput").ap(),
        WoT=p.din("WoT", [D, D]), bo_c=p.din("bo_c", [128, 8]),
        mask_c=p.din("mask_c", [16, 8]), bd_c=p.din("bd_c", [128, 128]),
        eg_c=p.din("eg_c", [16, 8 * 128]), oT=p.dout("oT", [D, T]))
    emit_lin_main(p, io, kvstride=D)
    return p


def emit_lin_main(p, io, kvstride=D):
    nc = p.nc
    xT = io["xT"]; qtb = io["qtb"]; kall = io["kall"]; vall = io["vall"]
    WoT = io["WoT"]; bo_c = io["bo_c"]; mask_c = io["mask_c"]
    bd_c = io["bd_c"]; eg_c = io["eg_c"]; oT = io["oT"]

    qt = [p.sb("qt%d" % g, [128, T], BF16) for g in range(NT)]
    kt = [p.sb("kt%d" % g, [128, T], BF16) for g in range(NT)]
    vt = [p.sb("vt%d" % g, [128, T], BF16) for g in range(NT)]
    num = [p.sb("num%d" % g, [128, T], F32) for g in range(NT)]
    den = p.sb("den", [16, T], F32)
    csr = p.sb("csr", [16, T], F32)
    csb = p.sb("csb", [16, T], BF16)
    pb = p.sb("pb", [128, T], BF16)
    tmp = p.sb("tmp", [128, T], F32)
    bdb = p.sb("bdb", [128, 128], BF16)
    eg_s = p.sb("eg_s", [16, 8 * 128], BF16)
    mask_s = p.sb("mask_s", [16, 8], F32)
    bo_s = p.sb("bo_s", [128, 8], F32)
    wbuf2 = [p.sb("w2%d" % b, [128, 1024], BF16) for b in range(2)]
    outb = p.sb("outb", [128, T], BF16)

    psb0 = p.ps("psb0", [128, T])
    psb1 = p.ps("psb1", [128, T])
    psst = psb1[0:16, :]

    lds = p.sem()
    emit_loads(
        p, lds,
        plain=[(mask_s[:, :], mask_c), (bo_s[:, :], bo_c)]
        + [(qt[g][:, :], qtb[g * 128:(g + 1) * 128, :]) for g in range(NT)],
        cast=[(bdb[:, :], bd_c), (eg_s[:, :], eg_c)],
    )
    with nc.Block() as block:
        @block.gpsimd
        def _(gp):
            gp.memset(den[:, :], 0.0)
            for g in range(NT):
                gp.memset(num[g][:, :], 0.0)
        @block.vector
        def _(v):
            lds.wait(v)

    s_k = p.sem()
    for s_idx in range(8):
        with nc.Block() as block:
            @block.sync
            def _(sync):
                for g in range(NT):
                    s_k.inc(sync.dma_start(
                        out=kt[g][:, :],
                        in_=kall[s_idx * kvstride + g * 128:
                                 s_idx * kvstride + (g + 1) * 128, :]), 16)
                    s_k.inc(sync.dma_start(
                        out=vt[g][:, :],
                        in_=vall[s_idx * kvstride + g * 128:
                                 s_idx * kvstride + (g + 1) * 128, :]), 16)
        for g in range(NT):
            with nc.Block() as block:
                @block.vector
                def _(v):
                    if g == 0:
                        s_k.wait(v)
                    v.tensor_tensor(pb[:, :], qt[g][:, :], kt[g][:, :], ALU.mult)
            with nc.Block() as block:
                @block.tensor
                def _(t):
                    for ch in range(NCH):
                        sl = slice(ch * 512, (ch + 1) * 512)
                        t.matmul(psst[:, sl], bdb[:, g * 16:(g + 1) * 16],
                                 pb[:, sl], start=(g == 0), stop=(g == NT - 1))
        with nc.Block() as block:
            @block.vector
            def _(v):
                v.tensor_scalar_mul(csr[:, :], psst[:, :],
                                    mask_s[:, s_idx:s_idx + 1])
        with nc.Block() as block:
            @block.vector
            def _(v):
                v.tensor_add(den[:, :], den[:, :], csr[:, :])
                v.tensor_copy(csb[:, :], csr[:, :])
        for g in range(NT):
            with nc.Block() as block:
                @block.tensor
                def _(t):
                    for ch in range(NCH):
                        sl = slice(ch * 512, (ch + 1) * 512)
                        t.matmul(psb0[:, sl], eg_s[:, g * 128:(g + 1) * 128],
                                 csb[:, sl], start=True, stop=True)
            with nc.Block() as block:
                @block.vector
                def _(v):
                    v.tensor_tensor(tmp[:, :], psb0[:, :], vt[g][:, :], ALU.mult)
                    v.tensor_add(num[g][:, :], num[g][:, :], tmp[:, :])
    with nc.Block() as block:
        @block.vector
        def _(v):
            v.reciprocal(csr[:, :], den[:, :])
    with nc.Block() as block:
        @block.vector
        def _(v):
            v.tensor_copy(csb[:, :], csr[:, :])
    for g in range(NT):
        with nc.Block() as block:
            @block.tensor
            def _(t):
                for ch in range(NCH):
                    sl = slice(ch * 512, (ch + 1) * 512)
                    t.matmul(psb0[:, sl], eg_s[:, g * 128:(g + 1) * 128],
                             csb[:, sl], start=True, stop=True)
        with nc.Block() as block:
            @block.vector
            def _(v):
                v.tensor_tensor(qt[g][:, :], num[g][:, :], psb0[:, :], ALU.mult)
    s_w = p.sem()
    s_y = p.sem()
    st = p.sem()
    for m in range(NT):
        wv = s_w.v
        with nc.Block() as block:
            @block.gpsimd
            def _(gp):
                for k in range(NT):
                    s_w.inc(gp.dma_start(
                        out=wbuf2[m % 2][:, k * 128:(k + 1) * 128],
                        in_=WoT[k * 128:(k + 1) * 128, m * 128:(m + 1) * 128]), 16)
            @block.sync
            def _(sync):
                s_y.inc(sync.dma_start(out=tmp[:, :],
                                       in_=xT[m * 128:(m + 1) * 128, :]), 16)
        with nc.Block() as block:
            @block.tensor
            def _(t):
                t.wait_ge(s_w.h, s_w.v)
                for ch in range(NCH):
                    sl = slice(ch * 512, (ch + 1) * 512)
                    for k in range(NT):
                        t.matmul(psb0[:, sl], wbuf2[m % 2][:, k * 128:(k + 1) * 128],
                                 qt[k][:, sl], start=(k == 0), stop=(k == NT - 1))
        with nc.Block() as block:
            @block.vector
            def _(v):
                s_y.wait(v)
                v.scalar_tensor_tensor(tmp[:, :], psb0[:, :], bo_s[:, m:m + 1],
                                       tmp[:, :], ALU.add, ALU.add)
        with nc.Block() as block:
            @block.sync
            def _(sync):
                st.inc(sync.dma_start(out=oT[m * 128:(m + 1) * 128, :],
                                      in_=tmp[:, :]), 16)
    with nc.Block() as block:
        @block.sync
        def _(sync):
            st.wait(sync)
    return p



def emit_quant_chain(p, nr, wcols, wt_ap, axn, q1, q3, scl, rscl, out_ap):
    nc = p.nc
    nb = wcols // 16
    A = lambda t: t[0:nr, 0:wcols]
    B = lambda t: t[0:nr, 0:nb]
    blk = lambda t: t[0:nr, 0:wcols].rearrange("p (b u) -> p b u", u=16)
    sbc = lambda t: t[0:nr, 0:nb].unsqueeze(2).broadcast_to([nr, nb, 16])
    with nc.Block() as block:
        @block.vector
        def _(v):
            v.reduce_max(B(scl), wt_ap.rearrange("p (b u) -> p b u", u=16),
                         axis=AX.X, apply_absolute_value=True)
    with nc.Block() as block:
        @block.vector
        def _(v):
            v.tensor_scalar(B(scl), B(scl), 1.0 / 6.0, 1e-12, ALU.mult, ALU.max)
    with nc.Block() as block:
        @block.vector
        def _(v):
            v.reciprocal(B(rscl), B(scl))
    with nc.Block() as block:
        @block.vector
        def _(v):
            v.tensor_tensor(A(q3).rearrange("p (b u) -> p b u", u=16),
                            wt_ap.rearrange("p (b u) -> p b u", u=16),
                            sbc(rscl), ALU.mult)
    with nc.Block() as block:
        @block.scalar
        def _(s):
            s.activation(A(axn), A(q3), AF.Abs)
            s.activation(A(q3), A(q3), AF.Sign)
    with nc.Block() as block:
        @block.vector
        def _(v):
            a = A(axn)
            v.tensor_single_scalar(A(q1), a, 0.25, ALU.is_gt)
            v.scalar_tensor_tensor(A(q1), a, 0.75, A(q1), ALU.is_gt, ALU.add)
            v.scalar_tensor_tensor(A(q1), a, 1.25, A(q1), ALU.is_gt, ALU.add)
            v.scalar_tensor_tensor(A(q1), a, 1.75, A(q1), ALU.is_gt, ALU.add)
            v.tensor_scalar_mul(A(q1), A(q1), 0.5)
            v.scalar_tensor_tensor(A(q1), a, 2.5, A(q1), ALU.is_gt, ALU.add)
            v.scalar_tensor_tensor(A(q1), a, 3.5, A(q1), ALU.is_gt, ALU.add)
            v.scalar_tensor_tensor(A(q1), a, 5.0, A(q1), ALU.is_gt, ALU.add)
            v.scalar_tensor_tensor(A(q1), a, 5.0, A(q1), ALU.is_gt, ALU.add)
    with nc.Block() as block:
        @block.vector
        def _(v):
            v.tensor_tensor(A(q1), A(q1), A(q3), ALU.mult)
    with nc.Block() as block:
        @block.vector
        def _(v):
            v.tensor_tensor(out_ap.rearrange("p (b u) -> p b u", u=16),
                            blk(A(q1)), sbc(scl), ALU.mult)


def build_wquant():
    p = Prog()
    io = dict(wg=p.din("wg", [2816, D]), wu=p.din("wu", [2816, D]),
              wd=p.din("wd", [D, 2816]), id_c=p.din("id_c", [128, 128]),
              wgo=p.dout("wgo", [D, 2816], BF16),
              wuo=p.dout("wuo", [D, 2816], BF16),
              wdo=p.dout("wdo", [2816, D], BF16))
    emit_wquant(p, io)
    return p


def emit_wquant(p, io):
    nc = p.nc
    wg = io["wg"]; wu = io["wu"]; wd = io["wd"]; id_c = io["id_c"]
    wgo = io["wgo"]; wuo = io["wuo"]; wdo = io["wdo"]

    wt = p.sb("wt", [128, 2816], F32)
    axn = p.sb("axn", [128, 2816], F32)
    q1 = p.sb("q1", [128, 2816], F32)
    q3 = p.sb("q3", [128, 2816], F32)
    scl = p.sb("scl", [128, 176], F32)
    rscl = p.sb("rscl", [128, 176], F32)
    wq = p.sb("wq", [128, 2816], BF16)
    evb = p.sb("evb", [128, 128], BF16)
    idb = p.sb("idb", [128, 128], BF16)
    psb1w = p.ps("psb1", [128, T])
    pst = psb1w[:, :].bitcast(BF16)[:, 0:512]

    lds = p.sem()
    emit_loads(p, lds, cast=[(idb[:, :], id_c)])
    s_x = p.sem()
    st = p.sem()

    jobs = []
    for r in range(22):
        jobs.append((wg, wgo, r, 1024, False))
    for r in range(22):
        jobs.append((wu, wuo, r, 1024, False))
    for r in range(8):
        jobs.append((wd, wdo, r, 2816, True))
    for (srcd, dstd, r, wcols, is_wd) in jobs:
        with nc.Block() as block:
            @block.sync
            def _(sync):
                s_x.inc(sync.dma_start(out=wt[:, 0:wcols],
                                       in_=srcd[r * 128:(r + 1) * 128, :]), 16)
            @block.vector
            def _(v):
                s_x.wait(v)
        emit_quant_chain(p, 128, wcols, wt[:, 0:wcols], axn, q1, q3,
                         scl, rscl, wq[:, 0:wcols])
        ncc = wcols // 128
        for cc in range(ncc):
            with nc.Block() as block:
                @block.tensor
                def _(t):
                    t.transpose(pst[:, 0:128], wq[:, cc * 128:(cc + 1) * 128],
                                idb[:, :])
            with nc.Block() as block:
                @block.vector
                def _(v):
                    v.tensor_copy(evb[:, :], pst[:, 0:128])
            with nc.Block() as block:
                @block.sync
                def _(sync):
                    st.inc(sync.dma_start(
                        out=dstd[cc * 128:(cc + 1) * 128,
                                 r * 128:(r + 1) * 128],
                        in_=evb[:, :]), 16)
    with nc.Block() as block:
        @block.sync
        def _(sync):
            st.wait(sync)
    return p


def build_fp4ffn():
    p = Prog()
    nc = p.nc
    io = dict(
        xT=p.din("xT", [D, T]),
        wgq=nc.dram_tensor("wgq", [D, 2816], BF16, kind="ExternalInput").ap(),
        wuq=nc.dram_tensor("wuq", [D, 2816], BF16, kind="ExternalInput").ap(),
        wdq=nc.dram_tensor("wdq", [2816, D], BF16, kind="ExternalInput").ap(),
        id_c=p.din("id_c", [128, 128]), oT=p.dout("oT", [D, T]))
    emit_fp4ffn(p, io)
    return p


def emit_fp4ffn(p, io):
    nc = p.nc
    xT = io["xT"]; wgq = io["wgq"]; wuq = io["wuq"]; wdq = io["wdq"]
    id_c = io["id_c"]; oT = io["oT"]

    pool = p.sb("pool2", [128, 60 * 1024], BF16)
    htm = [pool[:, tc * 2816:(tc + 1) * 2816] for tc in range(16)]
    xq_off = 16 * 2816
    xqfm = [pool[:, xq_off + g * T:xq_off + (g + 1) * T] for g in range(NT)]
    qscr = p.sb("qscr", [128, 2 * 2816], F32)
    axn = p.sb("axnf", [128, 2816], F32)
    xf = [p.sb("xf%d" % b, [128, T], F32) for b in range(2)]
    tmp = p.sb("tmpf", [128, T], F32)
    mbs = p.sb("mbs", [128, T], F32)
    scl = p.sb("sclf", [128, 176], F32)
    rscl = p.sb("rsclf", [128, 176], F32)
    ones_row = p.sb("ones_row", [1, 128], F32)
    ones_col = p.sb("ones_col", [128, 1], F32)
    idf = p.sb("idf", [128, 128], F32)
    idb = p.sb("idb", [128, 128], BF16)
    wgr = p.sb("wgr", [128, 8 * 512], BF16)
    wur = p.sb("wur", [128, 8 * 512], BF16)
    xqt = None
    wdt = [pool[:, xq_off + ki * 1024:xq_off + (ki + 1) * 1024] for ki in range(16)] \
        + [qscr[:, :].bitcast(BF16)[:, ki * 1024:(ki + 1) * 1024] for ki in range(6)]

    p.add_consts([EPS])
    psb0 = p.ps("psb0", [128, T])
    psb1 = p.ps("psb1", [128, T])
    psbq = psb1[:, :].bitcast(BF16)[:, 0:T]

    q1s = qscr[:, 0:2816]
    q3s = qscr[:, 2816:2 * 2816]

    xqt = axn[:, 0:1024].bitcast(BF16)
    lds = p.sem()
    emit_loads(p, lds, plain=[(idf[:, :], id_c)], cast=[(idb[:, :], id_c)])
    with nc.Block() as block:
        @block.gpsimd
        def _(gp):
            gp.memset(ones_row[:, :], 1.0)
            gp.memset(ones_col[:, :], 1.0)
            p.emit_const_init(gp)
        @block.vector
        def _(v):
            lds.wait(v)

    s_x = p.sem()
    for g in range(NT):
        b = g % 2
        with nc.Block() as block:
            @block.sync
            def _(sync):
                s_x.inc(sync.dma_start(out=xf[b][:, :],
                                       in_=xT[g * 128:(g + 1) * 128, :]), 16)
            @block.scalar
            def _(s):
                s_x.wait(s)
                s.activation(tmp[:, :], xf[b][:, :], AF.Square)
        with nc.Block() as block:
            @block.tensor
            def _(t):
                for ch in range(NCH):
                    sl = slice(ch * 512, (ch + 1) * 512)
                    t.matmul(psb0[0:1, sl], ones_col[:, :], xf[b][:, sl],
                             start=(g == 0), stop=(g == NT - 1))
                    t.matmul(psb1[0:1, sl], ones_col[:, :], tmp[:, sl],
                             start=(g == 0), stop=(g == NT - 1))
    mrow = tmp[0:1, 0:T]
    vrow = mbs[0:1, 0:T]
    with nc.Block() as block:
        @block.vector
        def _(v):
            v.tensor_scalar_mul(mrow, psb0[0:1, :], 1.0 / D)
            v.tensor_scalar_mul(vrow, psb1[0:1, :], 1.0 / D)
            v.tensor_tensor(xf[0][0:1, 0:T], mrow, mrow, ALU.mult)
            v.tensor_sub(vrow, vrow, xf[0][0:1, 0:T])
    with nc.Block() as block:
        @block.scalar
        def _(s):
            s.activation(vrow, vrow, AF.Sqrt, bias=EPS)
    with nc.Block() as block:
        @block.vector
        def _(v):
            v.reciprocal(vrow, vrow)
            v.scalar_tensor_tensor(mrow, mrow, -1.0, vrow, ALU.mult, ALU.mult)
    with nc.Block() as block:
        @block.tensor
        def _(t):
            for ch in range(NCH):
                sl = slice(ch * 512, (ch + 1) * 512)
                t.matmul(psb0[:, sl], ones_row[:, :], vrow[:, sl],
                         start=True, stop=True)
                t.matmul(psb1[:, sl], ones_row[:, :], mrow[:, sl],
                         start=True, stop=True)
    with nc.Block() as block:
        @block.vector
        def _(v):
            v.tensor_copy(mbs[:, :], psb1[:, :])

    for g in range(NT):
        b = g % 2
        with nc.Block() as block:
            @block.sync
            def _(sync):
                s_x.inc(sync.dma_start(out=xf[b][:, :],
                                       in_=xT[g * 128:(g + 1) * 128, :]), 16)
            @block.vector
            def _(v):
                s_x.wait(v)
                v.tensor_tensor(tmp[:, :], xf[b][:, :], psb0[:, :], ALU.mult)
                v.tensor_add(xf[b][:, :], tmp[:, :], mbs[:, :])
        with nc.Block() as block:
            @block.tensor
            def _(t):
                for tc in range(16):
                    t.transpose(psb1[:, tc * 128:(tc + 1) * 128],
                                xf[b][:, tc * 128:(tc + 1) * 128], idf[:, :])
        emit_quant_chain(p, 128, T, psb1[:, :], axn, q1s, q3s,
                         scl, rscl, xqt[:, :])
        with nc.Block() as block:
            @block.tensor
            def _(t):
                for tc in range(16):
                    t.transpose(psbq[:, tc * 128:(tc + 1) * 128],
                                xqt[:, tc * 128:(tc + 1) * 128], idb[:, :])
        with nc.Block() as block:
            @block.vector
            def _(v):
                v.tensor_copy(xqfm[g][:, :], psbq[:, :])

    IC = [(i * 512, 512) for i in range(5)] + [(2560, 256)]
    s_w = p.sem()
    for ici, (ioff, icw) in enumerate(IC):
        with nc.Block() as block:
            @block.gpsimd
            def _(gp):
                for k in range(NT):
                    s_w.inc(gp.dma_start(out=wgr[:, k * 512:k * 512 + icw],
                                         in_=wgq[k * 128:(k + 1) * 128,
                                                 ioff:ioff + icw]), 16)
                    s_w.inc(gp.dma_start(out=wur[:, k * 512:k * 512 + icw],
                                         in_=wuq[k * 128:(k + 1) * 128,
                                                 ioff:ioff + icw]), 16)
        for tc in range(16):
            with nc.Block() as block:
                @block.tensor
                def _(t):
                    t.wait_ge(s_w.h, s_w.v)
                    for k in range(NT):
                        t.matmul(psb0[:, 0:icw],
                                 xqfm[k][:, tc * 128:(tc + 1) * 128],
                                 wgr[:, k * 512:k * 512 + icw],
                                 start=(k == 0), stop=(k == NT - 1))
                    for k in range(NT):
                        t.matmul(psb1[:, 0:icw],
                                 xqfm[k][:, tc * 128:(tc + 1) * 128],
                                 wur[:, k * 512:k * 512 + icw],
                                 start=(k == 0), stop=(k == NT - 1))
            with nc.Block() as block:
                @block.scalar
                def _(s):
                    s.activation(tmp[:, 0:icw], psb0[:, 0:icw], AF.Silu)
            with nc.Block() as block:
                @block.vector
                def _(v):
                    v.tensor_tensor(q1s[:, 0:icw], psb1[:, 0:icw],
                                    tmp[:, 0:icw], ALU.mult)
            emit_quant_chain(p, 128, icw, q1s[:, 0:icw], axn, q1s, q3s,
                             scl, rscl, htm[tc][:, ioff:ioff + icw])

    with nc.Block() as block:
        @block.gpsimd
        def _(gp):
            for ki in range(22):
                s_w.inc(gp.dma_start(out=wdt[ki][:, :],
                                     in_=wdq[ki * 128:(ki + 1) * 128, :]), 16)
    for ki in range(22):
        with nc.Block() as block:
            @block.tensor
            def _(t):
                for tc in range(16):
                    t.transpose(psbq[:, tc * 128:(tc + 1) * 128],
                                htm[tc][:, ki * 128:(ki + 1) * 128], idb[:, :])
        with nc.Block() as block:
            @block.vector
            def _(v):
                for tc in range(16):
                    v.tensor_copy(htm[tc][:, ki * 128:(ki + 1) * 128],
                                  psbq[:, tc * 128:(tc + 1) * 128])
    st = p.sem()
    for tch in range(4):
        with nc.Block() as block:
            @block.tensor
            def _(t):
                t.wait_ge(s_w.h, s_w.v)
                for ki in range(22):
                    for m in range(NT):
                        ps = psb0 if m < 4 else psb1
                        rhs = pool[:, 0:16 * 2816].rearrange(
                            "p (c w) -> p c w", w=2816)[
                            :, tch * 4:(tch + 1) * 4, ki * 128:(ki + 1) * 128]
                        t.matmul(ps[:, (m % 4) * 512:(m % 4) * 512 + 512],
                                 wdt[ki][:, m * 128:(m + 1) * 128],
                                 rhs, start=(ki == 0), stop=(ki == 21))
        for m in range(NT):
            b = m % 2
            with nc.Block() as block:
                @block.sync
                def _(sync):
                    s_x.inc(sync.dma_start(
                        out=xf[b][:, 0:512],
                        in_=xT[m * 128:(m + 1) * 128,
                               tch * 512:(tch + 1) * 512]), 16)
                @block.vector
                def _(v):
                    s_x.wait(v)
                    ps = psb0 if m < 4 else psb1
                    v.tensor_tensor(xf[b][:, 0:512], xf[b][:, 0:512],
                                    ps[:, (m % 4) * 512:(m % 4) * 512 + 512],
                                    ALU.add)
            with nc.Block() as block:
                @block.sync
                def _(sync):
                    st.inc(sync.dma_start(
                        out=oT[m * 128:(m + 1) * 128,
                               tch * 512:(tch + 1) * 512],
                        in_=xf[b][:, 0:512]), 16)
    with nc.Block() as block:
        @block.sync
        def _(sync):
            st.wait(sync)
    return p



HM_SHAPES = dict(xT=[D, T], Wq=[D, D], Wk=[D, D], Wv=[D, D], Wo=[D, D],
                 bo_c=[128, 8], i64_c=[64, D], id_c=[128, 128])


def build_headmix():
    p = Prog()
    io = {n: p.din(n, s) for n, s in HM_SHAPES.items()}
    io["oT"] = p.dout("oT", [D, T])
    emit_headmix(p, io)
    return p


def emit_headmix(p, io):
    nc = p.nc
    xT = io["xT"]; Wq = io["Wq"]; Wk = io["Wk"]; Wv = io["Wv"]; Wo = io["Wo"]
    bo_c = io["bo_c"]; i64_c = io["i64_c"]; id_c = io["id_c"]; oT = io["oT"]
    vs3 = p.dram("vs3", [D, T], BF16)
    T1b = p.dram("T1b", [D, T], F32)
    A2 = p.dram("A2", [1, 256], BF16)

    pool = p.sb("pool", [128, 4 * T], F32)
    poolb = pool[:, :].bitcast(BF16)
    xhat = [poolb[:, g * T:(g + 1) * T] for g in range(NT)]
    xres = [pool[:, 0:T], pool[:, T:2 * T]]
    qtm = [p.sb("qtm%d" % u, [128, D], BF16) for u in range(16)]
    ktm = [p.sb("ktm%d" % u, [128, D], BF16) for u in range(16)]
    vx = [p.sb("vx%d" % g, [128, T], BF16) for g in range(NT)]
    wbuf = [p.sb("wb%d" % k, [128, D], BF16) for k in range(NT)]
    tmp = p.sb("tmp", [128, T], F32)
    rows = p.sb("rows", [1, 2 * T], F32)
    ens = p.sb("ens", [64, D], F32)
    i64s = p.sb("i64s", [64, D], F32)
    mst = p.sb("mst", [16, T], F32)
    vb = p.sb("vb", [16, T], BF16)
    ef = p.sb("ef", [1, 2 * 256], F32)
    e16 = p.sb("e16", [1, 64], F32)
    Ab = p.sb("Ab", [1, 256], BF16)
    As = p.sb("As", [16, 16], BF16)
    ones_row = p.sb("ones_row", [1, 128], F32)
    ones_col = p.sb("ones_col", [128, 1], F32)
    bo_s = p.sb("bo_s", [128, 8], F32)

    p.add_consts([EPS, 0.125])
    psb0 = p.ps("psb0", [128, T])
    psb1 = p.ps("psb1", [128, T])

    lds = p.sem()
    emit_loads(p, lds, plain=[(i64s[:, :], i64_c), (bo_s[:, :], bo_c)])
    with nc.Block() as block:
        @block.gpsimd
        def _(gp):
            gp.memset(ones_row[:, :], 1.0)
            gp.memset(ones_col[:, :], 1.0)
            p.emit_const_init(gp)
        @block.vector
        def _(v):
            lds.wait(v)

    s_x = p.sem()
    for g in range(NT):
        b = g % 2
        with nc.Block() as block:
            @block.sync
            def _(sync):
                s_x.inc(sync.dma_start(out=xres[b][:, :],
                                       in_=xT[g * 128:(g + 1) * 128, :]), 16)
            @block.scalar
            def _(s):
                s_x.wait(s)
                s.activation(tmp[:, :], xres[b][:, :], AF.Square)
        with nc.Block() as block:
            @block.tensor
            def _(t):
                for ch in range(NCH):
                    sl = slice(ch * 512, (ch + 1) * 512)
                    t.matmul(psb0[0:1, sl], ones_col[:, :], xres[b][:, sl],
                             start=(g == 0), stop=(g == NT - 1))
                    t.matmul(psb1[0:1, sl], ones_col[:, :], tmp[:, sl],
                             start=(g == 0), stop=(g == NT - 1))
    mrow = rows[0:1, 0:T]
    vrow = rows[0:1, T:2 * T]
    with nc.Block() as block:
        @block.vector
        def _(v):
            v.tensor_scalar_mul(mrow, psb0[0:1, :], 1.0 / D)
            v.tensor_scalar_mul(vrow, psb1[0:1, :], 1.0 / D)
            v.tensor_tensor(tmp[0:1, 0:T], mrow, mrow, ALU.mult)
            v.tensor_sub(vrow, vrow, tmp[0:1, 0:T])
    with nc.Block() as block:
        @block.scalar
        def _(s):
            s.activation(vrow, vrow, AF.Sqrt, bias=EPS)
    with nc.Block() as block:
        @block.vector
        def _(v):
            v.reciprocal(vrow, vrow)
            v.scalar_tensor_tensor(mrow, mrow, -1.0, vrow, ALU.mult, ALU.mult)
    with nc.Block() as block:
        @block.tensor
        def _(t):
            for ch in range(NCH):
                sl = slice(ch * 512, (ch + 1) * 512)
                t.matmul(psb0[:, sl], ones_row[:, :], vrow[:, sl],
                         start=True, stop=True)
                t.matmul(psb1[:, sl], ones_row[:, :], mrow[:, sl],
                         start=True, stop=True)
    for g in range(NT):
        b = g % 2
        with nc.Block() as block:
            @block.sync
            def _(sync):
                s_x.inc(sync.dma_start(out=tmp[:, :],
                                       in_=xT[g * 128:(g + 1) * 128, :]), 16)
            @block.vector
            def _(v):
                s_x.wait(v)
                v.tensor_tensor(tmp[:, :], tmp[:, :], psb0[:, :], ALU.mult)
                v.scalar_tensor_tensor(xhat[g][:, :], tmp[:, :], 1.0,
                                       psb1[:, :], ALU.mult, ALU.add)

    s_w = p.sem()
    st = p.sem()
    for which, Wd in ((0, Wq), (1, Wk), (2, Wv)):
        with nc.Block() as block:
            @block.gpsimd
            def _(gp):
                for k in range(NT):
                    s_w.inc(gp.dma_start(out=wbuf[k][:, :],
                                         in_=Wd[k * 128:(k + 1) * 128, :]), 16)
            @block.tensor
            def _(t):
                s_w.wait(t)
        if which < 2:
            dst = qtm if which == 0 else ktm
            for u in range(16):
                with nc.Block() as block:
                    @block.tensor
                    def _(t):
                        for oc in range(2):
                            sl = slice(oc * 512, (oc + 1) * 512)
                            for k in range(NT):
                                t.matmul(psb0[:, sl],
                                         xhat[k][:, u * 128:(u + 1) * 128],
                                         wbuf[k][:, sl], start=(k == 0),
                                         stop=(k == NT - 1))
                with nc.Block() as block:
                    @block.vector
                    def _(v):
                        v.tensor_copy(dst[u][:, :], psb0[:, 0:D])
        else:
            for m in range(NT):
                with nc.Block() as block:
                    @block.tensor
                    def _(t):
                        for ch in range(NCH):
                            sl = slice(ch * 512, (ch + 1) * 512)
                            for k in range(NT):
                                t.matmul(psb0[:, sl],
                                         wbuf[k][:, m * 128:(m + 1) * 128],
                                         xhat[k][:, sl], start=(k == 0),
                                         stop=(k == NT - 1))
                with nc.Block() as block:
                    @block.vector
                    def _(v):
                        v.tensor_copy(vx[m][:, :], psb0[:, :])
                with nc.Block() as block:
                    @block.sync
                    def _(sync):
                        st.inc(sync.dma_start(out=vs3[m * 128:(m + 1) * 128, :],
                                              in_=vx[m][:, :]), 16)

    for a in range(16):
        with nc.Block() as block:
            @block.tensor
            def _(t):
                for oc in range(2):
                    sl = slice(oc * 512, (oc + 1) * 512)
                    for u in range(16):
                        t.matmul(psb0[0:64, sl],
                                 qtm[u][:, 64 * a:64 * a + 64],
                                 ktm[u][:, sl], start=(u == 0), stop=(u == 15))
        with nc.Block() as block:
            @block.vector
            def _(v):
                v.tensor_copy(ens[:, :], psb0[0:64, 0:D])
        with nc.Block() as block:
            @block.vector
            def _(v):
                v.tensor_tensor(ens[:, :], ens[:, :], i64s[:, :], ALU.mult)
        with nc.Block() as block:
            @block.tensor
            def _(t):
                for oc in range(2):
                    sl = slice(oc * 512, (oc + 1) * 512)
                    t.matmul(psb1[0:1, sl], ones_col[0:64, :], ens[:, sl],
                             start=True, stop=True)
        with nc.Block() as block:
            @block.vector
            def _(v):
                v.reduce_sum(ef[0:1, 16 * a:16 * a + 16],
                             psb1[0:1, 0:D].rearrange("p (b u) -> p b u", u=64),
                             axis=AX.X)
    with nc.Block() as block:
        @block.vector
        def _(v):
            v.reduce_max(e16[0:1, 0:16],
                         ef[0:1, 0:256].rearrange("p (a b) -> p a b", b=16),
                         axis=AX.X)
    with nc.Block() as block:
        @block.vector
        def _(v):
            v.tensor_sub(ef[0:1, 256:512].rearrange("p (a b) -> p a b", b=16),
                         ef[0:1, 0:256].rearrange("p (a b) -> p a b", b=16),
                         e16[0:1, 0:16].unsqueeze(2).broadcast_to([1, 16, 16]))
    with nc.Block() as block:
        @block.scalar
        def _(s):
            s.activation(ef[0:1, 256:512], ef[0:1, 256:512], AF.Exp, scale=0.125)
    with nc.Block() as block:
        @block.vector
        def _(v):
            v.reduce_sum(e16[0:1, 16:32],
                         ef[0:1, 256:512].rearrange("p (a b) -> p a b", b=16),
                         axis=AX.X)
    with nc.Block() as block:
        @block.vector
        def _(v):
            v.reciprocal(e16[0:1, 32:48], e16[0:1, 16:32])
    with nc.Block() as block:
        @block.vector
        def _(v):
            v.tensor_tensor(Ab[0:1, 0:256].rearrange("p (a b) -> p a b", b=16),
                            ef[0:1, 256:512].rearrange("p (a b) -> p a b", b=16),
                            e16[0:1, 32:48].unsqueeze(2).broadcast_to([1, 16, 16]),
                            ALU.mult)
    with nc.Block() as block:
        @block.sync
        def _(sync):
            st.inc(sync.dma_start(out=A2[:, :], in_=Ab[0:1, :]), 16)
    with nc.Block() as block:
        @block.gpsimd
        def _(gp):
            st.inc(gp.dma_start(out=As[:, :],
                                in_=bass.AP(A2.tensor, 0, [[1, 16], [16, 16]]),
                                allow_slow_non_contiguous=True), 16)
        @block.tensor
        def _(t):
            st.wait(t)

    s_v = p.sem()
    for dd in range(64):
        with nc.Block() as block:
            @block.sync
            def _(sync):
                s_v.inc(sync.dma_start(
                    out=vb[:, :],
                    in_=bass.AP(vs3.tensor, dd * T,
                                [[64 * T, 16], [1, T]])), 16)
            @block.tensor
            def _(t):
                s_v.wait(t)
                for ch in range(NCH):
                    sl = slice(ch * 512, (ch + 1) * 512)
                    t.matmul(psb1[0:16, sl], As[:, :], vb[:, sl],
                             start=True, stop=True)
        with nc.Block() as block:
            @block.scalar
            def _(s):
                s.copy(mst[0:16, 0:T].rearrange("p (u w) -> p w u", u=16),
                       psb1[0:16, 0:T])
        with nc.Block() as block:
            @block.sync
            def _(sync):
                st.inc(sync.dma_start(
                    out=bass.AP(T1b.tensor, dd * T, [[64 * T, 16], [1, T]]),
                    in_=mst[0:16, :]), 16)
    with nc.Block() as block:
        @block.sync
        def _(sync):
            st.wait(sync)
    for jt in range(NT):
        with nc.Block() as block:
            @block.gpsimd
            def _(gp):
                for ui in range(2):
                    u = jt * 2 + ui
                    s_v.inc(gp.dma_start(
                        out=vx[jt][64 * ui:64 * ui + 64, :],
                        in_=bass.AP(T1b.tensor, u * 128,
                                    [[T, 64], [64 * T, 16], [1, 128]])), 16)
    with nc.Block() as block:
        @block.gpsimd
        def _(gp):
            for k in range(NT):
                s_w.inc(gp.dma_start(out=wbuf[k][:, :],
                                     in_=Wo[k * 128:(k + 1) * 128, :]), 16)
        @block.tensor
        def _(t):
            s_w.wait(t)
            s_v.wait(t)
    for m in range(NT):
        b = m % 2
        with nc.Block() as block:
            @block.sync
            def _(sync):
                s_x.inc(sync.dma_start(out=tmp[:, :],
                                       in_=xT[m * 128:(m + 1) * 128, :]), 16)
            @block.tensor
            def _(t):
                for ch in range(NCH):
                    sl = slice(ch * 512, (ch + 1) * 512)
                    for k in range(NT):
                        t.matmul(psb0[:, sl], wbuf[k][:, m * 128:(m + 1) * 128],
                                 vx[k][:, sl], start=(k == 0), stop=(k == NT - 1))
        with nc.Block() as block:
            @block.vector
            def _(v):
                s_x.wait(v)
                v.scalar_tensor_tensor(tmp[:, :], psb0[:, :], bo_s[:, m:m + 1],
                                       tmp[:, :], ALU.add, ALU.add)
        with nc.Block() as block:
            @block.sync
            def _(sync):
                st.inc(sync.dma_start(out=oT[m * 128:(m + 1) * 128, :],
                                      in_=tmp[:, :]), 16)
    with nc.Block() as block:
        @block.sync
        def _(sync):
            st.wait(sync)
    return p



def build_stream_ln():
    p = Prog()
    io = dict(xT=p.din("xT", [D, T]), oT=p.dout("oT", [D, T]))
    emit_stream_ln(p, io)
    return p


def emit_stream_ln(p, io):
    nc = p.nc
    xT = io["xT"]; oT = io["oT"]
    xres = [p.sb("xr%d" % b, [128, T], F32) for b in range(2)]
    tmp = p.sb("tmp", [128, T], F32)
    rows = p.sb("rows", [1, 2 * T], F32)
    ones_row = p.sb("ones_row", [1, 128], F32)
    ones_col = p.sb("ones_col", [128, 1], F32)
    p.add_consts([EPS])
    psb0 = p.ps("psb0", [128, T])
    psb1 = p.ps("psb1", [128, T])
    with nc.Block() as block:
        @block.gpsimd
        def _(gp):
            gp.memset(ones_row[:, :], 1.0)
            gp.memset(ones_col[:, :], 1.0)
            p.emit_const_init(gp)
    s_x = p.sem()
    st = p.sem()
    for g in range(NT):
        b = g % 2
        with nc.Block() as block:
            @block.sync
            def _(sync):
                s_x.inc(sync.dma_start(out=xres[b][:, :],
                                       in_=xT[g * 128:(g + 1) * 128, :]), 16)
            @block.scalar
            def _(s):
                s_x.wait(s)
                s.activation(tmp[:, :], xres[b][:, :], AF.Square)
        with nc.Block() as block:
            @block.tensor
            def _(t):
                for ch in range(NCH):
                    sl = slice(ch * 512, (ch + 1) * 512)
                    t.matmul(psb0[0:1, sl], ones_col[:, :], xres[b][:, sl],
                             start=(g == 0), stop=(g == NT - 1))
                    t.matmul(psb1[0:1, sl], ones_col[:, :], tmp[:, sl],
                             start=(g == 0), stop=(g == NT - 1))
    mrow = rows[0:1, 0:T]
    vrow = rows[0:1, T:2 * T]
    with nc.Block() as block:
        @block.vector
        def _(v):
            v.tensor_scalar_mul(mrow, psb0[0:1, :], 1.0 / D)
            v.tensor_scalar_mul(vrow, psb1[0:1, :], 1.0 / D)
            v.tensor_tensor(tmp[0:1, 0:T], mrow, mrow, ALU.mult)
            v.tensor_sub(vrow, vrow, tmp[0:1, 0:T])
    with nc.Block() as block:
        @block.scalar
        def _(s):
            s.activation(vrow, vrow, AF.Sqrt, bias=EPS)
    with nc.Block() as block:
        @block.vector
        def _(v):
            v.reciprocal(vrow, vrow)
            v.scalar_tensor_tensor(mrow, mrow, -1.0, vrow, ALU.mult, ALU.mult)
    with nc.Block() as block:
        @block.tensor
        def _(t):
            for ch in range(NCH):
                sl = slice(ch * 512, (ch + 1) * 512)
                t.matmul(psb0[:, sl], ones_row[:, :], vrow[:, sl],
                         start=True, stop=True)
                t.matmul(psb1[:, sl], ones_row[:, :], mrow[:, sl],
                         start=True, stop=True)
    for g in range(NT):
        b = g % 2
        with nc.Block() as block:
            @block.sync
            def _(sync):
                s_x.inc(sync.dma_start(out=xres[b][:, :],
                                       in_=xT[g * 128:(g + 1) * 128, :]), 16)
            @block.vector
            def _(v):
                s_x.wait(v)
                v.tensor_tensor(tmp[:, :], xres[b][:, :], psb0[:, :], ALU.mult)
                v.scalar_tensor_tensor(tmp[:, :], tmp[:, :], 1.0,
                                       psb1[:, :], ALU.mult, ALU.add)
        with nc.Block() as block:
            @block.sync
            def _(sync):
                st.inc(sync.dma_start(out=oT[g * 128:(g + 1) * 128, :],
                                      in_=tmp[:, :]), 16)
    with nc.Block() as block:
        @block.sync
        def _(sync):
            st.wait(sync)
    return p


def build_stream_add():
    p = Prog()
    io = dict(xT=p.din("xT", [D, T]), yT=p.din("yT", [D, T]),
              oT=p.dout("oT", [D, T]))
    emit_stream_add(p, io)
    return p


def emit_stream_add(p, io):
    nc = p.nc
    xT = io["xT"]; yT = io["yT"]; oT = io["oT"]
    a = p.sb("a", [128, T], F32)
    b_ = p.sb("b", [128, T], F32)
    s_x = p.sem()
    st = p.sem()
    for g in range(NT):
        with nc.Block() as block:
            @block.sync
            def _(sync):
                s_x.inc(sync.dma_start(out=a[:, :],
                                       in_=xT[g * 128:(g + 1) * 128, :]), 16)
                s_x.inc(sync.dma_start(out=b_[:, :],
                                       in_=yT[g * 128:(g + 1) * 128, :]), 16)
            @block.vector
            def _(v):
                s_x.wait(v)
                v.tensor_add(a[:, :], a[:, :], b_[:, :])
        with nc.Block() as block:
            @block.sync
            def _(sync):
                st.inc(sync.dma_start(out=oT[g * 128:(g + 1) * 128, :],
                                      in_=a[:, :]), 16)
    with nc.Block() as block:
        @block.sync
        def _(sync):
            st.wait(sync)
    return p



_PROGS = {}


def _prog(name, builder):
    if name not in _PROGS:
        _PROGS[name] = builder()
    return _PROGS[name]


def _run(p, in_maps):
    res = run_bass_kernel_spmd(p.nc, in_maps, core_ids=list(range(NCORES)))
    return res.results


def kernel(**inputs):
    ins = inputs
    x = np.asarray(ins["L0_attn_a_up_x"], dtype=np.float32)
    xT = [np.ascontiguousarray(x[c].T) for c in range(NCORES)]
    idc = np.eye(128, dtype=np.float32)
    bd = np.zeros((128, 128), np.float32)
    eg = np.zeros((16, 8 * 128), np.float32)
    for g in range(8):
        for hh in range(16):
            if hh // 2 == g:
                bd[64 * (hh % 2):64 * (hh % 2) + 64, g * 16 + hh] = 1.0
        for pp in range(128):
            eg[2 * g + pp // 64, g * 128 + pp] = 1.0

    r = _run(_prog("mamba", build_mamba),
             [mamba_inputs(xT[c], ins) for c in range(NCORES)])
    xT = [r[c]["oT"] for c in range(NCORES)]

    iconsts = inner_consts()
    p_ln = _prog("sln", build_stream_ln)
    p_in = _prog("inner", build_inner)
    p_add = _prog("sadd", build_stream_add)
    for _call in range(2):
        r = _run(p_ln, [{"xT": xT[c]} for c in range(NCORES)])
        yT = [r[c]["oT"] for c in range(NCORES)]
        for li in range(6):
            r = _run(p_in, [inner_inputs(yT[c], li, ins, iconsts)
                            for c in range(NCORES)])
            yT = [r[c]["oT"] for c in range(NCORES)]
        r = _run(p_add, [{"xT": xT[c], "yT": yT[c]} for c in range(NCORES)])
        xT = [r[c]["oT"] for c in range(NCORES)]
        if _call == 0:
            r = _run(_prog("attnb", build_attn_b),
                     [attn_b_inputs(xT[c], ins["L0_attn_b_Wq"],
                                    ins["L0_attn_b_Wk"], ins["L0_attn_b_Wv"],
                                    ins["L0_attn_b_Wo"], ins["L0_attn_b_bo"])
                      for c in range(NCORES)])
            xT = [r[c]["oT"] for c in range(NCORES)]

    beta = np.asarray(ins["L1_attn_a_beta"], np.float32)
    bidx = (2 * np.arange(8)[None, :] + (np.arange(128) // 64)[:, None])
    r = _run(_prog("linqkv", build_lin_qkv),
             [{"xT": xT[c],
               "WqT": _f32c(np.asarray(ins["L1_attn_a_Wq"]).T),
               "WkT": _f32c(np.asarray(ins["L1_attn_a_Wk"]).T),
               "WvT": _f32c(np.asarray(ins["L1_attn_a_Wv"]).T),
               "bq_c": _f32c(np.asarray(ins["L1_attn_a_bq"]).reshape(8, 128).T),
               "bk_c": _f32c(np.asarray(ins["L1_attn_a_bk"]).reshape(8, 128).T),
               "bv_c": _f32c(np.asarray(ins["L1_attn_a_bv"]).reshape(8, 128).T),
               "beta_e": _f32c(beta[bidx])} for c in range(NCORES)])
    kall = np.concatenate([r[c]["ko"] for c in range(NCORES)], axis=0)
    vall = np.concatenate([r[c]["vo"] for c in range(NCORES)], axis=0)
    qos = [r[c]["qo"] for c in range(NCORES)]
    r = _run(_prog("linmain", build_lin_main),
             [{"xT": xT[c], "qtb": qos[c], "kall": kall, "vall": vall,
               "WoT": _f32c(np.asarray(ins["L1_attn_a_Wo"]).T),
               "bo_c": _f32c(np.asarray(ins["L1_attn_a_bo"]).reshape(8, 128).T),
               "mask_c": _f32c(np.tile((np.arange(8) <= c).astype(np.float32),
                                       (16, 1))),
               "bd_c": bd, "eg_c": eg} for c in range(NCORES)])
    xT = [r[c]["oT"] for c in range(NCORES)]

    rq = _run(_prog("wquant", build_wquant),
              [{"wg": _f32c(ins["L1_ffn_w_gate"]), "wu": _f32c(ins["L1_ffn_w_up"]),
                "wd": _f32c(ins["L1_ffn_w_down"]), "id_c": idc}
               for c in range(NCORES)])
    p_f4 = _prog("fp4", build_fp4ffn)

    def fp4_call(xT_l):
        rr = _run(p_f4, [{"xT": xT_l[c], "wgq": rq[c]["wgo"],
                          "wuq": rq[c]["wuo"], "wdq": rq[c]["wdo"],
                          "id_c": idc} for c in range(NCORES)])
        return [rr[c]["oT"] for c in range(NCORES)]

    xT = fp4_call(xT)

    i64 = np.zeros((64, 1024), np.float32)
    for dd in range(64):
        i64[dd, dd::64] = 1.0
    r = _run(_prog("hm", build_headmix),
             [{"xT": xT[c], "Wq": _f32c(ins["L1_attn_b_Wq"]),
               "Wk": _f32c(ins["L1_attn_b_Wk"]), "Wv": _f32c(ins["L1_attn_b_Wv"]),
               "Wo": _f32c(ins["L1_attn_b_Wo"]),
               "bo_c": _f32c(np.asarray(ins["L1_attn_b_bo"]).reshape(8, 128).T),
               "i64_c": i64, "id_c": idc} for c in range(NCORES)])
    xT = [r[c]["oT"] for c in range(NCORES)]

    xT = fp4_call(xT)

    out = np.stack([np.asarray(xT[c], np.float32).T for c in range(NCORES)], axis=0)
    return np.ascontiguousarray(out)



def build_mega():
    p = Prog()
    nc = p.nc
    LN8 = float(np.log(8.0))
    p.add_consts([EPS, -1.0, 0.125, 8.0, LN8])

    io_m = {n: p.din("mb_" + n, MAMBA_SHAPES[n]) for n in MAMBA_INS if n != "xT"}
    X = p.din("X", [D, T])
    WqkvS = p.din("WqkvS", [6 * D, 3 * D])
    WopS = p.din("WopS", [6 * D, D])
    qsS = p.din("qsS", [128, 12])
    lngS = p.din("lngS", [128, 96])
    W1S = p.din("W1S", [6 * D, 5460])
    b1aS = p.din("b1aS", [128, 132])
    b1gS = p.din("b1gS", [128, 132])
    W2S = p.din("W2S", [6 * 2730, D])
    b2S = p.din("b2S", [128, 48])
    CQ_c = p.din("CQ_c", [128, T]); SQ_c = p.din("SQ_c", [128, T])
    CK_c = p.din("CK_c", [128, T]); SK_c = p.din("SK_c", [128, T])
    M2_c = p.din("M2_c", [128, 384])
    bd_c = p.din("bd_c", [128, 128])
    eg_c = p.din("eg_c", [16, 8 * 128])
    e2_c = p.din("e2_c", [128, 128])
    id_c = p.din("id_c", [128, 128])
    i64_c = p.din("i64_c", [64, D])
    ab = {n: p.din("ab_" + n, s) for n, s in ATTNB_SHAPES.items()
          if n not in ("xT", "bd_c", "eg_c", "e2_c")}
    ab.update(dict(bd_c=bd_c, eg_c=eg_c, e2_c=e2_c))
    lq = {n: p.din("lq_" + n, LINQKV_SHAPES[n]) for n in LINQKV_INS if n != "xT"}
    WoTl = p.din("lm_WoT", [D, D])
    bo_l = p.din("lm_bo_c", [128, 8])
    mask_l = p.din("lm_mask_c", [16, 8])
    wg_i = p.din("f4_wg", [2816, D])
    wu_i = p.din("f4_wu", [2816, D])
    wd_i = p.din("f4_wd", [D, 2816])
    hm = {n: p.din("hm_" + n, s) for n, s in HM_SHAPES.items()
          if n not in ("xT", "i64_c", "id_c")}
    hm.update(dict(i64_c=i64_c, id_c=id_c))
    OUT = p.dout("OUT", [D, T])

    xa = p.dram("xa", [D, T])
    xb = p.dram("xb", [D, T])
    ya = p.dram("ya", [D, T])
    yb = p.dram("yb", [D, T])
    qloc = p.dram("qloc", [D, T], BF16)
    kvp = p.dram("kvp", [2 * D, T], BF16)
    kvall = p.dram("kvall", [16 * D, T], BF16)
    wgq = p.dram("wgq", [D, 2816], BF16)
    wuq = p.dram("wuq", [D, 2816], BF16)
    wdq = p.dram("wdq", [2816, D], BF16)

    p.begin_phase("mb")
    emit_mamba(p, dict(io_m, xT=X, oT=xa))
    p.end_phase()

    cur, nxt = xa, xb
    for call in range(2):
        yc, yn = ya, yb
        p.begin_phase("ln%d" % call)
        emit_stream_ln(p, dict(xT=cur, oT=yc))
        p.end_phase()
        for li in range(6):
            p.begin_phase("in%d_%d" % (call, li))
            emit_inner(p, dict(
                yT=yc, oT=yn,
                Wqkv=WqkvS[li * D:(li + 1) * D, :],
                Wop=WopS[li * D:(li + 1) * D, :],
                qs_c=qsS[:, li * 2:(li + 1) * 2],
                lng_c=lngS[:, li * 16:(li + 1) * 16],
                W1=W1S[li * D:(li + 1) * D, :],
                b1a=b1aS[:, li * 22:(li + 1) * 22],
                b1g=b1gS[:, li * 22:(li + 1) * 22],
                W2=W2S[li * 2730:(li + 1) * 2730, :],
                b2_c=b2S[:, li * 8:(li + 1) * 8],
                CQ_c=CQ_c, SQ_c=SQ_c, CK_c=CK_c, SK_c=SK_c, M2_c=M2_c,
                bd_c=bd_c, eg_c=eg_c, id_c=id_c))
            p.end_phase()
            yc, yn = yn, yc
        p.begin_phase("ad%d" % call)
        emit_stream_add(p, dict(xT=cur, yT=yc, oT=nxt))
        p.end_phase()
        cur, nxt = nxt, cur
        if call == 0:
            p.begin_phase("ab")
            emit_attn_b(p, dict(ab, xT=cur, oT=nxt))
            p.end_phase()
            cur, nxt = nxt, cur

    p.begin_phase("lq")
    emit_lin_qkv(p, dict(lq, xT=cur, qo=qloc, ko=kvp[0:D, :],
                         vo=kvp[D:2 * D, :]))
    p.end_phase()
    p.begin_phase("cc")
    s_cc = p.sem()
    with nc.Block() as block:
        @block.gpsimd
        def _(gp):
            s_cc.inc(gp.collective_compute(
                "AllGather", ALU.bypass,
                replica_groups=[[0, 1, 2, 3, 4, 5, 6, 7]],
                ins=[kvp[:, :].opt()], outs=[kvall[:, :].opt()]), 1)
        @block.sync
        def _(sync):
            s_cc.wait(sync)
    p.end_phase()
    p.begin_phase("lm")
    emit_lin_main(p, dict(xT=cur, qtb=qloc, kall=kvall,
                          vall=kvall[D:16 * D, :], WoT=WoTl, bo_c=bo_l,
                          mask_c=mask_l, bd_c=bd_c, eg_c=eg_c, oT=nxt),
                  kvstride=2 * D)
    p.end_phase()
    cur, nxt = nxt, cur

    p.begin_phase("wq")
    emit_wquant(p, dict(wg=wg_i, wu=wu_i, wd=wd_i, id_c=id_c,
                        wgo=wgq, wuo=wuq, wdo=wdq))
    p.end_phase()
    p.begin_phase("f4a")
    emit_fp4ffn(p, dict(xT=cur, wgq=wgq, wuq=wuq, wdq=wdq, id_c=id_c, oT=nxt))
    p.end_phase()
    cur, nxt = nxt, cur
    p.begin_phase("hm")
    emit_headmix(p, dict(hm, xT=cur, oT=nxt))
    p.end_phase()
    cur, nxt = nxt, cur
    p.begin_phase("f4b")
    emit_fp4ffn(p, dict(xT=cur, wgq=wgq, wuq=wuq, wdq=wdq, id_c=id_c, oT=OUT))
    p.end_phase()
    return p


def mega_inputs(xT_c, core, ins):
    idc = np.eye(128, dtype=np.float32)
    bd = np.zeros((128, 128), np.float32)
    eg = np.zeros((16, 8 * 128), np.float32)
    for g in range(8):
        for hh in range(16):
            if hh // 2 == g:
                bd[64 * (hh % 2):64 * (hh % 2) + 64, g * 16 + hh] = 1.0
        for pp in range(128):
            eg[2 * g + pp // 64, g * 128 + pp] = 1.0
    e2 = np.zeros((128, 128), np.float32)
    for pp in range(128):
        e2[pp % 64, pp] = 1.0
        e2[64 + pp % 64, pp] = 1.0
    i64 = np.zeros((64, 1024), np.float32)
    for dd in range(64):
        i64[dd, dd::64] = 1.0
    im = {}
    mi = mamba_inputs(xT_c, ins)
    for n in MAMBA_INS:
        if n != "xT":
            im["mb_" + n] = mi[n]
    im["X"] = xT_c
    colperm = np.concatenate([h * 64 + PERM64 for h in range(16)])
    Wq_l, Wo_l, qs_l, lng_l, W1_l, b1a_l, b1g_l, W2_l, b2_l = \
        [], [], [], [], [], [], [], [], []
    for li in range(6):
        ii = inner_inputs(xT_c, li, ins, {})
        Wq_l.append(ii["Wqkv"]); Wo_l.append(ii["Wop"]); qs_l.append(ii["qs_c"])
        lng_l.append(ii["lng_c"]); W1_l.append(ii["W1"]); b1a_l.append(ii["b1a"])
        b1g_l.append(ii["b1g"]); W2_l.append(ii["W2"]); b2_l.append(ii["b2_c"])
    im["WqkvS"] = _f32c(np.vstack(Wq_l))
    im["WopS"] = _f32c(np.vstack(Wo_l))
    im["qsS"] = _f32c(np.hstack(qs_l))
    im["lngS"] = _f32c(np.hstack(lng_l))
    im["W1S"] = _f32c(np.vstack(W1_l))
    im["b1aS"] = _f32c(np.hstack(b1a_l))
    im["b1gS"] = _f32c(np.hstack(b1g_l))
    im["W2S"] = _f32c(np.vstack(W2_l))
    im["b2S"] = _f32c(np.hstack(b2_l))
    ic = inner_consts()
    im["CQ_c"] = ic["CQ_c"]; im["SQ_c"] = ic["SQ_c"]
    im["CK_c"] = ic["CK_c"]; im["SK_c"] = ic["SK_c"]; im["M2_c"] = ic["M2_c"]
    im["bd_c"] = bd; im["eg_c"] = eg; im["e2_c"] = e2; im["id_c"] = idc
    im["i64_c"] = i64
    ai = attn_b_inputs(xT_c, ins["L0_attn_b_Wq"], ins["L0_attn_b_Wk"],
                       ins["L0_attn_b_Wv"], ins["L0_attn_b_Wo"],
                       ins["L0_attn_b_bo"])
    for n in ATTNB_SHAPES:
        if n not in ("xT", "bd_c", "eg_c", "e2_c"):
            im["ab_" + n] = ai[n]
    beta = np.asarray(ins["L1_attn_a_beta"], np.float32)
    bidx = (2 * np.arange(8)[None, :] + (np.arange(128) // 64)[:, None])
    im["lq_WqT"] = _f32c(np.asarray(ins["L1_attn_a_Wq"]).T)
    im["lq_WkT"] = _f32c(np.asarray(ins["L1_attn_a_Wk"]).T)
    im["lq_WvT"] = _f32c(np.asarray(ins["L1_attn_a_Wv"]).T)
    im["lq_bq_c"] = _f32c(np.asarray(ins["L1_attn_a_bq"]).reshape(8, 128).T)
    im["lq_bk_c"] = _f32c(np.asarray(ins["L1_attn_a_bk"]).reshape(8, 128).T)
    im["lq_bv_c"] = _f32c(np.asarray(ins["L1_attn_a_bv"]).reshape(8, 128).T)
    im["lq_beta_e"] = _f32c(beta[bidx])
    im["lm_WoT"] = _f32c(np.asarray(ins["L1_attn_a_Wo"]).T)
    im["lm_bo_c"] = _f32c(np.asarray(ins["L1_attn_a_bo"]).reshape(8, 128).T)
    im["lm_mask_c"] = _f32c(np.tile((np.arange(8) <= core).astype(np.float32),
                                    (16, 1)))
    im["f4_wg"] = _f32c(ins["L1_ffn_w_gate"])
    im["f4_wu"] = _f32c(ins["L1_ffn_w_up"])
    im["f4_wd"] = _f32c(ins["L1_ffn_w_down"])
    im["hm_Wq"] = _f32c(ins["L1_attn_b_Wq"])
    im["hm_Wk"] = _f32c(ins["L1_attn_b_Wk"])
    im["hm_Wv"] = _f32c(ins["L1_attn_b_Wv"])
    im["hm_Wo"] = _f32c(ins["L1_attn_b_Wo"])
    im["hm_bo_c"] = _f32c(np.asarray(ins["L1_attn_b_bo"]).reshape(8, 128).T)
    return im


def kernel(**inputs):
    x = np.asarray(inputs["L0_attn_a_up_x"], dtype=np.float32)
    xT = [np.ascontiguousarray(x[c].T) for c in range(NCORES)]
    p = _prog("mega", build_mega)
    in_maps = [mega_inputs(xT[c], c, inputs) for c in range(NCORES)]
    res = run_bass_kernel_spmd(p.nc, in_maps, core_ids=list(range(NCORES)))
    out = np.stack([np.asarray(res.results[c]["OUT"], np.float32).T
                    for c in range(NCORES)], axis=0)
    return np.ascontiguousarray(out)


# version 05
# speedup vs baseline: 1.1001x; 1.1001x over previous
import numpy as np
from contextlib import ExitStack

import concourse.bass as bass
import concourse.mybir as mybir
from concourse.bass_utils import run_bass_kernel_spmd

F32 = mybir.dt.float32
BF16 = mybir.dt.bfloat16
AF = mybir.ActivationFunctionType
ALU = mybir.AluOpType
AX = mybir.AxisListType

NCORES = 8
T = 2048
D = 1024
NT = 8
EPS = 1e-5
NCH = 4


def _f32c(a):
    return np.ascontiguousarray(a, dtype=np.float32)


class Sem:
    def __init__(self, st, nc, name):
        self.h = st.enter_context(nc.semaphore(name))
        self.v = 0

    def inc(self, inst, n=1):
        self.v += n
        inst.then_inc(self.h, n)
        return inst

    def wait(self, eng, v=None):
        eng.wait_ge(self.h, self.v if v is None else v)


class Prog:
    def __init__(self):
        self.nc = bass.Bass("TRN2", target_bir_lowering=False)
        self.st = ExitStack()
        self._i = 0
        self.prefix = ""
        self._phase_st = None
        self._sempool = []
        self._semptr = 0
        self._psmap = {}

    def begin_phase(self, tag):
        self.prefix = tag + "_"
        self._phase_st = ExitStack()
        self._semptr = 0

    def end_phase(self):
        self._phase_st.close()
        self._phase_st = None
        self.prefix = ""

    def sem(self):
        if self._phase_st is None:
            self._i += 1
            return Sem(self.st, self.nc, "s%d" % self._i)
        if self._semptr >= len(self._sempool):
            self._sempool.append(Sem(self.st, self.nc,
                                     "sp%d" % len(self._sempool)))
        s = self._sempool[self._semptr]
        self._semptr += 1
        return s

    def sb(self, name, shape, dtype):
        st = self._phase_st if self._phase_st is not None else self.st
        return st.enter_context(
            self.nc.sbuf_tensor(self.prefix + name, shape, dtype))

    def ps(self, name, shape, dtype=F32):
        if name not in self._psmap:
            self._psmap[name] = self.st.enter_context(
                self.nc.psum_tensor(name, shape, dtype))
        return self._psmap[name]

    def dram(self, name, shape, dtype=F32, kind=None):
        if kind is None:
            return self.nc.dram_tensor(self.prefix + name, shape, dtype).ap()
        return self.nc.dram_tensor(self.prefix + name, shape, dtype,
                                   kind=kind).ap()

    def add_consts(self, vals):
        if not hasattr(self, "cvals"):
            self.cvals = []
            self.ctile = self.st.enter_context(
                self.nc.sbuf_tensor("cpool", [128, 16], F32))
            self.cmap = {}
        for v in vals:
            v = float(v)
            if v in self.cmap:
                continue
            i = len(self.cvals)
            self.cvals.append(v)
            ap = self.ctile[:, i:i + 1]
            self.nc.const_aps.register(F32, v, ap)
            self.cmap[v] = ap

    def emit_const_init(self, gp):
        for i, v in enumerate(self.cvals):
            gp.memset(self.ctile[:, i:i + 1], float(v))

    def din(self, name, shape):
        return self.nc.dram_tensor(name, shape, F32, kind="ExternalInput").ap()

    def dout(self, name, shape, dtype=F32):
        return self.nc.dram_tensor(name, shape, dtype, kind="ExternalOutput").ap()



def emit_ln(p, xview, xhat, aux, out_dtype=BF16):
    nc = p.nc
    psA = aux["psb0"][0:1, :]
    psB = aux["psb1"][0:1, :]
    rows = aux["rows"]
    tmp = aux["tmp"]
    for g in range(NT):
        with nc.Block() as block:
            @block.scalar
            def _(s):
                s.activation(tmp[:, :], xview(g), AF.Square)
        with nc.Block() as block:
            @block.tensor
            def _(t):
                for ch in range(NCH):
                    sl = slice(ch * 512, (ch + 1) * 512)
                    t.matmul(psA[:, sl], aux["ones_col"][:, :], xview(g)[:, sl],
                             start=(g == 0), stop=(g == NT - 1))
                    t.matmul(psB[:, sl], aux["ones_col"][:, :], tmp[:, sl],
                             start=(g == 0), stop=(g == NT - 1))
    mrow = rows[0:1, 0:T]
    vrow = rows[0:1, T:2 * T]
    with nc.Block() as block:
        @block.vector
        def _(v):
            v.tensor_scalar_mul(mrow, psA[:, :], 1.0 / D)
            v.tensor_scalar_mul(vrow, psB[:, :], 1.0 / D)
            v.tensor_tensor(tmp[0:1, 0:T], mrow, mrow, ALU.mult)
            v.tensor_sub(vrow, vrow, tmp[0:1, 0:T])
    with nc.Block() as block:
        @block.scalar
        def _(s):
            s.activation(vrow, vrow, AF.Sqrt, bias=EPS)
    with nc.Block() as block:
        @block.vector
        def _(v):
            v.reciprocal(vrow, vrow)
            v.scalar_tensor_tensor(mrow, mrow, -1.0, vrow,
                                   ALU.mult, ALU.mult)
    with nc.Block() as block:
        @block.tensor
        def _(t):
            for ch in range(NCH):
                sl = slice(ch * 512, (ch + 1) * 512)
                t.matmul(aux["psb0"][:, sl], aux["ones_row"][:, :],
                         vrow[:, sl], start=True, stop=True)
                t.matmul(aux["psb1"][:, sl], aux["ones_row"][:, :],
                         mrow[:, sl], start=True, stop=True)
    with nc.Block() as block:
        @block.vector
        def _(v):
            for g in range(NT):
                v.tensor_tensor(tmp[:, :], xview(g), aux["psb0"][:, :], ALU.mult)
                v.scalar_tensor_tensor(xhat[g][:, :], tmp[:, :], 1.0,
                                       aux["psb1"][:, :], ALU.mult, ALU.add)


def emit_loads(p, sem, plain=(), cast=()):
    with p.nc.Block() as block:
        @block.sync
        def _(sync):
            for dst, src in plain:
                sem.inc(sync.dma_start(out=dst, in_=src), 16)
        @block.gpsimd
        def _(gp):
            for dst, src in cast:
                sem.inc(gp.dma_start(out=dst, in_=src), 16)



MAMBA_INS = ["xT", "W_up", "b_upc", "conv_w", "conv_b", "W_delta", "b_delta",
             "W_B", "b_B", "W_C", "b_C", "A_log", "D_skip", "W_down", "b_dc",
             "selb"]
MAMBA_SHAPES = dict(xT=[D, T], W_up=[D, 256], b_upc=[128, 2], conv_w=[128, 3],
                    conv_b=[128, 1], W_delta=[128, 128], b_delta=[128, 1],
                    W_B=[128, 64], b_B=[64, 1], W_C=[128, 64], b_C=[64, 1],
                    A_log=[128, 64], D_skip=[128, 1], W_down=[128, D],
                    b_dc=[128, 8], selb=[64, 64 * 128])


def build_mamba():
    p = Prog()
    io = {n: p.din(n, MAMBA_SHAPES[n]) for n in MAMBA_INS}
    io["oT"] = p.dout("oT", [D, T])
    emit_mamba(p, io)
    return p


def emit_mamba(p, io):
    nc = p.nc
    (xT, W_up, b_upc, conv_w, conv_b, W_delta, b_delta, W_B, b_B, W_C, b_C,
     A_log, D_skip, W_down, b_dc, selb) = [io[n] for n in MAMBA_INS]
    oT = io["oT"]

    x = [p.sb("x%d" % g, [128, T], F32) for g in range(NT)]
    pool = p.sb("pool", [128, 4 * T], F32)
    poolb = pool[:, :].bitcast(BF16)
    xhat = [poolb[:, g * T:(g + 1) * T] for g in range(NT)]
    tmp = p.sb("tmp", [128, T], F32)
    rows = p.sb("rows", [1, 2 * T], F32)
    ones_row = p.sb("ones_row", [1, 128], F32)
    ones_col = p.sb("ones_col", [128, 1], F32)
    wup_s = [p.sb("wup%d" % k, [128, 256], BF16) for k in range(NT)]
    wdel_s = p.sb("wdel", [128, 128], BF16)
    wB_s = p.sb("wB", [128, 64], BF16)
    wC_s = p.sb("wC", [128, 64], BF16)
    wdn_s = p.sb("wdn", [128, D], BF16)
    selb_s = p.sb("selb_s", [64, 64 * 128], BF16)
    bup_s = p.sb("bup", [128, 2], F32)
    cw_s = p.sb("cw", [128, 3], F32)
    cb_s = p.sb("cb", [128, 1], F32)
    bdel_s = p.sb("bdel", [128, 1], F32)
    bB_s = p.sb("bB", [64, 1], F32)
    bC_s = p.sb("bC", [64, 1], F32)
    alog_s = p.sb("alog", [128, 64], F32)
    dskip_s = p.sb("dskip", [128, 1], F32)
    bdc_s = p.sb("bdc", [128, 8], F32)

    LEFT = p.sb("LEFT", [128, T + 2], F32)
    RIGHT = p.sb("RIGHT", [128, T], F32)
    xs = p.sb("xs", [128, T], F32)
    xsb = p.sb("xsb", [128, T], BF16)
    delta = pool[:, 0:T]
    wdx = pool[:, T:2 * T]
    dAn = pool[:, 2 * T:3 * T]
    hn = pool[:, 3 * T:4 * T]
    dBx = tmp
    Bm = p.sb("Bm", [64, T], BF16)
    Cm = p.sb("Cm", [64, T], BF16)
    yacc = p.sb("yacc", [128, T], F32)
    A_sb = p.sb("A_sb", [128, 64], F32)
    gat = xsb

    p.add_consts([EPS, -1.0])
    psb0 = p.ps("psb0", [128, T])
    psb1 = p.ps("psb1", [128, T])
    aux = dict(psb0=psb0, psb1=psb1, rows=rows, tmp=tmp,
               ones_row=ones_row, ones_col=ones_col)

    lds = p.sem()
    emit_loads(
        p, lds,
        plain=[(x[g][:, :], xT[g * 128:(g + 1) * 128, :]) for g in range(NT)]
        + [(bup_s[:, :], b_upc), (cw_s[:, :], conv_w), (cb_s[:, :], conv_b),
           (bdel_s[:, :], b_delta), (bB_s[:, :], b_B), (bC_s[:, :], b_C),
           (alog_s[:, :], A_log), (dskip_s[:, :], D_skip), (bdc_s[:, :], b_dc)],
        cast=[(wup_s[k][:, :], W_up[k * 128:(k + 1) * 128, :]) for k in range(NT)]
        + [(wdel_s[:, :], W_delta), (wB_s[:, :], W_B), (wC_s[:, :], W_C),
           (wdn_s[:, :], W_down), (selb_s[:, :], selb)],
    )
    with nc.Block() as block:
        @block.gpsimd
        def _(gp):
            gp.memset(ones_row[:, :], 1.0)
            gp.memset(ones_col[:, :], 1.0)
            p.emit_const_init(gp)
            gp.memset(LEFT[:, 0:2], 0.0)
            gp.memset(yacc[:, :], 0.0)
        @block.vector
        def _(v):
            lds.wait(v)

    emit_ln(p, lambda g: x[g][:, :], xhat, aux)

    for m in range(2):
        with nc.Block() as block:
            @block.tensor
            def _(t):
                for ch in range(NCH):
                    sl = slice(ch * 512, (ch + 1) * 512)
                    for k in range(NT):
                        t.matmul(psb0[:, sl], wup_s[k][:, m * 128:(m + 1) * 128],
                                 xhat[k][:, sl], start=(k == 0), stop=(k == NT - 1))
        with nc.Block() as block:
            @block.vector
            def _(v):
                if m == 0:
                    v.tensor_scalar_add(LEFT[:, 2:2 + T], psb0[:, :], bup_s[:, 0:1])
                else:
                    v.tensor_scalar_add(RIGHT[:, :], psb0[:, :], bup_s[:, 1:2])
    with nc.Block() as block:
        @block.vector
        def _(v):
            v.tensor_scalar_mul(tmp[:, :], LEFT[:, 0:T], cw_s[:, 0:1])
            v.scalar_tensor_tensor(tmp[:, :], LEFT[:, 1:1 + T], cw_s[:, 1:2],
                                   tmp[:, :], ALU.mult, ALU.add)
            v.scalar_tensor_tensor(tmp[:, :], LEFT[:, 2:2 + T], cw_s[:, 2:3],
                                   tmp[:, :], ALU.mult, ALU.add)
    with nc.Block() as block:
        @block.scalar
        def _(s):
            s.activation(xs[:, :], tmp[:, :], AF.Silu, bias=cb_s[:, 0:1])
    with nc.Block() as block:
        @block.vector
        def _(v):
            v.tensor_copy(xsb[:, :], xs[:, :])
    with nc.Block() as block:
        @block.tensor
        def _(t):
            for ch in range(NCH):
                sl = slice(ch * 512, (ch + 1) * 512)
                t.matmul(psb0[:, sl], wdel_s[:, :], xsb[:, sl], start=True, stop=True)
                t.matmul(psb1[0:64, sl], wB_s[:, :], xsb[:, sl], start=True, stop=True)
                t.matmul(psb1[64:128, sl], wC_s[:, :], xsb[:, sl], start=True, stop=True)
    with nc.Block() as block:
        @block.scalar
        def _(s):
            s.activation(delta[:, :], psb0[:, :], AF.Abs, bias=bdel_s[:, 0:1])
            s.activation(delta[:, :], delta[:, :], AF.Exp, scale=-1.0)
            s.activation(delta[:, :], delta[:, :], AF.Ln, bias=1.0)
            s.activation(tmp[:, :], psb0[:, :], AF.Relu, bias=bdel_s[:, 0:1])
            s.activation(A_sb[:, :], alog_s[:, :], AF.Exp)
        @block.vector
        def _(v):
            v.tensor_scalar_add(Bm[:, :], psb1[0:64, :], bB_s[:, 0:1])
            v.tensor_scalar_add(Cm[:, :], psb1[64:128, :], bC_s[:, 0:1])
    with nc.Block() as block:
        @block.vector
        def _(v):
            v.tensor_add(delta[:, :], delta[:, :], tmp[:, :])
            v.tensor_mul(wdx[:, :], delta[:, :], xs[:, :])
            v.tensor_scalar_mul(A_sb[:, :], A_sb[:, :], -1.0)
    s_bc = p.sem()
    s_da = p.sem()
    for n in range(64):
        with nc.Block() as block:
            @block.tensor
            def _(t):
                for ch in range(NCH):
                    sl = slice(ch * 512, (ch + 1) * 512)
                    t.matmul(psb0[:, sl], selb_s[:, n * 128:(n + 1) * 128],
                             Bm[:, sl], start=True, stop=True)
                    inst = t.matmul(psb1[:, sl], selb_s[:, n * 128:(n + 1) * 128],
                                    Cm[:, sl], start=True, stop=True)
                s_bc.inc(inst)
            @block.scalar
            def _(s):
                s_da.inc(s.activation(dAn[:, :], delta[:, :], AF.Exp,
                                      scale=A_sb[:, n:n + 1]))
            @block.vector
            def _(v):
                s_bc.wait(v)
                s_da.wait(v)
                v.tensor_tensor(dBx[:, :], wdx[:, :], psb0[:, :], ALU.mult)
                v.tensor_tensor_scan(hn[:, :], dAn[:, :], dBx[:, :], 0.0,
                                     ALU.mult, ALU.add)
                v.tensor_tensor(hn[:, :], hn[:, :], psb1[:, :], ALU.mult)
                v.tensor_add(yacc[:, :], yacc[:, :], hn[:, :])
    with nc.Block() as block:
        @block.vector
        def _(v):
            v.scalar_tensor_tensor(yacc[:, :], xs[:, :], dskip_s[:, 0:1],
                                   yacc[:, :], ALU.mult, ALU.add)
        @block.scalar
        def _(s):
            s.activation(RIGHT[:, :], RIGHT[:, :], AF.Silu)
    with nc.Block() as block:
        @block.vector
        def _(v):
            v.tensor_mul(gat[:, :], yacc[:, :], RIGHT[:, :])
    st = p.sem()
    for m in range(NT):
        with nc.Block() as block:
            @block.tensor
            def _(t):
                for ch in range(NCH):
                    sl = slice(ch * 512, (ch + 1) * 512)
                    t.matmul(psb0[:, sl], wdn_s[:, m * 128:(m + 1) * 128],
                             gat[:, sl], start=True, stop=True)
        with nc.Block() as block:
            @block.vector
            def _(v):
                v.scalar_tensor_tensor(x[m][:, :], psb0[:, :], bdc_s[:, m:m + 1],
                                       x[m][:, :], ALU.add, ALU.add)
        with nc.Block() as block:
            @block.sync
            def _(sync):
                st.inc(sync.dma_start(out=oT[m * 128:(m + 1) * 128, :],
                                      in_=x[m][:, :]), 16)
    with nc.Block() as block:
        @block.sync
        def _(sync):
            st.wait(sync)
    return p


def mamba_inputs(xT_c, ins):
    sel = np.zeros((64, 64 * 128), np.float32)
    for n in range(64):
        sel[n, n * 128:(n + 1) * 128] = 1.0
    return {
        "xT": xT_c,
        "W_up": _f32c(ins["L0_attn_a_W_up"]),
        "b_upc": _f32c(np.asarray(ins["L0_attn_a_b_up"]).reshape(2, 128).T),
        "conv_w": _f32c(ins["L0_attn_a_conv_w"]),
        "conv_b": _f32c(np.asarray(ins["L0_attn_a_conv_b"]).reshape(128, 1)),
        "W_delta": _f32c(ins["L0_attn_a_W_delta"]),
        "b_delta": _f32c(np.asarray(ins["L0_attn_a_b_delta"]).reshape(128, 1)),
        "W_B": _f32c(ins["L0_attn_a_W_B"]),
        "b_B": _f32c(np.asarray(ins["L0_attn_a_b_B"]).reshape(64, 1)),
        "W_C": _f32c(ins["L0_attn_a_W_C"]),
        "b_C": _f32c(np.asarray(ins["L0_attn_a_b_C"]).reshape(64, 1)),
        "A_log": _f32c(ins["L0_attn_a_A_log"]),
        "D_skip": _f32c(np.asarray(ins["L0_attn_a_D_skip"]).reshape(128, 1)),
        "W_down": _f32c(ins["L0_attn_a_W_down"]),
        "b_dc": _f32c(np.asarray(ins["L0_attn_a_b_down"]).reshape(8, 128).T),
        "selb": sel,
    }


def kernel(**inputs):
    x = np.asarray(inputs["L0_attn_a_up_x"], dtype=np.float32)
    xT = [np.ascontiguousarray(x[c].T) for c in range(NCORES)]

    p = build_mamba()
    in_maps = [mamba_inputs(xT[c], inputs) for c in range(NCORES)]
    res = run_bass_kernel_spmd(p.nc, in_maps, core_ids=list(range(NCORES)))
    xT = [res.results[c]["oT"] for c in range(NCORES)]
    p.st.close()

    out = np.stack([xT[c].T for c in range(NCORES)], axis=0)
    return np.ascontiguousarray(out, dtype=np.float32)



def build_attn_b():
    p = Prog()
    io = {n: p.din(n, s) for n, s in ATTNB_SHAPES.items()}
    io["oT"] = p.dout("oT", [D, T])
    emit_attn_b(p, io)
    return p


ATTNB_SHAPES = dict(xT=[D, T], WqT=[D, D], WkT=[D, D], WvT=[D, D], WoT=[D, D],
                    bo_c=[128, 8], bd_c=[128, 128], eg_c=[16, 8 * 128],
                    e2_c=[128, 128])


def emit_attn_b(p, io):
    nc = p.nc
    xT = io["xT"]; WqT = io["WqT"]; WkT = io["WkT"]; WvT = io["WvT"]
    WoT = io["WoT"]; bo_c = io["bo_c"]; bd_c = io["bd_c"]; eg_c = io["eg_c"]
    e2_c = io["e2_c"]; oT = io["oT"]
    O3b = p.dram("O3b", [D, T], BF16)

    SC = 256
    NSC = T // SC
    pacc = p.sb("pacc", [128, 8 * T], F32)
    acc = [pacc[:, g * T:(g + 1) * T] for g in range(NT)]
    paccb = pacc[:, :].bitcast(BF16)
    xhat = [paccb[:, g * T:(g + 1) * T] for g in range(NT)]
    xres = [pacc[:, 0:T], pacc[:, T:2 * T]]
    qf = [p.sb("qf%d" % g, [128, T], BF16) for g in range(NT)]
    kf = [p.sb("kf%d" % g, [128, T], BF16) for g in range(NT)]
    vf = [p.sb("vf%d" % g, [128, T], BF16) for g in range(NT)]
    wbuf = [p.sb("wb%d" % k, [128, D], BF16) for k in range(NT)]
    tmp = p.sb("tmp", [128, T], F32)
    Gc = p.sb("Gc", [16, 16 * SC], F32)
    Gm = p.sb("Gm", [16, 2 * SC], F32)
    kbcs = p.sb("kbcs", [128, SC], BF16)
    prodb = p.sb("prodb", [128, SC], BF16)
    vbcs = p.sb("vbcs", [128, SC], BF16)
    bd_s = p.sb("bd_s", [128, 128], BF16)
    eg_s = p.sb("eg_s", [16, 8 * 128], BF16)
    e2_s = p.sb("e2_s", [128, 128], BF16)
    bo_s = p.sb("bo_s", [128, 8], F32)

    Ac = tmp[:, :].bitcast(BF16)[0:16, 0:16 * SC]
    stg = kf[0]
    p.add_consts([EPS, 0.125])
    psb0 = p.ps("psb0", [128, T])
    psb1 = p.ps("psb1", [128, T])

    lds = p.sem()
    emit_loads(p, lds, plain=[(bo_s[:, :], bo_c)],
               cast=[(bd_s[:, :], bd_c), (eg_s[:, :], eg_c), (e2_s[:, :], e2_c)])
    ones_row = p.sb("ones_row", [1, 128], F32)
    ones_col = p.sb("ones_col", [128, 1], F32)
    with nc.Block() as block:
        @block.gpsimd
        def _(gp):
            gp.memset(ones_row[:, :], 1.0)
            gp.memset(ones_col[:, :], 1.0)
            p.emit_const_init(gp)
        @block.vector
        def _(v):
            lds.wait(v)

    s_x = p.sem()
    for g in range(NT):
        b = g % 2
        with nc.Block() as block:
            @block.sync
            def _(sync):
                s_x.inc(sync.dma_start(out=xres[b][:, :],
                                       in_=xT[g * 128:(g + 1) * 128, :]), 16)
            @block.scalar
            def _(s):
                s_x.wait(s)
                s.activation(tmp[:, :], xres[b][:, :], AF.Square)
        with nc.Block() as block:
            @block.tensor
            def _(t):
                for ch in range(NCH):
                    sl = slice(ch * 512, (ch + 1) * 512)
                    t.matmul(psb0[0:1, sl], ones_col[:, :], xres[b][:, sl],
                             start=(g == 0), stop=(g == NT - 1))
                    t.matmul(psb1[0:1, sl], ones_col[:, :], tmp[:, sl],
                             start=(g == 0), stop=(g == NT - 1))
    mrow = tmp[0:1, 0:T]
    vrow = tmp[0:1, T:2 * T] if False else None
    vrow = Gc[0:1, 0:T]
    scr = Gc[0:1, T:2 * T]
    with nc.Block() as block:
        @block.vector
        def _(v):
            v.tensor_scalar_mul(mrow, psb0[0:1, :], 1.0 / D)
            v.tensor_scalar_mul(vrow, psb1[0:1, :], 1.0 / D)
            v.tensor_tensor(scr, mrow, mrow, ALU.mult)
            v.tensor_sub(vrow, vrow, scr)
    with nc.Block() as block:
        @block.scalar
        def _(s):
            s.activation(vrow, vrow, AF.Sqrt, bias=EPS)
    with nc.Block() as block:
        @block.vector
        def _(v):
            v.reciprocal(vrow, vrow)
            v.scalar_tensor_tensor(mrow, mrow, -1.0, vrow, ALU.mult, ALU.mult)
    with nc.Block() as block:
        @block.tensor
        def _(t):
            for ch in range(NCH):
                sl = slice(ch * 512, (ch + 1) * 512)
                t.matmul(psb0[:, sl], ones_row[:, :], vrow[:, sl],
                         start=True, stop=True)
                t.matmul(psb1[:, sl], ones_row[:, :], mrow[:, sl],
                         start=True, stop=True)
    for g in range(NT):
        b = g % 2
        with nc.Block() as block:
            @block.sync
            def _(sync):
                s_x.inc(sync.dma_start(out=tmp[:, :],
                                       in_=xT[g * 128:(g + 1) * 128, :]), 16)
            @block.vector
            def _(v):
                s_x.wait(v)
                v.tensor_tensor(tmp[:, :], tmp[:, :], psb0[:, :], ALU.mult)
                v.scalar_tensor_tensor(xhat[g][:, :], tmp[:, :], 1.0,
                                       psb1[:, :], ALU.mult, ALU.add)

    s_w = p.sem()
    for Wd, dst in ((WqT, qf), (WkT, kf), (WvT, vf)):
        with nc.Block() as block:
            @block.gpsimd
            def _(gp):
                for k in range(NT):
                    s_w.inc(gp.dma_start(out=wbuf[k][:, :],
                                         in_=Wd[k * 128:(k + 1) * 128, :]), 16)
            @block.tensor
            def _(t):
                s_w.wait(t)
        for m in range(NT):
            with nc.Block() as block:
                @block.tensor
                def _(t):
                    for ch in range(NCH):
                        sl = slice(ch * 512, (ch + 1) * 512)
                        for k in range(NT):
                            t.matmul(psb0[:, sl], wbuf[k][:, m * 128:(m + 1) * 128],
                                     xhat[k][:, sl], start=(k == 0),
                                     stop=(k == NT - 1))
            with nc.Block() as block:
                @block.vector
                def _(v):
                    v.tensor_copy(dst[m][:, :], psb0[:, :])

    with nc.Block() as block:
        @block.gpsimd
        def _(gp):
            for g in range(NT):
                gp.memset(acc[g][:, :], 0.0)
    for sc in range(NSC):
        ssl = slice(sc * SC, (sc + 1) * SC)
        for kh in range(16):
            with nc.Block() as block:
                @block.tensor
                def _(t):
                    pb = 64 * (kh % 2)
                    t.matmul(psb0[:, 0:SC], e2_s[pb:pb + 64, :],
                             kf[kh // 2][pb:pb + 64, ssl],
                             start=True, stop=True)
            with nc.Block() as block:
                @block.vector
                def _(v):
                    v.tensor_copy(kbcs[:, :], psb0[:, 0:SC])
            for g in range(NT):
                with nc.Block() as block:
                    @block.vector
                    def _(v):
                        v.tensor_tensor(prodb[:, :], qf[g][:, ssl],
                                        kbcs[:, :], ALU.mult)
                with nc.Block() as block:
                    @block.tensor
                    def _(t):
                        t.matmul(psb1[0:16, 0:SC], bd_s[:, g * 16:(g + 1) * 16],
                                 prodb[:, :], start=(g == 0), stop=(g == NT - 1))
            with nc.Block() as block:
                @block.vector
                def _(v):
                    v.tensor_copy(
                        Gc[:, 0:16 * SC].rearrange("p (s k) -> p s k", k=16)[
                            :, :, kh:kh + 1],
                        psb1[0:16, 0:SC].unsqueeze(2))
        with nc.Block() as block:
            @block.vector
            def _(v):
                v.reduce_max(Gm[:, 0:SC],
                             Gc[:, 0:16 * SC].rearrange("p (s k) -> p s k", k=16),
                             axis=AX.X)
        with nc.Block() as block:
            @block.vector
            def _(v):
                v.tensor_sub(Gc[:, 0:16 * SC].rearrange("p (s k) -> p s k", k=16),
                             Gc[:, 0:16 * SC].rearrange("p (s k) -> p s k", k=16),
                             Gm[:, 0:SC].unsqueeze(2).broadcast_to([16, SC, 16]))
        with nc.Block() as block:
            @block.scalar
            def _(s):
                s.activation(Gc[:, 0:16 * SC], Gc[:, 0:16 * SC], AF.Exp,
                             scale=0.125)
        with nc.Block() as block:
            @block.vector
            def _(v):
                v.reduce_sum(Gm[:, SC:2 * SC],
                             Gc[:, 0:16 * SC].rearrange("p (s k) -> p s k", k=16),
                             axis=AX.X)
        with nc.Block() as block:
            @block.vector
            def _(v):
                v.reciprocal(Gm[:, SC:2 * SC], Gm[:, SC:2 * SC])
        with nc.Block() as block:
            @block.vector
            def _(v):
                v.tensor_tensor(Ac[:, 0:16 * SC].rearrange("p (s k) -> p s k", k=16),
                                Gc[:, 0:16 * SC].rearrange("p (s k) -> p s k", k=16),
                                Gm[:, SC:2 * SC].unsqueeze(2).broadcast_to([16, SC, 16]),
                                ALU.mult)
        for kh in range(16):
            with nc.Block() as block:
                @block.tensor
                def _(t):
                    pb = 64 * (kh % 2)
                    t.matmul(psb0[:, 0:SC], e2_s[pb:pb + 64, :],
                             vf[kh // 2][pb:pb + 64, ssl],
                             start=True, stop=True)
            with nc.Block() as block:
                @block.vector
                def _(v):
                    v.tensor_copy(vbcs[:, :], psb0[:, 0:SC])
            with nc.Block() as block:
                @block.tensor
                def _(t):
                    for g in range(NT):
                        t.matmul(psb1[:, g * SC:(g + 1) * SC] if SC * NT <= T
                                 else psb1[:, 0:SC],
                                 eg_s[:, g * 128:(g + 1) * 128],
                                 Ac[:, 0:16 * SC].rearrange(
                                     "p (s k) -> p k s", k=16)[:, kh, :],
                                 start=True, stop=True)
            with nc.Block() as block:
                @block.vector
                def _(v):
                    for g in range(NT):
                        v.tensor_tensor(prodb[:, :],
                                        psb1[:, g * SC:(g + 1) * SC],
                                        vbcs[:, :], ALU.mult)
                        v.tensor_tensor(acc[g][:, ssl], acc[g][:, ssl],
                                        prodb[:, :], ALU.add)

    st = p.sem()
    for g in range(NT):
        with nc.Block() as block:
            @block.scalar
            def _(s):
                s.copy(stg[:, 0:T].rearrange("p (u w) -> p w u", u=16),
                       acc[g][:, :])
        with nc.Block() as block:
            @block.sync
            def _(sync):
                st.inc(sync.dma_start(out=O3b[g * 128:(g + 1) * 128, :],
                                      in_=stg[:, :]), 16)
    with nc.Block() as block:
        @block.sync
        def _(sync):
            st.wait(sync)
    xrs = qf
    s_v = p.sem()
    with nc.Block() as block:
        @block.gpsimd
        def _(gp):
            for jt in range(NT):
                for ui in range(2):
                    u = jt * 2 + ui
                    s_v.inc(gp.dma_start(
                        out=xrs[jt][64 * ui:64 * ui + 64, :],
                        in_=bass.AP(O3b.tensor, u * 128,
                                    [[T, 64], [64 * T, 16], [1, 128]])), 16)
            for k in range(NT):
                s_w.inc(gp.dma_start(out=wbuf[k][:, :],
                                     in_=WoT[k * 128:(k + 1) * 128, :]), 16)
        @block.tensor
        def _(t):
            s_v.wait(t)
            s_w.wait(t)
    for m in range(NT):
        with nc.Block() as block:
            @block.sync
            def _(sync):
                s_x.inc(sync.dma_start(out=tmp[:, :],
                                       in_=xT[m * 128:(m + 1) * 128, :]), 16)
            @block.tensor
            def _(t):
                for ch in range(NCH):
                    sl = slice(ch * 512, (ch + 1) * 512)
                    for k in range(NT):
                        t.matmul(psb0[:, sl], wbuf[k][:, m * 128:(m + 1) * 128],
                                 xrs[k][:, sl], start=(k == 0), stop=(k == NT - 1))
        with nc.Block() as block:
            @block.vector
            def _(v):
                s_x.wait(v)
                v.scalar_tensor_tensor(tmp[:, :], psb0[:, :], bo_s[:, m:m + 1],
                                       tmp[:, :], ALU.add, ALU.add)
        with nc.Block() as block:
            @block.sync
            def _(sync):
                st.inc(sync.dma_start(out=oT[m * 128:(m + 1) * 128, :],
                                      in_=tmp[:, :]), 16)
    with nc.Block() as block:
        @block.sync
        def _(sync):
            st.wait(sync)
    return p


def attn_b_inputs(xT_c, Wq, Wk, Wv, Wo, bo):
    bd = np.zeros((128, 128), np.float32)
    eg = np.zeros((16, 8 * 128), np.float32)
    for g in range(8):
        for hh in range(16):
            if hh // 2 == g:
                bd[64 * (hh % 2):64 * (hh % 2) + 64, g * 16 + hh] = 1.0
        for pp in range(128):
            eg[2 * g + pp // 64, g * 128 + pp] = 1.0
    e2 = np.zeros((128, 128), np.float32)
    for pp in range(128):
        e2[pp % 64, pp] = 1.0
        e2[64 + pp % 64, pp] = 1.0
    return {
        "xT": xT_c,
        "WqT": _f32c(np.asarray(Wq).T),
        "WkT": _f32c(np.asarray(Wk).T),
        "WvT": _f32c(np.asarray(Wv).T),
        "WoT": _f32c(np.asarray(Wo).T),
        "bo_c": _f32c(np.asarray(bo).reshape(8, 128).T),
        "bd_c": bd, "eg_c": eg, "e2_c": e2,
    }



A_CH = [(i * 128, 128) for i in range(21)] + [(2688, 42)]
SWAP_MASK = [i ^ 1 for i in range(32)]


INNER_INS = ["yT", "Wqkv", "Wop", "qs_c", "lng_c", "W1", "b1a", "b1g", "W2",
             "b2_c", "CQ_c", "SQ_c", "CK_c", "SK_c", "M2_c", "bd_c", "eg_c",
             "id_c"]
INNER_SHAPES = dict(yT=[D, T], Wqkv=[D, 3 * D], Wop=[D, D], qs_c=[128, 2],
                    lng_c=[128, 16], W1=[D, 5460], b1a=[128, 22],
                    b1g=[128, 22], W2=[2730, D], b2_c=[128, 8], CQ_c=[128, T],
                    SQ_c=[128, T], CK_c=[128, T], SK_c=[128, T],
                    M2_c=[128, 384], bd_c=[128, 128], eg_c=[16, 8 * 128],
                    id_c=[128, 128])


def build_inner():
    p = Prog()
    io = {n: p.din(n, INNER_SHAPES[n]) for n in INNER_INS}
    io["oT"] = p.dout("oT", [D, T])
    emit_inner(p, io)
    return p


def emit_inner(p, io):
    nc = p.nc
    (yT, Wqkv, Wop, qs_c, lng_c, W1, b1a, b1g, W2, b2_c, CQ_c, SQ_c, CK_c,
     SK_c, M2_c, bd_c, eg_c, id_c) = [io[n] for n in INNER_INS]
    oT = io["oT"]
    ymid = p.dram("ymid", [D, T], F32)

    pool = p.sb("bigpool", [128, 72 * 1024], BF16)

    def bt(off_kb, n):
        o = off_kb * 512
        return pool[:, o:o + n]
    ybf = [bt(4 * g, T) for g in range(NT)]
    ofm = ybf
    qb = [bt(32 + 4 * g, T) for g in range(NT)]
    kb = [bt(64 + 4 * g, T) for g in range(NT)]
    vtm = [bt(96 + 2 * u, D) for u in range(16)]
    CQ = bt(128, T); SQ = bt(132, T); CK = bt(136, T); SK = bt(140, T)
    xf = [bt(32 + 4 * g, T) for g in range(NT)]
    hb = [bt(64 + 4 * i, T) for i in range(20)]
    hb = hb + [bt(0 + 4 * i, T) for i in range(2)]

    stage = [p.sb("stg%d" % i, [128, T + 1], F32) for i in range(2)]
    rqb = stage[1][0:16, 0:1024].bitcast(BF16)
    tmp = p.sb("tmp", [128, T], F32)
    tshb2 = tmp[:, :].bitcast(BF16)[:, 0:T]
    tshb = bt(124, T)
    nrm = p.sb("nrm", [80, 2 * T + 64], F32)
    rq = nrm[0:16, 0:T]
    rk = nrm[0:16, T:2 * T]
    r3 = nrm[64:80, 0:T]
    k3n = nrm[64:80, T:2 * T]
    khm = nrm[64:80, 2 * T:2 * T + 18]
    khw = nrm[64:80, 2 * T + 20:2 * T + 36]
    kht = nrm[64:80, 2 * T + 40:2 * T + 56]
    mhc = p.sb("mhc", [128, 256], F32)
    Pb = p.sb("Pb", [128, 384], BF16)
    PT = p.sb("PT", [128, 384], BF16)
    zcol = p.sb("zcol", [128, 2], F32)
    Db = p.sb("Db", [128, 128], BF16)
    ones_row = p.sb("ones_row", [1, 128], F32)
    ones_col = p.sb("ones_col", [128, 1], F32)
    wbig = p.sb("wbig", [128, 6144], BF16)
    wcol = [wbig[:, b * 1024:(b + 1) * 1024] for b in range(2)]
    wcg = [wbig[:, 2048 + b * 1024:2048 + (b + 1) * 1024] for b in range(2)]
    wrow = wbig[:, 0:4096]
    w2c = [wbig[:, b * 2816:(b + 1) * 2816] for b in range(2)]
    bd_s = p.sb("bd_s", [128, 128], F32)
    eg_s = p.sb("eg_s", [16, 8 * 128], BF16)
    idb = p.sb("idb", [128, 128], BF16)
    M2b = p.sb("M2b", [128, 384], BF16)
    qs_s = p.sb("qs_s", [128, 4], F32)
    lng_s = p.sb("lng_s", [128, 16], F32)
    b1a_s = p.sb("b1a_s", [128, 22], F32)
    b1g_s = p.sb("b1g_s", [128, 22], F32)
    b2_s = p.sb("b2_s", [128, 8], F32)

    p.add_consts([EPS, 8.0])
    psb0 = p.ps("psb0", [128, T])
    psb1 = p.ps("psb1", [128, T])
    psst = psb1[0:16, :]
    psot = psb1[0:64, 0:128]

    lds = p.sem()
    emit_loads(
        p, lds,
        plain=[(bd_s[:, :], bd_c), (qs_s[:, 0:2], qs_c), (lng_s[:, :], lng_c),
               (b1a_s[:, :], b1a), (b1g_s[:, :], b1g), (b2_s[:, :], b2_c)],
        cast=[(CQ[:, :], CQ_c), (SQ[:, :], SQ_c), (CK[:, :], CK_c),
              (SK[:, :], SK_c), (M2b[:, :], M2_c), (idb[:, :], id_c),
              (eg_s[:, :], eg_c)],
    )
    with nc.Block() as block:
        @block.gpsimd
        def _(gp):
            gp.memset(ones_row[:, :], 1.0)
            gp.memset(ones_col[:, :], 1.0)
            gp.memset(stage[0][:, 0:1], 0.0)
            gp.memset(stage[1][:, 0:1], 0.0)
            p.emit_const_init(gp)
        @block.vector
        def _(v):
            lds.wait(v)
            v.stream_shuffle(qs_s[:, 2:3], qs_s[:, 0:1], SWAP_MASK)
            v.stream_shuffle(qs_s[:, 3:4], qs_s[:, 1:2], SWAP_MASK)
    with nc.Block() as block:
        @block.vector
        def _(v):
            v.tensor_scalar_mul(CQ[:, :], CQ[:, :], qs_s[:, 0:1])
            v.tensor_scalar_mul(SQ[:, :], SQ[:, :], qs_s[:, 2:3])
            v.tensor_scalar_mul(CK[:, :], CK[:, :], qs_s[:, 1:2])
            v.tensor_scalar_mul(SK[:, :], SK[:, :], qs_s[:, 3:4])

    idf = p.sb("idf", [128, 128], F32)
    with nc.Block() as block:
        @block.sync
        def _(sync):
            lds.inc(sync.dma_start(out=idf[:, :], in_=id_c), 16)
        @block.vector
        def _(v):
            lds.wait(v)

    s_y = p.sem()
    for g in range(NT):
        with nc.Block() as block:
            @block.sync
            def _(sync):
                s_y.inc(sync.dma_start(out=stage[g % 2][:, 1:T + 1],
                                       in_=yT[g * 128:(g + 1) * 128, :]), 16)
            @block.vector
            def _(v):
                s_y.wait(v)
                sh = 0 if g >= 4 else 1
                v.tensor_copy(ybf[g][:, :], stage[g % 2][:, sh:sh + T])

    s_w = p.sem()
    def load_wcols(eng, buf, src_d, col0, width=128, nk=NT, rk2=128):
        for k in range(nk):
            s_w.inc(eng.dma_start(
                out=buf[:, k * width:(k + 1) * width][0:rk2, :],
                in_=src_d[k * 128:k * 128 + rk2, col0:col0 + width]), 16)

    s_mmB = p.sem()
    s_evB = p.sem()
    s_sqB = p.sem()
    bw0 = s_w.v
    with nc.Block() as block:
        @block.gpsimd
        def _(gp):
            load_wcols(gp, wcol[0], Wqkv, 0)
    for half in range(2):
        bmm, bev, bsq, bw = s_mmB.v, s_evB.v, s_sqB.v, 0
        ms = range(half * 8, half * 8 + 8)
        with nc.Block() as block:
            @block.gpsimd
            def _(gp):
                for mi, m in enumerate(ms):
                    gp.wait_ge(s_mmB.h, bmm + max(0, 4 * mi))
                    if m < 15:
                        load_wcols(gp, wcol[(m + 1) % 2], Wqkv, (m + 1) * 128)
            @block.tensor
            def _(t):
                for mi, m in enumerate(ms):
                    buf = wcol[m % 2]
                    t.wait_ge(s_w.h, bw0 + 16 * 8 * (m + 1))
                    for ch in range(NCH):
                        C = mi * 4 + ch
                        sl = slice(ch * 512, (ch + 1) * 512)
                        t.wait_ge(s_evB.h, bev + max(0, C - 3))
                        for k in range(NT):
                            ins = t.matmul(psb0[:, sl],
                                           buf[:, k * 128:(k + 1) * 128],
                                           ybf[k][:, sl], start=(k == 0),
                                           stop=(k == NT - 1))
                        ins.then_inc(s_mmB.h, 1)
                    for ch in range(NCH):
                        sl = slice(ch * 512, (ch + 1) * 512)
                        t.wait_ge(s_sqB.h, bsq + mi * 4 + ch + 1)
                        t.matmul(psb1[0:16, sl], bd_s[:, (m % 8) * 16:(m % 8) * 16 + 16],
                                 tmp[:, sl], start=(mi == 0), stop=(mi == 7))
            @block.scalar
            def _(s):
                for mi, m in enumerate(ms):
                    out_t = qb[m] if m < 8 else kb[m - 8]
                    for ch in range(NCH):
                        C = mi * 4 + ch
                        sl = slice(ch * 512, (ch + 1) * 512)
                        s.wait_ge(s_mmB.h, bmm + C + 1)
                        s.activation(tmp[:, sl], psb0[:, sl], AF.Square
                                     ).then_inc(s_sqB.h, 1)
                        s.activation(out_t[:, sl], psb0[:, sl], AF.Copy
                                     ).then_inc(s_evB.h, 1)
        s_mmB.v += 32
        s_sqB.v += 32
        s_evB.v += 32
        dst = rq if half == 0 else rk
        with nc.Block() as block:
            @block.scalar
            def _(s):
                s.activation(dst, psst[:, :], AF.Sqrt)
        with nc.Block() as block:
            @block.vector
            def _(v):
                v.tensor_scalar_max(dst, dst, 1e-12)
        with nc.Block() as block:
            @block.vector
            def _(v):
                v.reciprocal(dst, dst)

    for side in range(2):
        src_t = qb if side == 0 else kb
        Cp = CQ if side == 0 else CK
        Sp = SQ if side == 0 else SK
        rr = rq if side == 0 else rk
        with nc.Block() as block:
            @block.vector
            def _(v):
                v.tensor_copy(rqb[:, :], rr)
        for g in range(NT):
            with nc.Block() as block:
                @block.tensor
                def _(t):
                    for ch in range(NCH):
                        sl = slice(ch * 512, (ch + 1) * 512)
                        t.matmul(psb0[:, sl], eg_s[:, g * 128:(g + 1) * 128],
                                 rqb[:, sl], start=True, stop=True)
                @block.vector
                def _(v):
                    v.stream_shuffle(tshb[:, :], src_t[g][:, :], SWAP_MASK)
                    v.tensor_tensor(tshb2[:, :], src_t[g][:, :], Cp[:, :], ALU.mult)
                    v.tensor_tensor(tshb[:, :], tshb[:, :], Sp[:, :], ALU.mult)
            with nc.Block() as block:
                @block.vector
                def _(v):
                    v.tensor_add(tshb2[:, :], tshb2[:, :], tshb[:, :])
                    v.tensor_tensor(src_t[g][:, :], tshb2[:, :], psb0[:, :],
                                    ALU.mult)
        for g in range(NT):
            with nc.Block() as block:
                @block.scalar
                def _(s):
                    s.activation(tmp[:, :], src_t[g][:, :], AF.Square)
            with nc.Block() as block:
                @block.tensor
                def _(t):
                    for ch in range(NCH):
                        sl = slice(ch * 512, (ch + 1) * 512)
                        t.matmul(psb1[64:80, sl], bd_s[:, g * 16:(g + 1) * 16],
                                 tmp[:, sl], start=(g == 0), stop=(g == NT - 1))
        with nc.Block() as block:
            @block.scalar
            def _(s):
                s.activation(r3 if side == 0 else k3n, psb1[64:80, :], AF.Sqrt)
    with nc.Block() as block:
        @block.gpsimd
        def _(gp):
            gp.memset(khm[:, 0:2], 0.0)
        @block.vector
        def _(v):
            v.reduce_max(khm[:, 2:18], k3n.rearrange("p (c u) -> p c u", u=128),
                         axis=AX.X)
    with nc.Block() as block:
        @block.vector
        def _(v):
            v.tensor_max(kht, khm[:, 0:16], khm[:, 1:17])
    with nc.Block() as block:
        @block.vector
        def _(v):
            v.tensor_max(khw, kht, khm[:, 2:18])
    with nc.Block() as block:
        @block.vector
        def _(v):
            for qc in range(16):
                v.scalar_tensor_tensor(
                    r3[:, qc * 128:(qc + 1) * 128],
                    r3[:, qc * 128:(qc + 1) * 128], -8.0,
                    khw[:, qc:qc + 1].broadcast_to([16, 128]),
                    ALU.mult, ALU.mult)
    with nc.Block() as block:
        @block.tensor
        def _(t):
            for qc in range(16):
                t.transpose(psb0[:, qc * 16:(qc + 1) * 16],
                            r3[:, qc * 128:(qc + 1) * 128], idf[64:80, 0:16])
    with nc.Block() as block:
        @block.vector
        def _(v):
            v.tensor_copy(mhc[:, :], psb0[:, 0:256])

    s_mmD = p.sem()
    s_evD = p.sem()
    for oc in range(2):
        bw0 = s_w.v
        bmm, bev = s_mmD.v, s_evD.v
        with nc.Block() as block:
            @block.gpsimd
            def _(gp):
                for k in range(NT):
                    s_w.inc(gp.dma_start(
                        out=wrow[:, k * 512:(k + 1) * 512],
                        in_=Wqkv[k * 128:(k + 1) * 128,
                                 2048 + oc * 512:2560 + oc * 512]), 16)
        with nc.Block() as block:
            @block.tensor
            def _(t):
                t.wait_ge(s_w.h, bw0 + 16 * NT)
                for u in range(16):
                    t.wait_ge(s_evD.h, bev + max(0, u - 3))
                    for k in range(NT):
                        ins = t.matmul(psb0[:, (u % 4) * 512:(u % 4) * 512 + 512],
                                       ybf[k][:, u * 128:(u + 1) * 128],
                                       wrow[:, k * 512:(k + 1) * 512],
                                       start=(k == 0), stop=(k == NT - 1))
                    ins.then_inc(s_mmD.h, 1)
            @block.vector
            def _(v):
                for u in range(16):
                    v.wait_ge(s_mmD.h, bmm + u + 1)
                    v.tensor_copy(vtm[u][:, oc * 512:(oc + 1) * 512],
                                  psb0[:, (u % 4) * 512:(u % 4) * 512 + 512]
                                  ).then_inc(s_evD.h, 1)
        s_mmD.v += 16
        s_evD.v += 16

    s_sc = p.sem(); s_zi = p.sem(); s_exp = p.sem(); s_db = p.sem()
    s_tp = p.sem(); s_ptv = p.sem(); s_ot = p.sem()
    b_sc, b_zi, b_exp, b_db, b_tp, b_ptv, b_ot = (
        s_sc.v, s_zi.v, s_exp.v, s_db.v, s_tp.v, s_ptv.v, s_ot.v)
    iters = [(h, qc) for h in range(16) for qc in range(16)]
    with nc.Block() as block:
        @block.tensor
        def _(t):
            for i, (h, qc) in enumerate(iters):
                gt, pb = h // 2, 64 * (h % 2)
                qsl = slice(qc * 128, (qc + 1) * 128)
                kr0 = max(0, qc * 128 - 256)
                w = qc * 128 + 128 - kr0
                nu = w // 128
                t.wait_ge(s_ptv.h, b_ptv + i)
                t.matmul(psb0[:, 0:w], qb[gt][pb:pb + 64, qsl],
                         kb[gt][pb:pb + 64, kr0:kr0 + w], start=True, stop=False)
                t.matmul(psb0[:, 0:w], idb[:, :], M2b[:, 384 - w:384],
                         start=False, stop=True).then_inc(s_sc.h, 1)
                t.wait_ge(s_exp.h, b_exp + i + 1)
                t.wait_ge(s_db.h, b_db + i + 1)
                for u in range(nu):
                    ins = t.matmul(psb0[:, 512 + u * 128:512 + (u + 1) * 128],
                                   Pb[:, u * 128:(u + 1) * 128], Db[:, :],
                                   start=True, stop=True)
                ins.then_inc(s_tp.h, 1)
                t.wait_ge(s_ptv.h, b_ptv + i + 1)
                for u in range(nu):
                    ins = t.matmul(psot[:, :],
                                   vtm[kr0 // 128 + u][:, 64 * h:64 * h + 64],
                                   PT[:, u * 128:(u + 1) * 128],
                                   start=(u == 0), stop=(u == nu - 1))
                ins.then_inc(s_ot.h, 1)
        @block.scalar
        def _(s):
            for i, (h, qc) in enumerate(iters):
                kr0 = max(0, qc * 128 - 256)
                w = qc * 128 + 128 - kr0
                s.wait_ge(s_sc.h, b_sc + i + 1)
                if i > 0:
                    s.wait_ge(s_tp.h, b_tp + i)
                s.activation(Pb[:, 0:w], psb0[:, 0:w], AF.Exp,
                             bias=mhc[:, qc * 16 + h:qc * 16 + h + 1],
                             scale=8.0, accum_out=zcol[:, 0:1]
                             ).then_inc(s_exp.h, 1)
        @block.vector
        def _(v):
            for i, (h, qc) in enumerate(iters):
                gt, pb = h // 2, 64 * (h % 2)
                qsl = slice(qc * 128, (qc + 1) * 128)
                kr0 = max(0, qc * 128 - 256)
                w = qc * 128 + 128 - kr0
                v.wait_ge(s_exp.h, b_exp + i + 1)
                v.reciprocal(zcol[:, 1:2], zcol[:, 0:1]).then_inc(s_zi.h, 1)
                v.wait_ge(s_zi.h, b_zi + i + 1)
                v.tensor_scalar_mul(Db[:, :], idf[:, :], zcol[:, 1:2]
                                    ).then_inc(s_db.h, 1)
                v.wait_ge(s_tp.h, b_tp + i + 1)
                v.tensor_copy(PT[:, 0:w], psb0[:, 512:512 + w]
                              ).then_inc(s_ptv.h, 1)
                v.wait_ge(s_ot.h, b_ot + i + 1)
                v.tensor_copy(ofm[gt][pb:pb + 64, qsl], psot[:, :])
    for s in (s_sc, s_zi, s_exp, s_db, s_tp, s_ptv, s_ot):
        s.v += len(iters)

    s_st = p.sem()
    s_mmF = p.sem()
    s_evF = p.sem()
    bw0 = s_w.v
    bsy = s_y.v
    bmm, bev, bst = s_mmF.v, s_evF.v, s_st.v
    with nc.Block() as block:
        @block.gpsimd
        def _(gp):
            load_wcols(gp, wcol[0], Wop, 0)
    with nc.Block() as block:
        @block.gpsimd
        def _(gp):
            for m in range(NT):
                gp.wait_ge(s_mmF.h, bmm + max(0, 4 * m))
                if m < NT - 1:
                    load_wcols(gp, wcol[(m + 1) % 2], Wop, (m + 1) * 128)
        @block.sync
        def _(sync):
            for m in range(min(2, NT)):
                s_y.inc(sync.dma_start(out=stage[m % 2][:, 1:T + 1],
                                       in_=yT[m * 128:(m + 1) * 128, :]), 16)
            for m in range(NT):
                sync.wait_ge(s_evF.h, bev + 4 * (m + 1))
                s_st.inc(sync.dma_start(out=ymid[m * 128:(m + 1) * 128, :],
                                        in_=stage[m % 2][:, 1:T + 1]), 16)
                if m + 2 < NT:
                    s_y.inc(sync.dma_start(
                        out=stage[m % 2][:, 1:T + 1],
                        in_=yT[(m + 2) * 128:(m + 3) * 128, :]), 16)
        @block.tensor
        def _(t):
            for m in range(NT):
                buf = wcol[m % 2]
                t.wait_ge(s_w.h, bw0 + 16 * 8 * (m + 1))
                for ch in range(NCH):
                    C = m * 4 + ch
                    sl = slice(ch * 512, (ch + 1) * 512)
                    t.wait_ge(s_evF.h, bev + max(0, C - 3))
                    for k in range(NT):
                        ins = t.matmul(psb0[:, sl], buf[:, k * 128:(k + 1) * 128],
                                       ofm[k][:, sl], start=(k == 0),
                                       stop=(k == NT - 1))
                    ins.then_inc(s_mmF.h, 1)
        @block.vector
        def _(v):
            for m in range(NT):
                v.wait_ge(s_y.h, bsy + 16 * (m + 1))
                for ch in range(NCH):
                    C = m * 4 + ch
                    sl = slice(ch * 512, (ch + 1) * 512)
                    v.wait_ge(s_mmF.h, bmm + C + 1)
                    v.tensor_tensor(stage[m % 2][:, 1 + ch * 512:1 + (ch + 1) * 512],
                                    stage[m % 2][:, 1 + ch * 512:1 + (ch + 1) * 512],
                                    psb0[:, sl], ALU.add).then_inc(s_evF.h, 1)
    s_mmF.v += 32
    s_evF.v += 32

    with nc.Block() as block:
        @block.sync
        def _(sync):
            s_st.wait(sync)
    for g in range(NT):
        with nc.Block() as block:
            @block.sync
            def _(sync):
                s_y.inc(sync.dma_start(out=stage[g % 2][:, 1:T + 1],
                                       in_=ymid[g * 128:(g + 1) * 128, :]), 16)
            @block.scalar
            def _(s):
                s_y.wait(s)
                sh = 0 if g >= 4 else 1
                s.activation(tmp[:, :], stage[g % 2][:, sh:sh + T], AF.Square)
        with nc.Block() as block:
            @block.tensor
            def _(t):
                sh = 0 if g >= 4 else 1
                for ch in range(NCH):
                    sl = slice(ch * 512, (ch + 1) * 512)
                    t.matmul(psb0[0:1, sl], ones_col[:, :],
                             stage[g % 2][:, sh:sh + T][:, sl],
                             start=(g == 0), stop=(g == NT - 1))
                    t.matmul(psb1[0:1, sl], ones_col[:, :], tmp[:, sl],
                             start=(g == 0), stop=(g == NT - 1))
    mrow = tmp[0:1, 0:T]
    vrow = stage[0][0:1, 1:T + 1]
    scr = stage[1][0:1, 1:T + 1]
    with nc.Block() as block:
        @block.vector
        def _(v):
            v.tensor_scalar_mul(mrow, psb0[0:1, :], 1.0 / D)
            v.tensor_scalar_mul(vrow, psb1[0:1, :], 1.0 / D)
            v.tensor_tensor(scr, mrow, mrow, ALU.mult)
            v.tensor_sub(vrow, vrow, scr)
    with nc.Block() as block:
        @block.scalar
        def _(s):
            s.activation(vrow, vrow, AF.Sqrt, bias=EPS)
    with nc.Block() as block:
        @block.vector
        def _(v):
            v.reciprocal(vrow, vrow)
            v.scalar_tensor_tensor(mrow, mrow, -1.0, vrow, ALU.mult, ALU.mult)
    with nc.Block() as block:
        @block.tensor
        def _(t):
            for ch in range(NCH):
                sl = slice(ch * 512, (ch + 1) * 512)
                t.matmul(psb0[:, sl], ones_row[:, :], vrow[:, sl],
                         start=True, stop=True)
                t.matmul(psb1[:, sl], ones_row[:, :], mrow[:, sl],
                         start=True, stop=True)
    for g in range(NT):
        with nc.Block() as block:
            @block.sync
            def _(sync):
                s_y.inc(sync.dma_start(out=stage[g % 2][:, 1:T + 1],
                                       in_=ymid[g * 128:(g + 1) * 128, :]), 16)
            @block.vector
            def _(v):
                s_y.wait(v)
                sh = 0 if g >= 4 else 1
                v.tensor_tensor(tmp[:, :], stage[g % 2][:, sh:sh + T],
                                psb0[:, :], ALU.mult)
                v.scalar_tensor_tensor(tmp[:, :], tmp[:, :], 1.0,
                                       psb1[:, :], ALU.mult, ALU.add)
                v.tensor_scalar(xf[g][:, :], tmp[:, :], lng_s[:, g:g + 1],
                                lng_s[:, 8 + g:9 + g], ALU.mult, ALU.add)

    def load_w1(eng, bufa, bufg, mi):
        off, wd = A_CH[mi]
        for k in range(NT):
            s_w.inc(eng.dma_start(
                out=bufa[:, k * 128:k * 128 + wd],
                in_=W1[k * 128:(k + 1) * 128, off:off + wd]), 16)
            s_w.inc(eng.dma_start(
                out=bufg[:, k * 128:k * 128 + wd],
                in_=W1[k * 128:(k + 1) * 128, 2730 + off:2730 + off + wd]), 16)
    s_mmH = p.sem()
    s_glH = p.sem()
    s_evH = p.sem()
    bw0 = s_w.v
    bmm, bgl, bev = s_mmH.v, s_glH.v, s_evH.v
    with nc.Block() as block:
        @block.gpsimd
        def _(gp):
            load_w1(gp, wcol[0], wcg[0], 0)
    with nc.Block() as block:
        @block.gpsimd
        def _(gp):
            for mi in range(22):
                gp.wait_ge(s_mmH.h, bmm + max(0, 8 * mi))
                if mi < 21:
                    load_w1(gp, wcol[(mi + 1) % 2], wcg[(mi + 1) % 2], mi + 1)
        @block.tensor
        def _(t):
            for mi in range(22):
                off, wd = A_CH[mi]
                bufa, bufg = wcol[mi % 2], wcg[mi % 2]
                t.wait_ge(s_w.h, bw0 + 16 * 16 * (mi + 1))
                for ch in range(NCH):
                    C = mi * 4 + ch
                    sl = slice(ch * 512, (ch + 1) * 512)
                    t.wait_ge(s_evH.h, bev + max(0, C - 3))
                    for k in range(NT):
                        ins = t.matmul(psb0[0:wd, sl],
                                       bufa[:, k * 128:k * 128 + wd],
                                       xf[k][:, sl], start=(k == 0),
                                       stop=(k == NT - 1))
                    ins.then_inc(s_mmH.h, 1)
                    for k in range(NT):
                        ins = t.matmul(psb1[0:wd, sl],
                                       bufg[:, k * 128:k * 128 + wd],
                                       xf[k][:, sl], start=(k == 0),
                                       stop=(k == NT - 1))
                    ins.then_inc(s_mmH.h, 1)
        @block.scalar
        def _(s):
            for mi in range(22):
                off, wd = A_CH[mi]
                for ch in range(NCH):
                    C = mi * 4 + ch
                    sl = slice(ch * 512, (ch + 1) * 512)
                    s.wait_ge(s_mmH.h, bmm + 2 * C + 2)
                    s.wait_ge(s_evH.h, bev + max(0, C - 3))
                    s.activation(tmp[0:wd, sl], psb1[0:wd, sl], AF.Gelu,
                                 bias=b1g_s[0:wd, mi:mi + 1]
                                 ).then_inc(s_glH.h, 1)
        @block.vector
        def _(v):
            for mi in range(22):
                off, wd = A_CH[mi]
                for ch in range(NCH):
                    C = mi * 4 + ch
                    sl = slice(ch * 512, (ch + 1) * 512)
                    v.wait_ge(s_glH.h, bgl + C + 1)
                    v.scalar_tensor_tensor(hb[mi][0:wd, sl], psb0[0:wd, sl],
                                           b1a_s[0:wd, mi:mi + 1], tmp[0:wd, sl],
                                           ALU.add, ALU.mult
                                           ).then_inc(s_evH.h, 1)
    s_mmH.v += 22 * 8
    s_glH.v += 22 * 4
    s_evH.v += 22 * 4

    def load_w2(eng, buf, m):
        for ki in range(22):
            off, wd = A_CH[ki]
            s_w.inc(eng.dma_start(
                out=buf[:, ki * 128:(ki + 1) * 128][0:wd, :],
                in_=W2[off:off + wd, m * 128:(m + 1) * 128]), 16)
    so = p.sem()
    s_mmI = p.sem()
    s_evI = p.sem()
    bw0 = s_w.v
    bsy = s_y.v
    bmm, bev, bso = s_mmI.v, s_evI.v, so.v
    with nc.Block() as block:
        @block.gpsimd
        def _(gp):
            load_w2(gp, w2c[0], 0)
    with nc.Block() as block:
        @block.gpsimd
        def _(gp):
            for m in range(NT):
                gp.wait_ge(s_mmI.h, bmm + max(0, 4 * m))
                if m < NT - 1:
                    load_w2(gp, w2c[(m + 1) % 2], m + 1)
        @block.sync
        def _(sync):
            for m in range(min(2, NT)):
                s_y.inc(sync.dma_start(out=stage[m % 2][:, 1:T + 1],
                                       in_=ymid[m * 128:(m + 1) * 128, :]), 16)
            for m in range(NT):
                sync.wait_ge(s_evI.h, bev + 4 * (m + 1))
                so.inc(sync.dma_start(out=oT[m * 128:(m + 1) * 128, :],
                                      in_=stage[m % 2][:, 1:T + 1]), 16)
                if m + 2 < NT:
                    s_y.inc(sync.dma_start(
                        out=stage[m % 2][:, 1:T + 1],
                        in_=ymid[(m + 2) * 128:(m + 3) * 128, :]), 16)
        @block.tensor
        def _(t):
            for m in range(NT):
                buf = w2c[m % 2]
                t.wait_ge(s_w.h, bw0 + 16 * 22 * (m + 1))
                for ch in range(NCH):
                    C = m * 4 + ch
                    sl = slice(ch * 512, (ch + 1) * 512)
                    t.wait_ge(s_evI.h, bev + max(0, C - 3))
                    for ki in range(22):
                        off, wd = A_CH[ki]
                        ins = t.matmul(psb0[:, sl],
                                       buf[:, ki * 128:(ki + 1) * 128][0:wd, :],
                                       hb[ki][0:wd, sl],
                                       start=(ki == 0), stop=(ki == 21))
                    ins.then_inc(s_mmI.h, 1)
        @block.vector
        def _(v):
            for m in range(NT):
                v.wait_ge(s_y.h, bsy + 16 * (m + 1))
                for ch in range(NCH):
                    C = m * 4 + ch
                    sl = slice(ch * 512, (ch + 1) * 512)
                    v.wait_ge(s_mmI.h, bmm + C + 1)
                    v.scalar_tensor_tensor(
                        stage[m % 2][:, 1 + ch * 512:1 + (ch + 1) * 512],
                        psb0[:, sl], b2_s[:, m:m + 1],
                        stage[m % 2][:, 1 + ch * 512:1 + (ch + 1) * 512],
                        ALU.add, ALU.add).then_inc(s_evI.h, 1)
    s_mmI.v += 32
    s_evI.v += 32
    with nc.Block() as block:
        @block.sync
        def _(sync):
            so.wait(sync)


PERM64 = np.empty(64, np.int64)
PERM64[0::2] = np.arange(32)
PERM64[1::2] = np.arange(32, 64)


def rotary_tables():
    i = np.arange(32, dtype=np.float64)
    inv_freq = 1.0 / 10000.0 ** (2 * i / 64.0)
    t = np.arange(T, dtype=np.float64)
    f = t[:, None] * inv_freq[None, :]
    base = (2 * i + 0.4 * 64) / (1.4 * 64)
    power = (t - T // 2) / 128.0
    scale = base[None, :] ** power[:, None]
    inv = 1.0 / scale
    CQ = np.empty((T, 64)); SQ = np.empty((T, 64))
    CK = np.empty((T, 64)); SK = np.empty((T, 64))
    CQ[:, 0::2] = np.cos(f) * scale; CQ[:, 1::2] = np.cos(f) * scale
    SQ[:, 0::2] = -np.sin(f) * scale; SQ[:, 1::2] = np.sin(f) * scale
    CK[:, 0::2] = np.cos(f) * inv; CK[:, 1::2] = np.cos(f) * inv
    SK[:, 0::2] = -np.sin(f) * inv; SK[:, 1::2] = np.sin(f) * inv
    out = []
    for Z in (CQ, SQ, CK, SK):
        Z2 = np.tile(Z, (1, 2))
        out.append(_f32c(Z2.T))
    return out


def inner_mask():
    M = np.zeros((128, 384), np.float32)
    ii = np.arange(128)[:, None]
    jj = np.arange(128)[None, :]
    M[:, 0:128] = np.where(jj < ii, -1e9, 0.0)
    M[:, 256:384] = np.where(jj > ii, -1e9, 0.0)
    return M


def inner_consts():
    bd = np.zeros((128, 128), np.float32)
    for g in range(8):
        for hh in range(16):
            if hh // 2 == g:
                pb = 64 * (hh % 2)
                bd[pb:pb + 64, g * 16 + hh] = 1.0
    eg = np.zeros((16, 8 * 128), np.float32)
    for g in range(8):
        for pp in range(128):
            eg[2 * g + pp // 64, g * 128 + pp] = 1.0
    CQ, SQ, CK, SK = rotary_tables()
    return dict(bd_c=bd, eg_c=eg, id_c=np.eye(128, dtype=np.float32),
                CQ_c=CQ, SQ_c=SQ, CK_c=CK, SK_c=SK, M2_c=inner_mask())


def inner_inputs(yT_c, li, ins, consts):
    Wqkv = np.asarray(ins["L0_ffn_Wqkv"][li], np.float64).copy()
    colperm = np.concatenate([h * 64 + PERM64 for h in range(16)])
    Wp = Wqkv.copy()
    Wp[:, 0:1024] = Wqkv[:, 0:1024][:, colperm]
    Wp[:, 1024:2048] = Wqkv[:, 1024:2048][:, colperm]
    Wo = np.asarray(ins["L0_ffn_Wo"][li], np.float64)
    qs = np.asarray(ins["L0_ffn_q_scale"][li])[PERM64]
    ks = np.asarray(ins["L0_ffn_k_scale"][li])[PERM64]
    qsc = np.stack([np.tile(qs, 2), np.tile(ks, 2)], axis=1)
    lngc = np.concatenate(
        [np.asarray(ins["L0_ffn_ln_g"][li]).reshape(8, 128).T,
         np.asarray(ins["L0_ffn_ln_b"][li]).reshape(8, 128).T], axis=1)
    b1 = np.asarray(ins["L0_ffn_b1"][li])
    b1a = np.zeros((128, 22), np.float32)
    b1g = np.zeros((128, 22), np.float32)
    for mi, (off, wd) in enumerate(A_CH):
        b1a[0:wd, mi] = b1[off:off + wd]
        b1g[0:wd, mi] = b1[2730 + off:2730 + off + wd]
    return {
        "yT": yT_c,
        "Wqkv": _f32c(Wp),
        "Wop": _f32c(Wo),
        "qs_c": _f32c(qsc),
        "lng_c": _f32c(lngc),
        "W1": _f32c(ins["L0_ffn_W1"][li]),
        "b1a": b1a, "b1g": b1g,
        "W2": _f32c(ins["L0_ffn_W2"][li]),
        "b2_c": _f32c(np.asarray(ins["L0_ffn_b2"][li]).reshape(8, 128).T),
        **consts,
    }



LINQKV_INS = ["xT", "WqT", "WkT", "WvT", "bq_c", "bk_c", "bv_c", "beta_e"]
LINQKV_SHAPES = dict(xT=[D, T], WqT=[D, D], WkT=[D, D], WvT=[D, D],
                     bq_c=[128, 8], bk_c=[128, 8], bv_c=[128, 8],
                     beta_e=[128, 8])


def build_lin_qkv():
    p = Prog()
    io = {n: p.din(n, LINQKV_SHAPES[n]) for n in LINQKV_INS}
    io["qo"] = p.dout("qo", [D, T], BF16)
    io["ko"] = p.dout("ko", [D, T], BF16)
    io["vo"] = p.dout("vo", [D, T], BF16)
    emit_lin_qkv(p, io)
    return p


def emit_lin_qkv(p, io):
    nc = p.nc
    (xT, WqT, WkT, WvT, bq_c, bk_c, bv_c, beta_e) = [io[n] for n in LINQKV_INS]
    qo = io["qo"]; ko = io["ko"]; vo = io["vo"]

    pool = p.sb("pool", [128, 4 * T], F32)
    poolb = pool[:, :].bitcast(BF16)
    xhat = [poolb[:, g * T:(g + 1) * T] for g in range(NT)]
    tmp = p.sb("tmp", [128, T], F32)
    t2 = p.sb("t2", [128, T], F32)
    rows = p.sb("rows", [1, 2 * T], F32)
    ones_row = p.sb("ones_row", [1, 128], F32)
    ones_col = p.sb("ones_col", [128, 1], F32)
    wbuf = [p.sb("wb%d" % k, [128, D], BF16) for k in range(NT)]
    scol = p.sb("scol", [128, 8], F32)
    rcol = p.sb("rcol", [128, 8], F32)
    bq_s = p.sb("bq_s", [128, 8], F32)
    bk_s = p.sb("bk_s", [128, 8], F32)
    bv_s = p.sb("bv_s", [128, 8], F32)
    brq = p.sb("brq", [128, 8], F32)
    brk = p.sb("brk", [128, 8], F32)
    outb = p.sb("outb", [128, T], BF16)
    beta_sb = p.sb("beta_sb", [128, 8], F32)
    xres = [pool[:, 0:T], pool[:, T:2 * T]]

    LN8 = float(np.log(8.0))
    p.add_consts([EPS, LN8])
    psb0 = p.ps("psb0", [128, T])
    psb1 = p.ps("psb1", [128, T])

    lds = p.sem()
    emit_loads(p, lds,
               plain=[(beta_sb[:, :], beta_e), (bq_s[:, :], bq_c),
                      (bk_s[:, :], bk_c), (bv_s[:, :], bv_c)])
    with nc.Block() as block:
        @block.gpsimd
        def _(gp):
            gp.memset(ones_row[:, :], 1.0)
            gp.memset(ones_col[:, :], 1.0)
            p.emit_const_init(gp)
        @block.scalar
        def _(s):
            lds.wait(s)
            s.activation(scol[:, :], beta_sb[:, :], AF.Exp, bias=LN8)
    with nc.Block() as block:
        @block.vector
        def _(v):
            v.reciprocal(rcol[:, :], scol[:, :])
    with nc.Block() as block:
        @block.vector
        def _(v):
            v.tensor_mul(brq[:, :], bq_s[:, :], rcol[:, :])
            v.tensor_mul(brk[:, :], bk_s[:, :], rcol[:, :])

    s_x = p.sem()
    for g in range(NT):
        b = g % 2
        with nc.Block() as block:
            @block.sync
            def _(sync):
                s_x.inc(sync.dma_start(out=xres[b][:, :],
                                       in_=xT[g * 128:(g + 1) * 128, :]), 16)
            @block.scalar
            def _(s):
                s_x.wait(s)
                s.activation(tmp[:, :], xres[b][:, :], AF.Square)
        with nc.Block() as block:
            @block.tensor
            def _(t):
                for ch in range(NCH):
                    sl = slice(ch * 512, (ch + 1) * 512)
                    t.matmul(psb0[0:1, sl], ones_col[:, :], xres[b][:, sl],
                             start=(g == 0), stop=(g == NT - 1))
                    t.matmul(psb1[0:1, sl], ones_col[:, :], tmp[:, sl],
                             start=(g == 0), stop=(g == NT - 1))
    mrow = rows[0:1, 0:T]
    vrow = rows[0:1, T:2 * T]
    with nc.Block() as block:
        @block.vector
        def _(v):
            v.tensor_scalar_mul(mrow, psb0[0:1, :], 1.0 / D)
            v.tensor_scalar_mul(vrow, psb1[0:1, :], 1.0 / D)
            v.tensor_tensor(tmp[0:1, 0:T], mrow, mrow, ALU.mult)
            v.tensor_sub(vrow, vrow, tmp[0:1, 0:T])
    with nc.Block() as block:
        @block.scalar
        def _(s):
            s.activation(vrow, vrow, AF.Sqrt, bias=EPS)
    with nc.Block() as block:
        @block.vector
        def _(v):
            v.reciprocal(vrow, vrow)
            v.scalar_tensor_tensor(mrow, mrow, -1.0, vrow, ALU.mult, ALU.mult)
    with nc.Block() as block:
        @block.tensor
        def _(t):
            for ch in range(NCH):
                sl = slice(ch * 512, (ch + 1) * 512)
                t.matmul(psb0[:, sl], ones_row[:, :], vrow[:, sl],
                         start=True, stop=True)
                t.matmul(psb1[:, sl], ones_row[:, :], mrow[:, sl],
                         start=True, stop=True)
    for g in range(NT):
        b = g % 2
        with nc.Block() as block:
            @block.sync
            def _(sync):
                s_x.inc(sync.dma_start(out=tmp[:, :],
                                       in_=xT[g * 128:(g + 1) * 128, :]), 16)
            @block.vector
            def _(v):
                s_x.wait(v)
                v.tensor_tensor(t2[:, :], tmp[:, :], psb0[:, :], ALU.mult)
                v.scalar_tensor_tensor(xhat[g][:, :], t2[:, :], 1.0,
                                       psb1[:, :], ALU.mult, ALU.add)

    s_w = p.sem()
    st = p.sem()
    for which, Wd, bc, br, outd in ((0, WqT, bq_s, brq, qo), (1, WkT, bk_s, brk, ko),
                                    (2, WvT, bv_s, None, vo)):
        with nc.Block() as block:
            @block.gpsimd
            def _(gp):
                for k in range(NT):
                    s_w.inc(gp.dma_start(out=wbuf[k][:, :],
                                         in_=Wd[k * 128:(k + 1) * 128, :]), 16)
            @block.tensor
            def _(t):
                s_w.wait(t)
        for m in range(NT):
            with nc.Block() as block:
                @block.tensor
                def _(t):
                    for ch in range(NCH):
                        sl = slice(ch * 512, (ch + 1) * 512)
                        for k in range(NT):
                            t.matmul(psb0[:, sl], wbuf[k][:, m * 128:(m + 1) * 128],
                                     xhat[k][:, sl], start=(k == 0),
                                     stop=(k == NT - 1))
            if which < 2:
                with nc.Block() as block:
                    @block.scalar
                    def _(s):
                        s.activation(tmp[:, :], psb0[:, :], AF.Abs,
                                     bias=br[:, m:m + 1], scale=rcol[:, m:m + 1])
                        s.activation(tmp[:, :], tmp[:, :], AF.Exp, scale=-1.0)
                        s.activation(tmp[:, :], tmp[:, :], AF.Ln, bias=1.0)
                        s.activation(t2[:, :], psb0[:, :], AF.Relu,
                                     bias=br[:, m:m + 1], scale=rcol[:, m:m + 1])
                with nc.Block() as block:
                    @block.vector
                    def _(v):
                        v.tensor_add(outb[:, :], tmp[:, :], t2[:, :])
            else:
                with nc.Block() as block:
                    @block.vector
                    def _(v):
                        v.tensor_scalar_add(outb[:, :], psb0[:, :], bc[:, m:m + 1])
            with nc.Block() as block:
                @block.sync
                def _(sync):
                    st.inc(sync.dma_start(out=outd[m * 128:(m + 1) * 128, :],
                                          in_=outb[:, :]), 16)
    with nc.Block() as block:
        @block.sync
        def _(sync):
            st.wait(sync)
    return p


def build_lin_main():
    p = Prog()
    nc = p.nc
    io = dict(
        xT=p.din("xT", [D, T]),
        qtb=nc.dram_tensor("qtb", [D, T], BF16, kind="ExternalInput").ap(),
        kall=nc.dram_tensor("kall", [8 * D, T], BF16, kind="ExternalInput").ap(),
        vall=nc.dram_tensor("vall", [8 * D, T], BF16, kind="ExternalInput").ap(),
        WoT=p.din("WoT", [D, D]), bo_c=p.din("bo_c", [128, 8]),
        mask_c=p.din("mask_c", [16, 8]), bd_c=p.din("bd_c", [128, 128]),
        eg_c=p.din("eg_c", [16, 8 * 128]), oT=p.dout("oT", [D, T]))
    emit_lin_main(p, io, kvstride=D)
    return p


def emit_lin_main(p, io, kvstride=D):
    nc = p.nc
    xT = io["xT"]; qtb = io["qtb"]; kall = io["kall"]; vall = io["vall"]
    WoT = io["WoT"]; bo_c = io["bo_c"]; mask_c = io["mask_c"]
    bd_c = io["bd_c"]; eg_c = io["eg_c"]; oT = io["oT"]

    qt = [p.sb("qt%d" % g, [128, T], BF16) for g in range(NT)]
    kt = [p.sb("kt%d" % g, [128, T], BF16) for g in range(NT)]
    vt = [p.sb("vt%d" % g, [128, T], BF16) for g in range(NT)]
    num = [p.sb("num%d" % g, [128, T], F32) for g in range(NT)]
    den = p.sb("den", [16, T], F32)
    csr = p.sb("csr", [16, T], F32)
    csb = p.sb("csb", [16, T], BF16)
    pb = p.sb("pb", [128, T], BF16)
    tmp = p.sb("tmp", [128, T], F32)
    bdb = p.sb("bdb", [128, 128], BF16)
    eg_s = p.sb("eg_s", [16, 8 * 128], BF16)
    mask_s = p.sb("mask_s", [16, 8], F32)
    bo_s = p.sb("bo_s", [128, 8], F32)
    wbuf2 = [p.sb("w2%d" % b, [128, 1024], BF16) for b in range(2)]
    outb = p.sb("outb", [128, T], BF16)

    psb0 = p.ps("psb0", [128, T])
    psb1 = p.ps("psb1", [128, T])
    psst = psb1[0:16, :]

    lds = p.sem()
    emit_loads(
        p, lds,
        plain=[(mask_s[:, :], mask_c), (bo_s[:, :], bo_c)]
        + [(qt[g][:, :], qtb[g * 128:(g + 1) * 128, :]) for g in range(NT)],
        cast=[(bdb[:, :], bd_c), (eg_s[:, :], eg_c)],
    )
    with nc.Block() as block:
        @block.gpsimd
        def _(gp):
            gp.memset(den[:, :], 0.0)
            for g in range(NT):
                gp.memset(num[g][:, :], 0.0)
        @block.vector
        def _(v):
            lds.wait(v)

    s_k = p.sem()
    for s_idx in range(8):
        with nc.Block() as block:
            @block.sync
            def _(sync):
                for g in range(NT):
                    s_k.inc(sync.dma_start(
                        out=kt[g][:, :],
                        in_=kall[s_idx * kvstride + g * 128:
                                 s_idx * kvstride + (g + 1) * 128, :]), 16)
                    s_k.inc(sync.dma_start(
                        out=vt[g][:, :],
                        in_=vall[s_idx * kvstride + g * 128:
                                 s_idx * kvstride + (g + 1) * 128, :]), 16)
        for g in range(NT):
            with nc.Block() as block:
                @block.vector
                def _(v):
                    if g == 0:
                        s_k.wait(v)
                    v.tensor_tensor(pb[:, :], qt[g][:, :], kt[g][:, :], ALU.mult)
            with nc.Block() as block:
                @block.tensor
                def _(t):
                    for ch in range(NCH):
                        sl = slice(ch * 512, (ch + 1) * 512)
                        t.matmul(psst[:, sl], bdb[:, g * 16:(g + 1) * 16],
                                 pb[:, sl], start=(g == 0), stop=(g == NT - 1))
        with nc.Block() as block:
            @block.vector
            def _(v):
                v.tensor_scalar_mul(csr[:, :], psst[:, :],
                                    mask_s[:, s_idx:s_idx + 1])
        with nc.Block() as block:
            @block.vector
            def _(v):
                v.tensor_add(den[:, :], den[:, :], csr[:, :])
                v.tensor_copy(csb[:, :], csr[:, :])
        for g in range(NT):
            with nc.Block() as block:
                @block.tensor
                def _(t):
                    for ch in range(NCH):
                        sl = slice(ch * 512, (ch + 1) * 512)
                        t.matmul(psb0[:, sl], eg_s[:, g * 128:(g + 1) * 128],
                                 csb[:, sl], start=True, stop=True)
            with nc.Block() as block:
                @block.vector
                def _(v):
                    v.tensor_tensor(tmp[:, :], psb0[:, :], vt[g][:, :], ALU.mult)
                    v.tensor_add(num[g][:, :], num[g][:, :], tmp[:, :])
    with nc.Block() as block:
        @block.vector
        def _(v):
            v.reciprocal(csr[:, :], den[:, :])
    with nc.Block() as block:
        @block.vector
        def _(v):
            v.tensor_copy(csb[:, :], csr[:, :])
    for g in range(NT):
        with nc.Block() as block:
            @block.tensor
            def _(t):
                for ch in range(NCH):
                    sl = slice(ch * 512, (ch + 1) * 512)
                    t.matmul(psb0[:, sl], eg_s[:, g * 128:(g + 1) * 128],
                             csb[:, sl], start=True, stop=True)
        with nc.Block() as block:
            @block.vector
            def _(v):
                v.tensor_tensor(qt[g][:, :], num[g][:, :], psb0[:, :], ALU.mult)
    s_w = p.sem()
    s_y = p.sem()
    st = p.sem()
    for m in range(NT):
        wv = s_w.v
        with nc.Block() as block:
            @block.gpsimd
            def _(gp):
                for k in range(NT):
                    s_w.inc(gp.dma_start(
                        out=wbuf2[m % 2][:, k * 128:(k + 1) * 128],
                        in_=WoT[k * 128:(k + 1) * 128, m * 128:(m + 1) * 128]), 16)
            @block.sync
            def _(sync):
                s_y.inc(sync.dma_start(out=tmp[:, :],
                                       in_=xT[m * 128:(m + 1) * 128, :]), 16)
        with nc.Block() as block:
            @block.tensor
            def _(t):
                t.wait_ge(s_w.h, s_w.v)
                for ch in range(NCH):
                    sl = slice(ch * 512, (ch + 1) * 512)
                    for k in range(NT):
                        t.matmul(psb0[:, sl], wbuf2[m % 2][:, k * 128:(k + 1) * 128],
                                 qt[k][:, sl], start=(k == 0), stop=(k == NT - 1))
        with nc.Block() as block:
            @block.vector
            def _(v):
                s_y.wait(v)
                v.scalar_tensor_tensor(tmp[:, :], psb0[:, :], bo_s[:, m:m + 1],
                                       tmp[:, :], ALU.add, ALU.add)
        with nc.Block() as block:
            @block.sync
            def _(sync):
                st.inc(sync.dma_start(out=oT[m * 128:(m + 1) * 128, :],
                                      in_=tmp[:, :]), 16)
    with nc.Block() as block:
        @block.sync
        def _(sync):
            st.wait(sync)
    return p



def emit_quant_chain(p, nr, wcols, wt_ap, axn, q1, q3, scl, rscl, out_ap):
    nc = p.nc
    nb = wcols // 16
    A = lambda t: t[0:nr, 0:wcols]
    B = lambda t: t[0:nr, 0:nb]
    blk = lambda t: t[0:nr, 0:wcols].rearrange("p (b u) -> p b u", u=16)
    sbc = lambda t: t[0:nr, 0:nb].unsqueeze(2).broadcast_to([nr, nb, 16])
    with nc.Block() as block:
        @block.vector
        def _(v):
            v.reduce_max(B(scl), wt_ap.rearrange("p (b u) -> p b u", u=16),
                         axis=AX.X, apply_absolute_value=True)
    with nc.Block() as block:
        @block.vector
        def _(v):
            v.tensor_scalar(B(scl), B(scl), 1.0 / 6.0, 1e-12, ALU.mult, ALU.max)
    with nc.Block() as block:
        @block.vector
        def _(v):
            v.reciprocal(B(rscl), B(scl))
    with nc.Block() as block:
        @block.vector
        def _(v):
            v.tensor_tensor(A(q3).rearrange("p (b u) -> p b u", u=16),
                            wt_ap.rearrange("p (b u) -> p b u", u=16),
                            sbc(rscl), ALU.mult)
    with nc.Block() as block:
        @block.scalar
        def _(s):
            s.activation(A(axn), A(q3), AF.Abs)
            s.activation(A(q3), A(q3), AF.Sign)
    with nc.Block() as block:
        @block.vector
        def _(v):
            a = A(axn)
            v.tensor_single_scalar(A(q1), a, 0.25, ALU.is_gt)
            v.scalar_tensor_tensor(A(q1), a, 0.75, A(q1), ALU.is_gt, ALU.add)
            v.scalar_tensor_tensor(A(q1), a, 1.25, A(q1), ALU.is_gt, ALU.add)
            v.scalar_tensor_tensor(A(q1), a, 1.75, A(q1), ALU.is_gt, ALU.add)
            v.tensor_scalar_mul(A(q1), A(q1), 0.5)
            v.scalar_tensor_tensor(A(q1), a, 2.5, A(q1), ALU.is_gt, ALU.add)
            v.scalar_tensor_tensor(A(q1), a, 3.5, A(q1), ALU.is_gt, ALU.add)
            v.scalar_tensor_tensor(A(q1), a, 5.0, A(q1), ALU.is_gt, ALU.add)
            v.scalar_tensor_tensor(A(q1), a, 5.0, A(q1), ALU.is_gt, ALU.add)
    with nc.Block() as block:
        @block.vector
        def _(v):
            v.tensor_tensor(A(q1), A(q1), A(q3), ALU.mult)
    with nc.Block() as block:
        @block.vector
        def _(v):
            v.tensor_tensor(out_ap.rearrange("p (b u) -> p b u", u=16),
                            blk(A(q1)), sbc(scl), ALU.mult)


def build_wquant():
    p = Prog()
    io = dict(wg=p.din("wg", [2816, D]), wu=p.din("wu", [2816, D]),
              wd=p.din("wd", [D, 2816]), id_c=p.din("id_c", [128, 128]),
              wgo=p.dout("wgo", [D, 2816], BF16),
              wuo=p.dout("wuo", [D, 2816], BF16),
              wdo=p.dout("wdo", [2816, D], BF16))
    emit_wquant(p, io)
    return p


def emit_wquant(p, io):
    nc = p.nc
    wg = io["wg"]; wu = io["wu"]; wd = io["wd"]; id_c = io["id_c"]
    wgo = io["wgo"]; wuo = io["wuo"]; wdo = io["wdo"]

    wt = p.sb("wt", [128, 2816], F32)
    axn = p.sb("axn", [128, 2816], F32)
    q1 = p.sb("q1", [128, 2816], F32)
    q3 = p.sb("q3", [128, 2816], F32)
    scl = p.sb("scl", [128, 176], F32)
    rscl = p.sb("rscl", [128, 176], F32)
    wq = p.sb("wq", [128, 2816], BF16)
    evb = p.sb("evb", [128, 128], BF16)
    idb = p.sb("idb", [128, 128], BF16)
    psb1w = p.ps("psb1", [128, T])
    pst = psb1w[:, :].bitcast(BF16)[:, 0:512]

    lds = p.sem()
    emit_loads(p, lds, cast=[(idb[:, :], id_c)])
    s_x = p.sem()
    st = p.sem()

    jobs = []
    for r in range(22):
        jobs.append((wg, wgo, r, 1024, False))
    for r in range(22):
        jobs.append((wu, wuo, r, 1024, False))
    for r in range(8):
        jobs.append((wd, wdo, r, 2816, True))
    for (srcd, dstd, r, wcols, is_wd) in jobs:
        with nc.Block() as block:
            @block.sync
            def _(sync):
                s_x.inc(sync.dma_start(out=wt[:, 0:wcols],
                                       in_=srcd[r * 128:(r + 1) * 128, :]), 16)
            @block.vector
            def _(v):
                s_x.wait(v)
        emit_quant_chain(p, 128, wcols, wt[:, 0:wcols], axn, q1, q3,
                         scl, rscl, wq[:, 0:wcols])
        ncc = wcols // 128
        for cc in range(ncc):
            with nc.Block() as block:
                @block.tensor
                def _(t):
                    t.transpose(pst[:, 0:128], wq[:, cc * 128:(cc + 1) * 128],
                                idb[:, :])
            with nc.Block() as block:
                @block.vector
                def _(v):
                    v.tensor_copy(evb[:, :], pst[:, 0:128])
            with nc.Block() as block:
                @block.sync
                def _(sync):
                    st.inc(sync.dma_start(
                        out=dstd[cc * 128:(cc + 1) * 128,
                                 r * 128:(r + 1) * 128],
                        in_=evb[:, :]), 16)
    with nc.Block() as block:
        @block.sync
        def _(sync):
            st.wait(sync)
    return p


def build_fp4ffn():
    p = Prog()
    nc = p.nc
    io = dict(
        xT=p.din("xT", [D, T]),
        wgq=nc.dram_tensor("wgq", [D, 2816], BF16, kind="ExternalInput").ap(),
        wuq=nc.dram_tensor("wuq", [D, 2816], BF16, kind="ExternalInput").ap(),
        wdq=nc.dram_tensor("wdq", [2816, D], BF16, kind="ExternalInput").ap(),
        id_c=p.din("id_c", [128, 128]), oT=p.dout("oT", [D, T]))
    emit_fp4ffn(p, io)
    return p


def emit_fp4ffn(p, io):
    nc = p.nc
    xT = io["xT"]; wgq = io["wgq"]; wuq = io["wuq"]; wdq = io["wdq"]
    id_c = io["id_c"]; oT = io["oT"]

    pool = p.sb("pool2", [128, 60 * 1024], BF16)
    htm = [pool[:, tc * 2816:(tc + 1) * 2816] for tc in range(16)]
    xq_off = 16 * 2816
    xqfm = [pool[:, xq_off + g * T:xq_off + (g + 1) * T] for g in range(NT)]
    qscr = p.sb("qscr", [128, 2 * 2816], F32)
    axn = p.sb("axnf", [128, 2816], F32)
    xf = [p.sb("xf%d" % b, [128, T], F32) for b in range(2)]
    tmp = p.sb("tmpf", [128, T], F32)
    mbs = p.sb("mbs", [128, T], F32)
    scl = p.sb("sclf", [128, 176], F32)
    rscl = p.sb("rsclf", [128, 176], F32)
    ones_row = p.sb("ones_row", [1, 128], F32)
    ones_col = p.sb("ones_col", [128, 1], F32)
    idf = p.sb("idf", [128, 128], F32)
    idb = p.sb("idb", [128, 128], BF16)
    wgr = p.sb("wgr", [128, 8 * 512], BF16)
    wur = p.sb("wur", [128, 8 * 512], BF16)
    xqt = None
    wdt = [pool[:, xq_off + ki * 1024:xq_off + (ki + 1) * 1024] for ki in range(16)] \
        + [qscr[:, :].bitcast(BF16)[:, ki * 1024:(ki + 1) * 1024] for ki in range(6)]

    p.add_consts([EPS])
    psb0 = p.ps("psb0", [128, T])
    psb1 = p.ps("psb1", [128, T])
    psbq = psb1[:, :].bitcast(BF16)[:, 0:T]

    q1s = qscr[:, 0:2816]
    q3s = qscr[:, 2816:2 * 2816]

    xqt = axn[:, 0:1024].bitcast(BF16)
    lds = p.sem()
    emit_loads(p, lds, plain=[(idf[:, :], id_c)], cast=[(idb[:, :], id_c)])
    with nc.Block() as block:
        @block.gpsimd
        def _(gp):
            gp.memset(ones_row[:, :], 1.0)
            gp.memset(ones_col[:, :], 1.0)
            p.emit_const_init(gp)
        @block.vector
        def _(v):
            lds.wait(v)

    s_x = p.sem()
    for g in range(NT):
        b = g % 2
        with nc.Block() as block:
            @block.sync
            def _(sync):
                s_x.inc(sync.dma_start(out=xf[b][:, :],
                                       in_=xT[g * 128:(g + 1) * 128, :]), 16)
            @block.scalar
            def _(s):
                s_x.wait(s)
                s.activation(tmp[:, :], xf[b][:, :], AF.Square)
        with nc.Block() as block:
            @block.tensor
            def _(t):
                for ch in range(NCH):
                    sl = slice(ch * 512, (ch + 1) * 512)
                    t.matmul(psb0[0:1, sl], ones_col[:, :], xf[b][:, sl],
                             start=(g == 0), stop=(g == NT - 1))
                    t.matmul(psb1[0:1, sl], ones_col[:, :], tmp[:, sl],
                             start=(g == 0), stop=(g == NT - 1))
    mrow = tmp[0:1, 0:T]
    vrow = mbs[0:1, 0:T]
    with nc.Block() as block:
        @block.vector
        def _(v):
            v.tensor_scalar_mul(mrow, psb0[0:1, :], 1.0 / D)
            v.tensor_scalar_mul(vrow, psb1[0:1, :], 1.0 / D)
            v.tensor_tensor(xf[0][0:1, 0:T], mrow, mrow, ALU.mult)
            v.tensor_sub(vrow, vrow, xf[0][0:1, 0:T])
    with nc.Block() as block:
        @block.scalar
        def _(s):
            s.activation(vrow, vrow, AF.Sqrt, bias=EPS)
    with nc.Block() as block:
        @block.vector
        def _(v):
            v.reciprocal(vrow, vrow)
            v.scalar_tensor_tensor(mrow, mrow, -1.0, vrow, ALU.mult, ALU.mult)
    with nc.Block() as block:
        @block.tensor
        def _(t):
            for ch in range(NCH):
                sl = slice(ch * 512, (ch + 1) * 512)
                t.matmul(psb0[:, sl], ones_row[:, :], vrow[:, sl],
                         start=True, stop=True)
                t.matmul(psb1[:, sl], ones_row[:, :], mrow[:, sl],
                         start=True, stop=True)
    with nc.Block() as block:
        @block.vector
        def _(v):
            v.tensor_copy(mbs[:, :], psb1[:, :])

    for g in range(NT):
        b = g % 2
        with nc.Block() as block:
            @block.sync
            def _(sync):
                s_x.inc(sync.dma_start(out=xf[b][:, :],
                                       in_=xT[g * 128:(g + 1) * 128, :]), 16)
            @block.vector
            def _(v):
                s_x.wait(v)
                v.tensor_tensor(tmp[:, :], xf[b][:, :], psb0[:, :], ALU.mult)
                v.tensor_add(xf[b][:, :], tmp[:, :], mbs[:, :])
        with nc.Block() as block:
            @block.tensor
            def _(t):
                for tc in range(16):
                    t.transpose(psb1[:, tc * 128:(tc + 1) * 128],
                                xf[b][:, tc * 128:(tc + 1) * 128], idf[:, :])
        emit_quant_chain(p, 128, T, psb1[:, :], axn, q1s, q3s,
                         scl, rscl, xqt[:, :])
        with nc.Block() as block:
            @block.tensor
            def _(t):
                for tc in range(16):
                    t.transpose(psbq[:, tc * 128:(tc + 1) * 128],
                                xqt[:, tc * 128:(tc + 1) * 128], idb[:, :])
        with nc.Block() as block:
            @block.vector
            def _(v):
                v.tensor_copy(xqfm[g][:, :], psbq[:, :])

    IC = [(i * 512, 512) for i in range(5)] + [(2560, 256)]
    s_w = p.sem()
    for ici, (ioff, icw) in enumerate(IC):
        with nc.Block() as block:
            @block.gpsimd
            def _(gp):
                for k in range(NT):
                    s_w.inc(gp.dma_start(out=wgr[:, k * 512:k * 512 + icw],
                                         in_=wgq[k * 128:(k + 1) * 128,
                                                 ioff:ioff + icw]), 16)
                    s_w.inc(gp.dma_start(out=wur[:, k * 512:k * 512 + icw],
                                         in_=wuq[k * 128:(k + 1) * 128,
                                                 ioff:ioff + icw]), 16)
        for tc in range(16):
            with nc.Block() as block:
                @block.tensor
                def _(t):
                    t.wait_ge(s_w.h, s_w.v)
                    for k in range(NT):
                        t.matmul(psb0[:, 0:icw],
                                 xqfm[k][:, tc * 128:(tc + 1) * 128],
                                 wgr[:, k * 512:k * 512 + icw],
                                 start=(k == 0), stop=(k == NT - 1))
                    for k in range(NT):
                        t.matmul(psb1[:, 0:icw],
                                 xqfm[k][:, tc * 128:(tc + 1) * 128],
                                 wur[:, k * 512:k * 512 + icw],
                                 start=(k == 0), stop=(k == NT - 1))
            with nc.Block() as block:
                @block.scalar
                def _(s):
                    s.activation(tmp[:, 0:icw], psb0[:, 0:icw], AF.Silu)
            with nc.Block() as block:
                @block.vector
                def _(v):
                    v.tensor_tensor(q1s[:, 0:icw], psb1[:, 0:icw],
                                    tmp[:, 0:icw], ALU.mult)
            emit_quant_chain(p, 128, icw, q1s[:, 0:icw], axn, q1s, q3s,
                             scl, rscl, htm[tc][:, ioff:ioff + icw])

    with nc.Block() as block:
        @block.gpsimd
        def _(gp):
            for ki in range(22):
                s_w.inc(gp.dma_start(out=wdt[ki][:, :],
                                     in_=wdq[ki * 128:(ki + 1) * 128, :]), 16)
    for ki in range(22):
        with nc.Block() as block:
            @block.tensor
            def _(t):
                for tc in range(16):
                    t.transpose(psbq[:, tc * 128:(tc + 1) * 128],
                                htm[tc][:, ki * 128:(ki + 1) * 128], idb[:, :])
        with nc.Block() as block:
            @block.vector
            def _(v):
                for tc in range(16):
                    v.tensor_copy(htm[tc][:, ki * 128:(ki + 1) * 128],
                                  psbq[:, tc * 128:(tc + 1) * 128])
    st = p.sem()
    for tch in range(4):
        with nc.Block() as block:
            @block.tensor
            def _(t):
                t.wait_ge(s_w.h, s_w.v)
                for ki in range(22):
                    for m in range(NT):
                        ps = psb0 if m < 4 else psb1
                        rhs = pool[:, 0:16 * 2816].rearrange(
                            "p (c w) -> p c w", w=2816)[
                            :, tch * 4:(tch + 1) * 4, ki * 128:(ki + 1) * 128]
                        t.matmul(ps[:, (m % 4) * 512:(m % 4) * 512 + 512],
                                 wdt[ki][:, m * 128:(m + 1) * 128],
                                 rhs, start=(ki == 0), stop=(ki == 21))
        for m in range(NT):
            b = m % 2
            with nc.Block() as block:
                @block.sync
                def _(sync):
                    s_x.inc(sync.dma_start(
                        out=xf[b][:, 0:512],
                        in_=xT[m * 128:(m + 1) * 128,
                               tch * 512:(tch + 1) * 512]), 16)
                @block.vector
                def _(v):
                    s_x.wait(v)
                    ps = psb0 if m < 4 else psb1
                    v.tensor_tensor(xf[b][:, 0:512], xf[b][:, 0:512],
                                    ps[:, (m % 4) * 512:(m % 4) * 512 + 512],
                                    ALU.add)
            with nc.Block() as block:
                @block.sync
                def _(sync):
                    st.inc(sync.dma_start(
                        out=oT[m * 128:(m + 1) * 128,
                               tch * 512:(tch + 1) * 512],
                        in_=xf[b][:, 0:512]), 16)
    with nc.Block() as block:
        @block.sync
        def _(sync):
            st.wait(sync)
    return p



HM_SHAPES = dict(xT=[D, T], Wq=[D, D], Wk=[D, D], Wv=[D, D], Wo=[D, D],
                 bo_c=[128, 8], i64_c=[64, D], id_c=[128, 128])


def build_headmix():
    p = Prog()
    io = {n: p.din(n, s) for n, s in HM_SHAPES.items()}
    io["oT"] = p.dout("oT", [D, T])
    emit_headmix(p, io)
    return p


def emit_headmix(p, io):
    nc = p.nc
    xT = io["xT"]; Wq = io["Wq"]; Wk = io["Wk"]; Wv = io["Wv"]; Wo = io["Wo"]
    bo_c = io["bo_c"]; i64_c = io["i64_c"]; id_c = io["id_c"]; oT = io["oT"]
    vs3 = p.dram("vs3", [D, T], BF16)
    T1b = p.dram("T1b", [D, T], F32)
    A2 = p.dram("A2", [1, 256], BF16)

    pool = p.sb("pool", [128, 4 * T], F32)
    poolb = pool[:, :].bitcast(BF16)
    xhat = [poolb[:, g * T:(g + 1) * T] for g in range(NT)]
    xres = [pool[:, 0:T], pool[:, T:2 * T]]
    qtm = [p.sb("qtm%d" % u, [128, D], BF16) for u in range(16)]
    ktm = [p.sb("ktm%d" % u, [128, D], BF16) for u in range(16)]
    vx = [p.sb("vx%d" % g, [128, T], BF16) for g in range(NT)]
    wbuf = [p.sb("wb%d" % k, [128, D], BF16) for k in range(NT)]
    tmp = p.sb("tmp", [128, T], F32)
    rows = p.sb("rows", [1, 2 * T], F32)
    ens = p.sb("ens", [64, D], F32)
    i64s = p.sb("i64s", [64, D], F32)
    mst = p.sb("mst", [16, T], F32)
    vb = p.sb("vb", [16, T], BF16)
    ef = p.sb("ef", [1, 2 * 256], F32)
    e16 = p.sb("e16", [1, 64], F32)
    Ab = p.sb("Ab", [1, 256], BF16)
    As = p.sb("As", [16, 16], BF16)
    ones_row = p.sb("ones_row", [1, 128], F32)
    ones_col = p.sb("ones_col", [128, 1], F32)
    bo_s = p.sb("bo_s", [128, 8], F32)

    p.add_consts([EPS, 0.125])
    psb0 = p.ps("psb0", [128, T])
    psb1 = p.ps("psb1", [128, T])

    lds = p.sem()
    emit_loads(p, lds, plain=[(i64s[:, :], i64_c), (bo_s[:, :], bo_c)])
    with nc.Block() as block:
        @block.gpsimd
        def _(gp):
            gp.memset(ones_row[:, :], 1.0)
            gp.memset(ones_col[:, :], 1.0)
            p.emit_const_init(gp)
        @block.vector
        def _(v):
            lds.wait(v)

    s_x = p.sem()
    for g in range(NT):
        b = g % 2
        with nc.Block() as block:
            @block.sync
            def _(sync):
                s_x.inc(sync.dma_start(out=xres[b][:, :],
                                       in_=xT[g * 128:(g + 1) * 128, :]), 16)
            @block.scalar
            def _(s):
                s_x.wait(s)
                s.activation(tmp[:, :], xres[b][:, :], AF.Square)
        with nc.Block() as block:
            @block.tensor
            def _(t):
                for ch in range(NCH):
                    sl = slice(ch * 512, (ch + 1) * 512)
                    t.matmul(psb0[0:1, sl], ones_col[:, :], xres[b][:, sl],
                             start=(g == 0), stop=(g == NT - 1))
                    t.matmul(psb1[0:1, sl], ones_col[:, :], tmp[:, sl],
                             start=(g == 0), stop=(g == NT - 1))
    mrow = rows[0:1, 0:T]
    vrow = rows[0:1, T:2 * T]
    with nc.Block() as block:
        @block.vector
        def _(v):
            v.tensor_scalar_mul(mrow, psb0[0:1, :], 1.0 / D)
            v.tensor_scalar_mul(vrow, psb1[0:1, :], 1.0 / D)
            v.tensor_tensor(tmp[0:1, 0:T], mrow, mrow, ALU.mult)
            v.tensor_sub(vrow, vrow, tmp[0:1, 0:T])
    with nc.Block() as block:
        @block.scalar
        def _(s):
            s.activation(vrow, vrow, AF.Sqrt, bias=EPS)
    with nc.Block() as block:
        @block.vector
        def _(v):
            v.reciprocal(vrow, vrow)
            v.scalar_tensor_tensor(mrow, mrow, -1.0, vrow, ALU.mult, ALU.mult)
    with nc.Block() as block:
        @block.tensor
        def _(t):
            for ch in range(NCH):
                sl = slice(ch * 512, (ch + 1) * 512)
                t.matmul(psb0[:, sl], ones_row[:, :], vrow[:, sl],
                         start=True, stop=True)
                t.matmul(psb1[:, sl], ones_row[:, :], mrow[:, sl],
                         start=True, stop=True)
    for g in range(NT):
        b = g % 2
        with nc.Block() as block:
            @block.sync
            def _(sync):
                s_x.inc(sync.dma_start(out=tmp[:, :],
                                       in_=xT[g * 128:(g + 1) * 128, :]), 16)
            @block.vector
            def _(v):
                s_x.wait(v)
                v.tensor_tensor(tmp[:, :], tmp[:, :], psb0[:, :], ALU.mult)
                v.scalar_tensor_tensor(xhat[g][:, :], tmp[:, :], 1.0,
                                       psb1[:, :], ALU.mult, ALU.add)

    s_w = p.sem()
    st = p.sem()
    for which, Wd in ((0, Wq), (1, Wk), (2, Wv)):
        with nc.Block() as block:
            @block.gpsimd
            def _(gp):
                for k in range(NT):
                    s_w.inc(gp.dma_start(out=wbuf[k][:, :],
                                         in_=Wd[k * 128:(k + 1) * 128, :]), 16)
            @block.tensor
            def _(t):
                s_w.wait(t)
        if which < 2:
            dst = qtm if which == 0 else ktm
            for u in range(16):
                with nc.Block() as block:
                    @block.tensor
                    def _(t):
                        for oc in range(2):
                            sl = slice(oc * 512, (oc + 1) * 512)
                            for k in range(NT):
                                t.matmul(psb0[:, sl],
                                         xhat[k][:, u * 128:(u + 1) * 128],
                                         wbuf[k][:, sl], start=(k == 0),
                                         stop=(k == NT - 1))
                with nc.Block() as block:
                    @block.vector
                    def _(v):
                        v.tensor_copy(dst[u][:, :], psb0[:, 0:D])
        else:
            for m in range(NT):
                with nc.Block() as block:
                    @block.tensor
                    def _(t):
                        for ch in range(NCH):
                            sl = slice(ch * 512, (ch + 1) * 512)
                            for k in range(NT):
                                t.matmul(psb0[:, sl],
                                         wbuf[k][:, m * 128:(m + 1) * 128],
                                         xhat[k][:, sl], start=(k == 0),
                                         stop=(k == NT - 1))
                with nc.Block() as block:
                    @block.vector
                    def _(v):
                        v.tensor_copy(vx[m][:, :], psb0[:, :])
                with nc.Block() as block:
                    @block.sync
                    def _(sync):
                        st.inc(sync.dma_start(out=vs3[m * 128:(m + 1) * 128, :],
                                              in_=vx[m][:, :]), 16)

    for a in range(16):
        with nc.Block() as block:
            @block.tensor
            def _(t):
                for oc in range(2):
                    sl = slice(oc * 512, (oc + 1) * 512)
                    for u in range(16):
                        t.matmul(psb0[0:64, sl],
                                 qtm[u][:, 64 * a:64 * a + 64],
                                 ktm[u][:, sl], start=(u == 0), stop=(u == 15))
        with nc.Block() as block:
            @block.vector
            def _(v):
                v.tensor_copy(ens[:, :], psb0[0:64, 0:D])
        with nc.Block() as block:
            @block.vector
            def _(v):
                v.tensor_tensor(ens[:, :], ens[:, :], i64s[:, :], ALU.mult)
        with nc.Block() as block:
            @block.tensor
            def _(t):
                for oc in range(2):
                    sl = slice(oc * 512, (oc + 1) * 512)
                    t.matmul(psb1[0:1, sl], ones_col[0:64, :], ens[:, sl],
                             start=True, stop=True)
        with nc.Block() as block:
            @block.vector
            def _(v):
                v.reduce_sum(ef[0:1, 16 * a:16 * a + 16],
                             psb1[0:1, 0:D].rearrange("p (b u) -> p b u", u=64),
                             axis=AX.X)
    with nc.Block() as block:
        @block.vector
        def _(v):
            v.reduce_max(e16[0:1, 0:16],
                         ef[0:1, 0:256].rearrange("p (a b) -> p a b", b=16),
                         axis=AX.X)
    with nc.Block() as block:
        @block.vector
        def _(v):
            v.tensor_sub(ef[0:1, 256:512].rearrange("p (a b) -> p a b", b=16),
                         ef[0:1, 0:256].rearrange("p (a b) -> p a b", b=16),
                         e16[0:1, 0:16].unsqueeze(2).broadcast_to([1, 16, 16]))
    with nc.Block() as block:
        @block.scalar
        def _(s):
            s.activation(ef[0:1, 256:512], ef[0:1, 256:512], AF.Exp, scale=0.125)
    with nc.Block() as block:
        @block.vector
        def _(v):
            v.reduce_sum(e16[0:1, 16:32],
                         ef[0:1, 256:512].rearrange("p (a b) -> p a b", b=16),
                         axis=AX.X)
    with nc.Block() as block:
        @block.vector
        def _(v):
            v.reciprocal(e16[0:1, 32:48], e16[0:1, 16:32])
    with nc.Block() as block:
        @block.vector
        def _(v):
            v.tensor_tensor(Ab[0:1, 0:256].rearrange("p (a b) -> p a b", b=16),
                            ef[0:1, 256:512].rearrange("p (a b) -> p a b", b=16),
                            e16[0:1, 32:48].unsqueeze(2).broadcast_to([1, 16, 16]),
                            ALU.mult)
    with nc.Block() as block:
        @block.sync
        def _(sync):
            st.inc(sync.dma_start(out=A2[:, :], in_=Ab[0:1, :]), 16)
    with nc.Block() as block:
        @block.gpsimd
        def _(gp):
            st.inc(gp.dma_start(out=As[:, :],
                                in_=bass.AP(A2.tensor, 0, [[1, 16], [16, 16]]),
                                allow_slow_non_contiguous=True), 16)
        @block.tensor
        def _(t):
            st.wait(t)

    s_v = p.sem()
    for dd in range(64):
        with nc.Block() as block:
            @block.sync
            def _(sync):
                s_v.inc(sync.dma_start(
                    out=vb[:, :],
                    in_=bass.AP(vs3.tensor, dd * T,
                                [[64 * T, 16], [1, T]])), 16)
            @block.tensor
            def _(t):
                s_v.wait(t)
                for ch in range(NCH):
                    sl = slice(ch * 512, (ch + 1) * 512)
                    t.matmul(psb1[0:16, sl], As[:, :], vb[:, sl],
                             start=True, stop=True)
        with nc.Block() as block:
            @block.scalar
            def _(s):
                s.copy(mst[0:16, 0:T].rearrange("p (u w) -> p w u", u=16),
                       psb1[0:16, 0:T])
        with nc.Block() as block:
            @block.sync
            def _(sync):
                st.inc(sync.dma_start(
                    out=bass.AP(T1b.tensor, dd * T, [[64 * T, 16], [1, T]]),
                    in_=mst[0:16, :]), 16)
    with nc.Block() as block:
        @block.sync
        def _(sync):
            st.wait(sync)
    for jt in range(NT):
        with nc.Block() as block:
            @block.gpsimd
            def _(gp):
                for ui in range(2):
                    u = jt * 2 + ui
                    s_v.inc(gp.dma_start(
                        out=vx[jt][64 * ui:64 * ui + 64, :],
                        in_=bass.AP(T1b.tensor, u * 128,
                                    [[T, 64], [64 * T, 16], [1, 128]])), 16)
    with nc.Block() as block:
        @block.gpsimd
        def _(gp):
            for k in range(NT):
                s_w.inc(gp.dma_start(out=wbuf[k][:, :],
                                     in_=Wo[k * 128:(k + 1) * 128, :]), 16)
        @block.tensor
        def _(t):
            s_w.wait(t)
            s_v.wait(t)
    for m in range(NT):
        b = m % 2
        with nc.Block() as block:
            @block.sync
            def _(sync):
                s_x.inc(sync.dma_start(out=tmp[:, :],
                                       in_=xT[m * 128:(m + 1) * 128, :]), 16)
            @block.tensor
            def _(t):
                for ch in range(NCH):
                    sl = slice(ch * 512, (ch + 1) * 512)
                    for k in range(NT):
                        t.matmul(psb0[:, sl], wbuf[k][:, m * 128:(m + 1) * 128],
                                 vx[k][:, sl], start=(k == 0), stop=(k == NT - 1))
        with nc.Block() as block:
            @block.vector
            def _(v):
                s_x.wait(v)
                v.scalar_tensor_tensor(tmp[:, :], psb0[:, :], bo_s[:, m:m + 1],
                                       tmp[:, :], ALU.add, ALU.add)
        with nc.Block() as block:
            @block.sync
            def _(sync):
                st.inc(sync.dma_start(out=oT[m * 128:(m + 1) * 128, :],
                                      in_=tmp[:, :]), 16)
    with nc.Block() as block:
        @block.sync
        def _(sync):
            st.wait(sync)
    return p



def build_stream_ln():
    p = Prog()
    io = dict(xT=p.din("xT", [D, T]), oT=p.dout("oT", [D, T]))
    emit_stream_ln(p, io)
    return p


def emit_stream_ln(p, io):
    nc = p.nc
    xT = io["xT"]; oT = io["oT"]
    xres = [p.sb("xr%d" % b, [128, T], F32) for b in range(2)]
    tmp = p.sb("tmp", [128, T], F32)
    rows = p.sb("rows", [1, 2 * T], F32)
    ones_row = p.sb("ones_row", [1, 128], F32)
    ones_col = p.sb("ones_col", [128, 1], F32)
    p.add_consts([EPS])
    psb0 = p.ps("psb0", [128, T])
    psb1 = p.ps("psb1", [128, T])
    with nc.Block() as block:
        @block.gpsimd
        def _(gp):
            gp.memset(ones_row[:, :], 1.0)
            gp.memset(ones_col[:, :], 1.0)
            p.emit_const_init(gp)
    s_x = p.sem()
    st = p.sem()
    for g in range(NT):
        b = g % 2
        with nc.Block() as block:
            @block.sync
            def _(sync):
                s_x.inc(sync.dma_start(out=xres[b][:, :],
                                       in_=xT[g * 128:(g + 1) * 128, :]), 16)
            @block.scalar
            def _(s):
                s_x.wait(s)
                s.activation(tmp[:, :], xres[b][:, :], AF.Square)
        with nc.Block() as block:
            @block.tensor
            def _(t):
                for ch in range(NCH):
                    sl = slice(ch * 512, (ch + 1) * 512)
                    t.matmul(psb0[0:1, sl], ones_col[:, :], xres[b][:, sl],
                             start=(g == 0), stop=(g == NT - 1))
                    t.matmul(psb1[0:1, sl], ones_col[:, :], tmp[:, sl],
                             start=(g == 0), stop=(g == NT - 1))
    mrow = rows[0:1, 0:T]
    vrow = rows[0:1, T:2 * T]
    with nc.Block() as block:
        @block.vector
        def _(v):
            v.tensor_scalar_mul(mrow, psb0[0:1, :], 1.0 / D)
            v.tensor_scalar_mul(vrow, psb1[0:1, :], 1.0 / D)
            v.tensor_tensor(tmp[0:1, 0:T], mrow, mrow, ALU.mult)
            v.tensor_sub(vrow, vrow, tmp[0:1, 0:T])
    with nc.Block() as block:
        @block.scalar
        def _(s):
            s.activation(vrow, vrow, AF.Sqrt, bias=EPS)
    with nc.Block() as block:
        @block.vector
        def _(v):
            v.reciprocal(vrow, vrow)
            v.scalar_tensor_tensor(mrow, mrow, -1.0, vrow, ALU.mult, ALU.mult)
    with nc.Block() as block:
        @block.tensor
        def _(t):
            for ch in range(NCH):
                sl = slice(ch * 512, (ch + 1) * 512)
                t.matmul(psb0[:, sl], ones_row[:, :], vrow[:, sl],
                         start=True, stop=True)
                t.matmul(psb1[:, sl], ones_row[:, :], mrow[:, sl],
                         start=True, stop=True)
    for g in range(NT):
        b = g % 2
        with nc.Block() as block:
            @block.sync
            def _(sync):
                s_x.inc(sync.dma_start(out=xres[b][:, :],
                                       in_=xT[g * 128:(g + 1) * 128, :]), 16)
            @block.vector
            def _(v):
                s_x.wait(v)
                v.tensor_tensor(tmp[:, :], xres[b][:, :], psb0[:, :], ALU.mult)
                v.scalar_tensor_tensor(tmp[:, :], tmp[:, :], 1.0,
                                       psb1[:, :], ALU.mult, ALU.add)
        with nc.Block() as block:
            @block.sync
            def _(sync):
                st.inc(sync.dma_start(out=oT[g * 128:(g + 1) * 128, :],
                                      in_=tmp[:, :]), 16)
    with nc.Block() as block:
        @block.sync
        def _(sync):
            st.wait(sync)
    return p


def build_stream_add():
    p = Prog()
    io = dict(xT=p.din("xT", [D, T]), yT=p.din("yT", [D, T]),
              oT=p.dout("oT", [D, T]))
    emit_stream_add(p, io)
    return p


def emit_stream_add(p, io):
    nc = p.nc
    xT = io["xT"]; yT = io["yT"]; oT = io["oT"]
    a = p.sb("a", [128, T], F32)
    b_ = p.sb("b", [128, T], F32)
    s_x = p.sem()
    st = p.sem()
    for g in range(NT):
        with nc.Block() as block:
            @block.sync
            def _(sync):
                s_x.inc(sync.dma_start(out=a[:, :],
                                       in_=xT[g * 128:(g + 1) * 128, :]), 16)
                s_x.inc(sync.dma_start(out=b_[:, :],
                                       in_=yT[g * 128:(g + 1) * 128, :]), 16)
            @block.vector
            def _(v):
                s_x.wait(v)
                v.tensor_add(a[:, :], a[:, :], b_[:, :])
        with nc.Block() as block:
            @block.sync
            def _(sync):
                st.inc(sync.dma_start(out=oT[g * 128:(g + 1) * 128, :],
                                      in_=a[:, :]), 16)
    with nc.Block() as block:
        @block.sync
        def _(sync):
            st.wait(sync)
    return p



_PROGS = {}


def _prog(name, builder):
    if name not in _PROGS:
        _PROGS[name] = builder()
    return _PROGS[name]


def _run(p, in_maps):
    res = run_bass_kernel_spmd(p.nc, in_maps, core_ids=list(range(NCORES)))
    return res.results


def kernel(**inputs):
    ins = inputs
    x = np.asarray(ins["L0_attn_a_up_x"], dtype=np.float32)
    xT = [np.ascontiguousarray(x[c].T) for c in range(NCORES)]
    idc = np.eye(128, dtype=np.float32)
    bd = np.zeros((128, 128), np.float32)
    eg = np.zeros((16, 8 * 128), np.float32)
    for g in range(8):
        for hh in range(16):
            if hh // 2 == g:
                bd[64 * (hh % 2):64 * (hh % 2) + 64, g * 16 + hh] = 1.0
        for pp in range(128):
            eg[2 * g + pp // 64, g * 128 + pp] = 1.0

    r = _run(_prog("mamba", build_mamba),
             [mamba_inputs(xT[c], ins) for c in range(NCORES)])
    xT = [r[c]["oT"] for c in range(NCORES)]

    iconsts = inner_consts()
    p_ln = _prog("sln", build_stream_ln)
    p_in = _prog("inner", build_inner)
    p_add = _prog("sadd", build_stream_add)
    for _call in range(2):
        r = _run(p_ln, [{"xT": xT[c]} for c in range(NCORES)])
        yT = [r[c]["oT"] for c in range(NCORES)]
        for li in range(6):
            r = _run(p_in, [inner_inputs(yT[c], li, ins, iconsts)
                            for c in range(NCORES)])
            yT = [r[c]["oT"] for c in range(NCORES)]
        r = _run(p_add, [{"xT": xT[c], "yT": yT[c]} for c in range(NCORES)])
        xT = [r[c]["oT"] for c in range(NCORES)]
        if _call == 0:
            r = _run(_prog("attnb", build_attn_b),
                     [attn_b_inputs(xT[c], ins["L0_attn_b_Wq"],
                                    ins["L0_attn_b_Wk"], ins["L0_attn_b_Wv"],
                                    ins["L0_attn_b_Wo"], ins["L0_attn_b_bo"])
                      for c in range(NCORES)])
            xT = [r[c]["oT"] for c in range(NCORES)]

    beta = np.asarray(ins["L1_attn_a_beta"], np.float32)
    bidx = (2 * np.arange(8)[None, :] + (np.arange(128) // 64)[:, None])
    r = _run(_prog("linqkv", build_lin_qkv),
             [{"xT": xT[c],
               "WqT": _f32c(np.asarray(ins["L1_attn_a_Wq"]).T),
               "WkT": _f32c(np.asarray(ins["L1_attn_a_Wk"]).T),
               "WvT": _f32c(np.asarray(ins["L1_attn_a_Wv"]).T),
               "bq_c": _f32c(np.asarray(ins["L1_attn_a_bq"]).reshape(8, 128).T),
               "bk_c": _f32c(np.asarray(ins["L1_attn_a_bk"]).reshape(8, 128).T),
               "bv_c": _f32c(np.asarray(ins["L1_attn_a_bv"]).reshape(8, 128).T),
               "beta_e": _f32c(beta[bidx])} for c in range(NCORES)])
    kall = np.concatenate([r[c]["ko"] for c in range(NCORES)], axis=0)
    vall = np.concatenate([r[c]["vo"] for c in range(NCORES)], axis=0)
    qos = [r[c]["qo"] for c in range(NCORES)]
    r = _run(_prog("linmain", build_lin_main),
             [{"xT": xT[c], "qtb": qos[c], "kall": kall, "vall": vall,
               "WoT": _f32c(np.asarray(ins["L1_attn_a_Wo"]).T),
               "bo_c": _f32c(np.asarray(ins["L1_attn_a_bo"]).reshape(8, 128).T),
               "mask_c": _f32c(np.tile((np.arange(8) <= c).astype(np.float32),
                                       (16, 1))),
               "bd_c": bd, "eg_c": eg} for c in range(NCORES)])
    xT = [r[c]["oT"] for c in range(NCORES)]

    rq = _run(_prog("wquant", build_wquant),
              [{"wg": _f32c(ins["L1_ffn_w_gate"]), "wu": _f32c(ins["L1_ffn_w_up"]),
                "wd": _f32c(ins["L1_ffn_w_down"]), "id_c": idc}
               for c in range(NCORES)])
    p_f4 = _prog("fp4", build_fp4ffn)

    def fp4_call(xT_l):
        rr = _run(p_f4, [{"xT": xT_l[c], "wgq": rq[c]["wgo"],
                          "wuq": rq[c]["wuo"], "wdq": rq[c]["wdo"],
                          "id_c": idc} for c in range(NCORES)])
        return [rr[c]["oT"] for c in range(NCORES)]

    xT = fp4_call(xT)

    i64 = np.zeros((64, 1024), np.float32)
    for dd in range(64):
        i64[dd, dd::64] = 1.0
    r = _run(_prog("hm", build_headmix),
             [{"xT": xT[c], "Wq": _f32c(ins["L1_attn_b_Wq"]),
               "Wk": _f32c(ins["L1_attn_b_Wk"]), "Wv": _f32c(ins["L1_attn_b_Wv"]),
               "Wo": _f32c(ins["L1_attn_b_Wo"]),
               "bo_c": _f32c(np.asarray(ins["L1_attn_b_bo"]).reshape(8, 128).T),
               "i64_c": i64, "id_c": idc} for c in range(NCORES)])
    xT = [r[c]["oT"] for c in range(NCORES)]

    xT = fp4_call(xT)

    out = np.stack([np.asarray(xT[c], np.float32).T for c in range(NCORES)], axis=0)
    return np.ascontiguousarray(out)



def build_mega():
    p = Prog()
    nc = p.nc
    LN8 = float(np.log(8.0))
    p.add_consts([EPS, -1.0, 0.125, 8.0, LN8])

    io_m = {n: p.din("mb_" + n, MAMBA_SHAPES[n]) for n in MAMBA_INS if n != "xT"}
    X = p.din("X", [D, T])
    WqkvS = p.din("WqkvS", [6 * D, 3 * D])
    WopS = p.din("WopS", [6 * D, D])
    qsS = p.din("qsS", [128, 12])
    lngS = p.din("lngS", [128, 96])
    W1S = p.din("W1S", [6 * D, 5460])
    b1aS = p.din("b1aS", [128, 132])
    b1gS = p.din("b1gS", [128, 132])
    W2S = p.din("W2S", [6 * 2730, D])
    b2S = p.din("b2S", [128, 48])
    CQ_c = p.din("CQ_c", [128, T]); SQ_c = p.din("SQ_c", [128, T])
    CK_c = p.din("CK_c", [128, T]); SK_c = p.din("SK_c", [128, T])
    M2_c = p.din("M2_c", [128, 384])
    bd_c = p.din("bd_c", [128, 128])
    eg_c = p.din("eg_c", [16, 8 * 128])
    e2_c = p.din("e2_c", [128, 128])
    id_c = p.din("id_c", [128, 128])
    i64_c = p.din("i64_c", [64, D])
    ab = {n: p.din("ab_" + n, s) for n, s in ATTNB_SHAPES.items()
          if n not in ("xT", "bd_c", "eg_c", "e2_c")}
    ab.update(dict(bd_c=bd_c, eg_c=eg_c, e2_c=e2_c))
    lq = {n: p.din("lq_" + n, LINQKV_SHAPES[n]) for n in LINQKV_INS if n != "xT"}
    WoTl = p.din("lm_WoT", [D, D])
    bo_l = p.din("lm_bo_c", [128, 8])
    mask_l = p.din("lm_mask_c", [16, 8])
    wg_i = p.din("f4_wg", [2816, D])
    wu_i = p.din("f4_wu", [2816, D])
    wd_i = p.din("f4_wd", [D, 2816])
    hm = {n: p.din("hm_" + n, s) for n, s in HM_SHAPES.items()
          if n not in ("xT", "i64_c", "id_c")}
    hm.update(dict(i64_c=i64_c, id_c=id_c))
    OUT = p.dout("OUT", [D, T])

    xa = p.dram("xa", [D, T])
    xb = p.dram("xb", [D, T])
    ya = p.dram("ya", [D, T])
    yb = p.dram("yb", [D, T])
    qloc = p.dram("qloc", [D, T], BF16)
    kvp = p.dram("kvp", [2 * D, T], BF16)
    kvall = p.dram("kvall", [16 * D, T], BF16)
    wgq = p.dram("wgq", [D, 2816], BF16)
    wuq = p.dram("wuq", [D, 2816], BF16)
    wdq = p.dram("wdq", [2816, D], BF16)

    p.begin_phase("mb")
    emit_mamba(p, dict(io_m, xT=X, oT=xa))
    p.end_phase()

    cur, nxt = xa, xb
    for call in range(2):
        yc, yn = ya, yb
        p.begin_phase("ln%d" % call)
        emit_stream_ln(p, dict(xT=cur, oT=yc))
        p.end_phase()
        for li in range(6):
            p.begin_phase("in%d_%d" % (call, li))
            emit_inner(p, dict(
                yT=yc, oT=yn,
                Wqkv=WqkvS[li * D:(li + 1) * D, :],
                Wop=WopS[li * D:(li + 1) * D, :],
                qs_c=qsS[:, li * 2:(li + 1) * 2],
                lng_c=lngS[:, li * 16:(li + 1) * 16],
                W1=W1S[li * D:(li + 1) * D, :],
                b1a=b1aS[:, li * 22:(li + 1) * 22],
                b1g=b1gS[:, li * 22:(li + 1) * 22],
                W2=W2S[li * 2730:(li + 1) * 2730, :],
                b2_c=b2S[:, li * 8:(li + 1) * 8],
                CQ_c=CQ_c, SQ_c=SQ_c, CK_c=CK_c, SK_c=SK_c, M2_c=M2_c,
                bd_c=bd_c, eg_c=eg_c, id_c=id_c))
            p.end_phase()
            yc, yn = yn, yc
        p.begin_phase("ad%d" % call)
        emit_stream_add(p, dict(xT=cur, yT=yc, oT=nxt))
        p.end_phase()
        cur, nxt = nxt, cur
        if call == 0:
            p.begin_phase("ab")
            emit_attn_b(p, dict(ab, xT=cur, oT=nxt))
            p.end_phase()
            cur, nxt = nxt, cur

    p.begin_phase("lq")
    emit_lin_qkv(p, dict(lq, xT=cur, qo=qloc, ko=kvp[0:D, :],
                         vo=kvp[D:2 * D, :]))
    p.end_phase()
    p.begin_phase("cc")
    s_cc = p.sem()
    with nc.Block() as block:
        @block.gpsimd
        def _(gp):
            s_cc.inc(gp.collective_compute(
                "AllGather", ALU.bypass,
                replica_groups=[[0, 1, 2, 3, 4, 5, 6, 7]],
                ins=[kvp[:, :].opt()], outs=[kvall[:, :].opt()]), 1)
        @block.sync
        def _(sync):
            s_cc.wait(sync)
    p.end_phase()
    p.begin_phase("lm")
    emit_lin_main(p, dict(xT=cur, qtb=qloc, kall=kvall,
                          vall=kvall[D:16 * D, :], WoT=WoTl, bo_c=bo_l,
                          mask_c=mask_l, bd_c=bd_c, eg_c=eg_c, oT=nxt),
                  kvstride=2 * D)
    p.end_phase()
    cur, nxt = nxt, cur

    p.begin_phase("wq")
    emit_wquant(p, dict(wg=wg_i, wu=wu_i, wd=wd_i, id_c=id_c,
                        wgo=wgq, wuo=wuq, wdo=wdq))
    p.end_phase()
    p.begin_phase("f4a")
    emit_fp4ffn(p, dict(xT=cur, wgq=wgq, wuq=wuq, wdq=wdq, id_c=id_c, oT=nxt))
    p.end_phase()
    cur, nxt = nxt, cur
    p.begin_phase("hm")
    emit_headmix(p, dict(hm, xT=cur, oT=nxt))
    p.end_phase()
    cur, nxt = nxt, cur
    p.begin_phase("f4b")
    emit_fp4ffn(p, dict(xT=cur, wgq=wgq, wuq=wuq, wdq=wdq, id_c=id_c, oT=OUT))
    p.end_phase()
    return p


def mega_inputs(xT_c, core, ins):
    idc = np.eye(128, dtype=np.float32)
    bd = np.zeros((128, 128), np.float32)
    eg = np.zeros((16, 8 * 128), np.float32)
    for g in range(8):
        for hh in range(16):
            if hh // 2 == g:
                bd[64 * (hh % 2):64 * (hh % 2) + 64, g * 16 + hh] = 1.0
        for pp in range(128):
            eg[2 * g + pp // 64, g * 128 + pp] = 1.0
    e2 = np.zeros((128, 128), np.float32)
    for pp in range(128):
        e2[pp % 64, pp] = 1.0
        e2[64 + pp % 64, pp] = 1.0
    i64 = np.zeros((64, 1024), np.float32)
    for dd in range(64):
        i64[dd, dd::64] = 1.0
    im = {}
    mi = mamba_inputs(xT_c, ins)
    for n in MAMBA_INS:
        if n != "xT":
            im["mb_" + n] = mi[n]
    im["X"] = xT_c
    colperm = np.concatenate([h * 64 + PERM64 for h in range(16)])
    Wq_l, Wo_l, qs_l, lng_l, W1_l, b1a_l, b1g_l, W2_l, b2_l = \
        [], [], [], [], [], [], [], [], []
    for li in range(6):
        ii = inner_inputs(xT_c, li, ins, {})
        Wq_l.append(ii["Wqkv"]); Wo_l.append(ii["Wop"]); qs_l.append(ii["qs_c"])
        lng_l.append(ii["lng_c"]); W1_l.append(ii["W1"]); b1a_l.append(ii["b1a"])
        b1g_l.append(ii["b1g"]); W2_l.append(ii["W2"]); b2_l.append(ii["b2_c"])
    im["WqkvS"] = _f32c(np.vstack(Wq_l))
    im["WopS"] = _f32c(np.vstack(Wo_l))
    im["qsS"] = _f32c(np.hstack(qs_l))
    im["lngS"] = _f32c(np.hstack(lng_l))
    im["W1S"] = _f32c(np.vstack(W1_l))
    im["b1aS"] = _f32c(np.hstack(b1a_l))
    im["b1gS"] = _f32c(np.hstack(b1g_l))
    im["W2S"] = _f32c(np.vstack(W2_l))
    im["b2S"] = _f32c(np.hstack(b2_l))
    ic = inner_consts()
    im["CQ_c"] = ic["CQ_c"]; im["SQ_c"] = ic["SQ_c"]
    im["CK_c"] = ic["CK_c"]; im["SK_c"] = ic["SK_c"]; im["M2_c"] = ic["M2_c"]
    im["bd_c"] = bd; im["eg_c"] = eg; im["e2_c"] = e2; im["id_c"] = idc
    im["i64_c"] = i64
    ai = attn_b_inputs(xT_c, ins["L0_attn_b_Wq"], ins["L0_attn_b_Wk"],
                       ins["L0_attn_b_Wv"], ins["L0_attn_b_Wo"],
                       ins["L0_attn_b_bo"])
    for n in ATTNB_SHAPES:
        if n not in ("xT", "bd_c", "eg_c", "e2_c"):
            im["ab_" + n] = ai[n]
    beta = np.asarray(ins["L1_attn_a_beta"], np.float32)
    bidx = (2 * np.arange(8)[None, :] + (np.arange(128) // 64)[:, None])
    im["lq_WqT"] = _f32c(np.asarray(ins["L1_attn_a_Wq"]).T)
    im["lq_WkT"] = _f32c(np.asarray(ins["L1_attn_a_Wk"]).T)
    im["lq_WvT"] = _f32c(np.asarray(ins["L1_attn_a_Wv"]).T)
    im["lq_bq_c"] = _f32c(np.asarray(ins["L1_attn_a_bq"]).reshape(8, 128).T)
    im["lq_bk_c"] = _f32c(np.asarray(ins["L1_attn_a_bk"]).reshape(8, 128).T)
    im["lq_bv_c"] = _f32c(np.asarray(ins["L1_attn_a_bv"]).reshape(8, 128).T)
    im["lq_beta_e"] = _f32c(beta[bidx])
    im["lm_WoT"] = _f32c(np.asarray(ins["L1_attn_a_Wo"]).T)
    im["lm_bo_c"] = _f32c(np.asarray(ins["L1_attn_a_bo"]).reshape(8, 128).T)
    im["lm_mask_c"] = _f32c(np.tile((np.arange(8) <= core).astype(np.float32),
                                    (16, 1)))
    im["f4_wg"] = _f32c(ins["L1_ffn_w_gate"])
    im["f4_wu"] = _f32c(ins["L1_ffn_w_up"])
    im["f4_wd"] = _f32c(ins["L1_ffn_w_down"])
    im["hm_Wq"] = _f32c(ins["L1_attn_b_Wq"])
    im["hm_Wk"] = _f32c(ins["L1_attn_b_Wk"])
    im["hm_Wv"] = _f32c(ins["L1_attn_b_Wv"])
    im["hm_Wo"] = _f32c(ins["L1_attn_b_Wo"])
    im["hm_bo_c"] = _f32c(np.asarray(ins["L1_attn_b_bo"]).reshape(8, 128).T)
    return im


def kernel(**inputs):
    x = np.asarray(inputs["L0_attn_a_up_x"], dtype=np.float32)
    xT = [np.ascontiguousarray(x[c].T) for c in range(NCORES)]
    p = _prog("mega", build_mega)
    in_maps = [mega_inputs(xT[c], c, inputs) for c in range(NCORES)]
    res = run_bass_kernel_spmd(p.nc, in_maps, core_ids=list(range(NCORES)))
    out = np.stack([np.asarray(res.results[c]["OUT"], np.float32).T
                    for c in range(NCORES)], axis=0)
    return np.ascontiguousarray(out)
